# version 13
# baseline (speedup 1.0000x reference)
import math
from collections import defaultdict

import numpy as np
import concourse.bass as bass
import concourse.mybir as mybir
from concourse.bass_utils import run_bass_kernel_spmd

F32 = mybir.dt.float32
BF16 = mybir.dt.bfloat16
AF = mybir.ActivationFunctionType
ALU = mybir.AluOpType
AX = mybir.AxisListType

S = 2048
DM = 1024
NH = 16
HD = 64
DC = 256
NE = 8
GS = 16
KSEL = 64
NT = S // 128
NB = S // 512
EPS = 1e-5
NEG = -1.0e30

N_CORES = 8
SEQ_PER_CORE = 2


def _aux_arrays():
    idm = np.eye(128, dtype=np.float32)
    ones = np.ones((128, 128), dtype=np.float32)
    caus = np.where(np.arange(128)[:, None] > np.arange(128)[None, :], NEG,
                    0.0).astype(np.float32)
    tril01 = (np.arange(128)[None, :] < np.arange(128)[:, None]).astype(np.float32)
    t = np.arange(128)
    u = np.arange(248)
    auxg = ((u[None, :] == (120 + t[:, None] // GS)).astype(np.float32) / GS)
    g = np.arange(128)
    tok = np.arange(S)
    auxe = (g[:, None] == (tok[None, :] // GS)).astype(np.float32)
    return {"aux_id": idm, "aux_ones": ones, "aux_caus": caus,
            "aux_tril": tril01, "aux_g": auxg, "aux_e": auxe}


ENGINES = ("tensor", "scalar", "vector", "gpsimd", "sync")


class Em:

    def __init__(self, nc):
        self.nc = nc
        self.sems = {e: nc.alloc_semaphore(f"sem_{e}") for e in ENGINES}
        self.dma_sems = {}
        self.mode = "count"
        self.cur = None
        self.cnt = defaultdict(int)
        self.dma_cnt = defaultdict(int)
        self.mark = {}
        self.water = defaultdict(int)
        self.emit_cnt = defaultdict(int)
        self.blk_no = 0

    def _dma_group(self, grp, eng):
        qc = "sw" if eng == "gpsimd" else "hw"
        if grp not in self.dma_sems:
            self.dma_sems[grp] = (self.nc.alloc_semaphore(f"dg_{grp}"), qc)
        assert self.dma_sems[grp][1] == qc, (grp, qc)
        return self.dma_sems[grp][0]

    def op(self, eng, fn, inc=None, waits=(), dma=False, grp=None):
        if dma and grp is None:
            grp = inc or "misc_" + eng
        if self.mode == "count":
            if dma:
                self._dma_group(grp, eng)
                self.dma_cnt[grp] += 1
                if inc is not None:
                    self.mark[inc] = (grp + "!dma", self.dma_cnt[grp] * 16)
            elif inc is not None:
                self.cnt[eng] += 1
                self.mark[inc] = (eng, self.cnt[eng])
            return None
        name, obj = self.cur
        if name != eng:
            return None
        for w in waits:
            prod, val = self.mark[w]
            if self.water[(eng, prod)] >= val:
                continue
            if prod.endswith("!dma"):
                obj.wait_ge(self.dma_sems[prod[:-4]][0], val)
            else:
                obj.wait_ge(self.sems[prod], val)
            self.water[(eng, prod)] = val
        instr = fn(obj)
        if dma:
            instr.then_inc(self.dma_sems[grp][0], 16)
            self.emit_cnt[grp + "!d"] += 1
        elif inc is not None:
            instr.then_inc(self.sems[eng], 1)
            self.emit_cnt[eng] += 1
        return instr

    def gate(self, waits, engines=("tensor", "scalar", "vector")):
        if self.mode == "count":
            return
        name, obj = self.cur
        if name not in engines:
            return
        for w in waits:
            prod, val = self.mark[w]
            if self.water[(name, prod)] >= val:
                continue
            if prod.endswith("!dma"):
                obj.wait_ge(self.dma_sems[prod[:-4]][0], val)
            else:
                obj.wait_ge(self.sems[prod], val)
            self.water[(name, prod)] = val

    def run_block(self, body):
        self.mode = "count"
        self.cur = None
        body(self)
        with self.nc.Block() as blk:
            for eng_name in ENGINES:
                def mk(_n):
                    def section(obj):
                        self.mode = "emit"
                        self.cur = (_n, obj)
                        body(self)
                    return section
                getattr(blk, eng_name)(mk(eng_name))
        self.blk_no += 1
        for e in ENGINES:
            assert self.emit_cnt[e] == self.cnt[e], (
                "count/emit mismatch", self.blk_no, e,
                self.emit_cnt[e], self.cnt[e])
        for key in list(self.water):
            cons, prod = key
            if not prod.endswith("!dma"):
                self.water[key] = self.cnt[prod]


POOL_COLS = 58624


def _lay(entries, extra=()):
    out, off = {}, 0
    for name, n in entries:
        out[name] = (off, n)
        off += n
    assert off <= POOL_COLS, (off, POOL_COLS)
    for name, o, n in extra:
        out[name] = (o, n)
    return out


LAY_L0A = _lay([
    ("w5", 5 * 2048),
    ("kt", 8 * 2048),
    ("v", NT * NH * 65),
    ("cqt", 2 * 2048),
    ("ckvt", 2 * 2048),
    ("xnt", 8 * 512),
    ("qt", 4 * 512),
])
LAY_L0A["attn"] = (LAY_L0A["xnt"][0], 8 * 512)
LAY_L0A["ctxt"] = (LAY_L0A["ckvt"][0], 8 * 512)
LAY_L0A["wo_a"] = (LAY_L0A["w5"][0], 4 * 1024)
LAY_L0A["wo_b"] = (LAY_L0A["w5"][0] + 3 * 2048, 4 * 1024)
_ = LAY_L0A

LAY_L0F = None
LAY_L1A = None
LAY_L1F = None


class Bufs:
    def __init__(self, nc):
        a = nc.alloc_sbuf_tensor
        self.xres = a("xres", [128, NT * DM], F32)
        self.id_bf = a("id_bf", [128, 128], BF16)
        self.ones_bf = a("ones_bf", [128, 128], BF16)
        self.caus = a("caus", [128, 128], F32)
        self.tril = a("tril", [128, 128], F32)
        self.auxg = a("auxg", [128, 248], BF16)
        self.auxe = a("auxe", [128, S], BF16)
        self.stats = a("stats", [128, 128], F32)
        self.bn12 = a("bn12", [128, 12], F32)
        self.bqkv = a("bqkv", [128, 32], F32)
        self.gbfm = a("gbfm", [128, 16], F32)
        self.rows = a("rows", [128, 3072], BF16)
        self.ev = a("ev", [128, 2048], F32)
        self.s32 = a("s32", [128, 1024], F32)
        self.sbf = a("sbf", [128, 2048], BF16)
        self.consts = a("consts", [128, 8], F32)
        self.pool = a("pool", [128, POOL_COLS], BF16)
        self.psum_t = nc.alloc_psum_tensor("ps", [128, 4096], F32)
        self.lay = None

    def bank(self, i, p0=0, p1=128, c0=0, w=512):
        return self.psum_t.ap()[p0:p1, i * 512 + c0: i * 512 + c0 + w]

    def bank_bf(self, i, p0=0, p1=128, c0=0, w=1024):
        return self.psum_t.ap().bitcast(BF16)[p0:p1, i * 1024 + c0: i * 1024 + c0 + w]

    def pl(self, name, c0=0, w=None, p0=0, p1=128):
        off, n = self.lay[name]
        if w is None:
            w = n - c0
        assert c0 + w <= n, (name, c0, w, n)
        return self.pool.ap()[p0:p1, off + c0: off + c0 + w]


def fm_col(ap_2d, c):
    return ap_2d[:, c:c + 1]


def emit_load_x(em, B, x_dram_seq, tag, waits=()):
    src = x_dram_seq.rearrange("(t p) d -> p t d", p=128)
    dst = B.xres.ap().rearrange("p (t d) -> p t d", d=DM)
    em.op("sync", lambda e: e.dma_start(dst, src), inc=f"{tag}_xin",
          waits=waits, dma=True)


def emit_store_x(em, B, out_dram_seq, tag, waits):
    src = B.xres.ap().rearrange("p (t d) -> p t d", d=DM)
    dst = out_dram_seq.rearrange("(t p) d -> p t d", p=128)
    em.op("sync", lambda e: e.dma_start(dst, src), inc=f"{tag}_xout",
          waits=waits, dma=True)


def emit_ln_stats(em, B, tag, rstd2=False, waits=()):
    st = B.stats.ap()
    for t in range(NT):
        x_t = B.xres.ap()[:, t * DM:(t + 1) * DM]
        w0 = list(waits) if t == 0 else [f"{tag}_ag{t-1}"]
        em.op("vector", lambda e, x_t=x_t: e.bn_stats(B.bn12.ap()[:, 0:6],
                                                      x_t[:, 0:512]),
              inc=f"{tag}_bna{t}", waits=tuple(w0))
        em.op("vector", lambda e, x_t=x_t: e.bn_stats(B.bn12.ap()[:, 6:12],
                                                      x_t[:, 512:1024]),
              inc=f"{tag}_bnb{t}")
        em.op("vector", lambda e, t=t: e.bn_aggr(st[:, 2 * t:2 * t + 2],
                                                 B.bn12.ap()),
              inc=f"{tag}_ag{t}", waits=(f"{tag}_bna{t}", f"{tag}_bnb{t}"))
    mv = B.stats.ap().rearrange("p (t two) -> p t two", two=2)
    em.op("vector", lambda e: e.tensor_copy(
        st[:, 32:48].unsqueeze(2), mv[:, 0:16, 0:1]),
        inc=f"{tag}_m", waits=(f"{tag}_ag{NT-1}",))
    em.op("scalar", lambda e: e.activation(
        st[:, 80:96].unsqueeze(2), mv[:, 0:16, 1:2], AF.Ln,
        bias=fm_col(B.consts.ap(), 0), scale=1.0),
        inc=f"{tag}_lnv", waits=(f"{tag}_ag{NT-1}", "k_consts"))
    em.op("scalar", lambda e: e.activation(
        st[:, 48:64], st[:, 80:96], AF.Exp, bias=0.0, scale=-0.5),
        inc=f"{tag}_rstd", waits=(f"{tag}_lnv",))
    if rstd2:
        em.op("scalar", lambda e: e.activation(
            st[:, 80:96].unsqueeze(2), mv[:, 0:16, 1:2], AF.Ln,
            bias=fm_col(B.consts.ap(), 1), scale=float(1.0 + EPS)),
            inc=f"{tag}_lnv2", waits=(f"{tag}_rstd",))
        em.op("scalar", lambda e: e.activation(
            st[:, 64:80], st[:, 80:96], AF.Exp, bias=0.0, scale=-0.5),
            inc=f"{tag}_rstd2", waits=(f"{tag}_lnv2",))


def emit_xnt_tiles(em, B, tag, tiles, dst_name, blockw, rstd_col=48, gb=False,
                   waits=(), lo=False):
    st = B.stats.ap()
    tpb = blockw // 128
    prev = None
    for t in tiles:
        rt = t % tpb
        x_t = B.xres.ap()[:, t * DM:(t + 1) * DM]
        wx = list(waits) + ([prev] if prev else [])
        if not lo:
            em.op("vector",
                  lambda e, x_t=x_t, t=t: e.tensor_scalar(
                      B.sbf.ap()[:, 0:1024], x_t, fm_col(st, 32 + t),
                      fm_col(st, rstd_col + t), ALU.subtract, ALU.mult),
                  inc=f"{tag}_xn{t}", waits=tuple(wx))
        else:
            em.op("vector",
                  lambda e, x_t=x_t, t=t: e.tensor_scalar(
                      B.s32.ap()[:, 0:1024], x_t, fm_col(st, 32 + t),
                      fm_col(st, rstd_col + t), ALU.subtract, ALU.mult),
                  inc=f"{tag}_x32{t}", waits=tuple(wx))
            em.op("vector", lambda e: e.tensor_copy(B.sbf.ap()[:, 0:1024],
                                                    B.s32.ap()[:, 0:1024]),
                  inc=f"{tag}_xn{t}", waits=(f"{tag}_x32{t}",))
            em.op("vector",
                  lambda e: e.tensor_tensor(B.sbf.ap()[:, 1024:2048],
                                            B.s32.ap()[:, 0:1024],
                                            B.sbf.ap()[:, 0:1024],
                                            ALU.subtract),
                  inc=f"{tag}_xlo{t}", waits=(f"{tag}_xn{t}",))
        parts = [("", 0, [f"{tag}_xn{t}"])]
        if lo:
            parts.append(("_lo", 1024, [f"{tag}_xlo{t}", f"{tag}_ev{t}"]))
        for sfx, srcoff, wnames in parts:
            evw = f"{tag}_ev{sfx}{t}"
            trw = f"{tag}_tr{sfx}{t}"
            for c in range(8):
                bnk = 6 + c // 4
                ws = []
                if c == 0:
                    ws = list(wnames)
                    pe = f"{tag}_evprev{t}"
                    if sfx == "" and pe in em.mark:
                        ws.append(pe)
                em.op("tensor",
                      lambda e, c=c, bnk=bnk, srcoff=srcoff: e.matmul(
                          B.bank(bnk, c0=(c % 4) * 128, w=128),
                          B.sbf.ap()[:, srcoff + c * 128:srcoff + (c + 1) * 128],
                          B.id_bf.ap()[:], start=True, stop=True),
                      inc=(trw if c == 7 else None), waits=tuple(ws))
            dst = B.pl(dst_name + sfx).rearrange(
                "p (c n) -> p c n", c=8)[:, :, rt * 128:rt * 128 + 128]
            if not gb:
                em.op("vector", lambda e, dst=dst: e.tensor_copy(
                    dst[:, 0:4],
                    B.bank(6).rearrange("p (c n) -> p c n", c=4)),
                    waits=(trw,), inc=f"{tag}_evh{sfx}{t}")
                em.op("vector", lambda e, dst=dst: e.tensor_copy(
                    dst[:, 4:8],
                    B.bank(7).rearrange("p (c n) -> p c n", c=4)),
                    inc=evw)
            else:
                for c in range(8):
                    bnk = 6 + c // 4
                    em.op("vector",
                          lambda e, c=c, bnk=bnk, dst=dst: e.tensor_scalar(
                              dst[:, c], B.bank(bnk, c0=(c % 4) * 128, w=128),
                              fm_col(B.gbfm.ap(), c),
                              fm_col(B.gbfm.ap(), 8 + c), ALU.mult, ALU.add),
                          waits=((trw,) if c == 0 else ()),
                          inc=(evw if c == 7 else None))
        idx = tiles.index(t)
        nxt = tiles[idx + 1] if idx + 1 < len(tiles) else None
        if nxt is not None and em.mode == "count":
            em.mark[f"{tag}_evprev{nxt}"] = em.mark[
                f"{tag}_ev_lo{t}" if lo else f"{tag}_ev{t}"]
        prev = f"{tag}_tr{'_lo' if lo else ''}{t}"


def emit_l0_cproj(em, B, tag):
    for b in range(NB):
        emit_xnt_tiles(em, B, f"{tag}b{b}", [b * 4 + i for i in range(4)],
                       "xnt", 512, waits=(f"{tag}_rstd", f"{tag}_cev{b-1}")
                       if b else (f"{tag}_rstd",))
        for wi, (wslot, dstn) in enumerate((("wdq", "cqt"), ("wdkv", "ckvt"))):
            for cc in range(2):
                bnk = wi * 2 + cc
                for c in range(8):
                    em.op("tensor",
                          lambda e, wi=wi, cc=cc, c=c, b=b, bnk=bnk: e.matmul(
                              B.bank(bnk),
                              B.pl("w5", wi * 2048 + c * 256 + cc * 128, 128),
                              B.pl("xnt", c * 512, 512),
                              start=(c == 0), stop=(c == 7)),
                          inc=(f"{tag}_cmm{b}_{wi}{cc}" if c == 7 else None),
                          waits=(tuple([f"{tag}b{b}_ev{b*4+3}", "l0a_w5"] +
                                       ([f"{tag}_cev{b-1}"] if b else []))
                                 if c == 0 and cc == 0 and wi == 0 else ()))
                dst = "cqt" if wi == 0 else "ckvt"
                em.op("vector",
                      lambda e, dst=dst, cc=cc, b=b, bnk=bnk: e.tensor_copy(
                          B.pl(dst, cc * 2048 + b * 512, 512), B.bank(bnk)),
                      inc=(f"{tag}_cev{b}" if wi == 1 and cc == 1 else None),
                      waits=(f"{tag}_cmm{b}_{wi}{cc}",))


def emit_l0_kv(em, B, tag):
    for j in range(32):
        hp, qb = j // 4, j % 4
        bnk = j % 2
        ws = [f"{tag}_cev3"]
        if j >= 2:
            ws.append(f"{tag}_ktev{j-2}")
        for cc in range(2):
            em.op("tensor",
                  lambda e, hp=hp, qb=qb, cc=cc, bnk=bnk: e.matmul(
                      B.bank(bnk),
                      B.pl("w5", 3 * 2048 + cc * 1024 + hp * 128, 128),
                      B.pl("ckvt", cc * 2048 + qb * 512, 512),
                      start=(cc == 0), stop=(cc == 1)),
                  inc=(f"{tag}_ktm{j}" if cc == 1 else None),
                  waits=tuple(ws) if cc == 0 else ())
        em.op("vector",
              lambda e, hp=hp, qb=qb, bnk=bnk: e.tensor_copy(
                  B.pl("kt", hp * 2048 + qb * 512, 512), B.bank(bnk)),
              inc=f"{tag}_ktev{j}", waits=(f"{tag}_ktm{j}",))
    for i in range(NT * 2):
        t, h8 = i // 2, i % 2
        bnk = 2 + i % 2
        ws = [f"{tag}_cev3"]
        if i >= 2:
            ws.append(f"{tag}_vev{i-2}")
        for cc in range(2):
            em.op("tensor",
                  lambda e, t=t, h8=h8, cc=cc, bnk=bnk: e.matmul(
                      B.bank(bnk),
                      B.pl("ckvt", cc * 2048 + t * 128, 128),
                      B.pl("w5", 4 * 2048 + cc * 1024 + h8 * 512, 512),
                      start=(cc == 0), stop=(cc == 1)),
                  inc=(f"{tag}_vm{i}" if cc == 1 else None),
                  waits=tuple(ws) if cc == 0 else ())
        em.op("vector",
              lambda e, t=t, h8=h8, bnk=bnk: e.tensor_copy(
                  B.pl("v", (t * 16 + h8 * 8) * 65, 8 * 65).rearrange(
                      "p (h n) -> p h n", h=8)[:, :, 0:64],
                  B.bank(bnk).rearrange("p (h n) -> p h n", h=8)),
              inc=f"{tag}_vev{i}", waits=(f"{tag}_vm{i}",))
    em.op("vector",
          lambda e: e.memset(B.pl("v").rearrange(
              "p (g n) -> p g n", n=65)[:, :, 64:65], 1.0),
          inc=f"{tag}_vones", waits=(f"{tag}_vev{NT*2-1}",))
    for hp in range(8):
        em.op("vector",
              lambda e, hp=hp: e.tensor_tensor(
                  B.sbf.ap()[:, 0:2048], B.pl("kt", hp * 2048, 2048),
                  B.pl("kt", hp * 2048, 2048), ALU.mult),
              inc=f"{tag}_ksq{hp}",
              waits=(f"{tag}_ktev{hp*4+3}", f"{tag}_kred{hp-1}") if hp
              else (f"{tag}_ktev{hp*4+3}",))
        for qb in range(4):
            em.op("tensor",
                  lambda e, qb=qb: e.matmul(
                      B.bank(4 + qb, p1=1),
                      B.ones_bf.ap()[:, 0:1], B.sbf.ap()[:, qb * 512:(qb + 1) * 512],
                      start=True, stop=True),
                  inc=(f"{tag}_knm{hp}" if qb == 3 else None),
                  waits=(f"{tag}_ksq{hp}",) if qb == 0 else ())
        em.op("vector",
              lambda e, hp=hp: e.tensor_reduce(
                  B.ev.ap()[0:1, 1024 + hp:1025 + hp],
                  B.psum_t.ap()[0:1, 4 * 512:8 * 512], AX.X, ALU.max),
              inc=f"{tag}_kred{hp}", waits=(f"{tag}_knm{hp}",))
    em.op("scalar", lambda e: e.activation(
        B.ev.ap()[0:1, 1032:1040], B.ev.ap()[0:1, 1024:1032], AF.Ln,
        bias=0.0, scale=1.0),
        inc=f"{tag}_lnk", waits=(f"{tag}_kred7",))


def emit_attn_core(em, B, tag, *, causal, n_hp, ktmax_f, kt_slice, v_slice,
                   qt_emit, qt_bias, wo_slice, n_wo_chunks, bias_rows,
                   wo_marks, xadd_acc):
    ev = B.ev.ap()
    st_attn = lambda slot: B.pl("attn", slot * 512, 512)

    def pname(qb, hp):
        return f"{tag}a{qb}_{hp}"

    hps_prev_tail = [None]

    for qb in range(NB):
        ktmax = ktmax_f(qb)
        assert ktmax % 2 == 0
        npair = ktmax // 2
        waves = [range(0, min(4, n_hp))] + \
            ([range(4, 8)] if n_hp == 8 else [])

        def emit_wave(wv, qb=qb, ktmax=ktmax):
            wvl = list(wv)
            negbase = 1052 if wvl[0] == 0 else 1056
            for i, hp in enumerate(wvl):
                p = pname(qb, hp)
                qslot = hp % 4
                ws = []
                if hp == wvl[0]:
                    if hp >= 4:
                        ws.append(f"{pname(qb, hp-1)}_expg{ktmax-1}")
                    elif qb > 0:
                        ws.append(f"{pname(qb-1, n_hp-1)}_expg"
                                  f"{ktmax_f(qb-1)-1}")
                qt_emit(em, p, tuple(ws), qb, hp)
                prevu = None
                if n_hp == 8:
                    prevu = pname(qb, hp - 4) if hp >= 4 else (
                        pname(qb - 1, hp + 4) if qb > 0 else None)
                else:
                    prevu = pname(qb - 1, hp) if qb > 0 else None
                qbias = qt_bias(hp)
                qtw = [f"{p}_qtm"]
                if prevu:
                    qtw.append(f"{prevu}_scl")
                if isinstance(qbias, float):
                    em.op("scalar",
                          lambda e, qslot=qslot, hp=hp, qbias=qbias:
                          e.activation(
                              B.pl("qt", qslot * 512, 512), B.bank(hp % 4),
                              AF.Copy, bias=qbias, scale=0.125),
                          inc=f"{p}_qtev", waits=tuple(qtw))
                else:
                    em.op("vector",
                          lambda e, qslot=qslot, hp=hp, qbias=qbias:
                          e.tensor_scalar(
                              B.pl("qt", qslot * 512, 512), B.bank(hp % 4),
                              qbias, 0.125, ALU.add, ALU.mult),
                          inc=f"{p}_qtev", waits=tuple(qtw))
                em.op("vector",
                      lambda e, qslot=qslot: e.tensor_tensor(
                          B.sbf.ap()[:, qslot * 512:(qslot + 1) * 512],
                          B.pl("qt", qslot * 512, 512),
                          B.pl("qt", qslot * 512, 512), ALU.mult),
                      inc=f"{p}_qsq", waits=(f"{p}_qtev",))
                bnk = 6 + i // 2
                row = 32 * (i % 2)
                wqn = [f"{p}_qsq"]
                if hp == wvl[0]:
                    lastp = hps_prev_tail[0]
                    if lastp:
                        wqn.append(f"{lastp}_cp")
                    if wvl[0] == 4:
                        wqn.append(f"{pname(qb, 3)}_cp")
                    if qb > 0 and wvl[0] == 0:
                        wqn += [f"{tag}a{qb-1}_xadd6", f"{tag}a{qb-1}_xadd7"]
                em.op("tensor",
                      lambda e, qslot=qslot, bnk=bnk, row=row: e.matmul(
                          B.bank(bnk, p0=row, p1=row + 1),
                          B.ones_bf.ap()[:, 0:1],
                          B.sbf.ap()[:, qslot * 512:(qslot + 1) * 512],
                          start=True, stop=True,
                          tile_position=(0, row)),
                      inc=f"{p}_qnm", waits=tuple(wqn))
                em.op("vector",
                      lambda e, bnk=bnk, row=row, i=i: e.tensor_reduce(
                          ev[row:row + 1, 1040 + i:1041 + i],
                          B.bank(bnk, p0=row, p1=row + 1), AX.X, ALU.max),
                      inc=f"{p}_qred", waits=(f"{p}_qnm",))
            for i, hp in enumerate(wvl):
                row = 32 * (i % 2)
                if row:
                    em.op("vector",
                          lambda e, i=i: e.tensor_copy(
                              ev[0:1, 1040 + i:1041 + i],
                              ev[32:33, 1040 + i:1041 + i]),
                          inc=f"{pname(qb, wvl[i])}_qmv",
                          waits=(f"{pname(qb, wvl[i])}_qred",))
            last_red = [f"{pname(qb, h)}_qred" for h in wvl] + \
                [f"{pname(qb, wvl[i])}_qmv" for i in range(len(wvl))
                 if 32 * (i % 2)]
            nw = len(wvl)
            em.op("scalar",
                  lambda e, nw=nw: e.activation(
                      ev[0:1, 1044:1044 + nw], ev[0:1, 1040:1040 + nw],
                      AF.Ln, bias=0.0, scale=1.0),
                  inc=f"{tag}a{qb}w{wvl[0]}_lnq", waits=tuple(last_red))
            em.op("vector",
                  lambda e, nw=nw, w0=wvl[0]: e.tensor_tensor(
                      ev[0:1, 1048:1048 + nw], ev[0:1, 1044:1044 + nw],
                      ev[0:1, 1032 + w0:1032 + w0 + nw], ALU.add),
                  inc=f"{tag}a{qb}w{wvl[0]}_lns",
                  waits=(f"{tag}a{qb}w{wvl[0]}_lnq", f"{tag}_lnk"))
            em.op("scalar",
                  lambda e, nw=nw, w0=wvl[0]: e.activation(
                      B.rows.ap()[64:65, 1024 + w0:1024 + w0 + nw],
                      ev[0:1, 1048:1048 + nw], AF.Exp, bias=0.0, scale=0.5),
                  inc=f"{tag}a{qb}w{wvl[0]}_cexp",
                  waits=(f"{tag}a{qb}w{wvl[0]}_lns",))
            for i, hp in enumerate(wvl):
                em.op("tensor",
                      lambda e, i=i, hp=hp: e.matmul(
                          B.bank(6, c0=64 + i, w=1),
                          B.ones_bf.ap()[64:65, 0:128],
                          B.rows.ap()[64:65, 1024 + hp:1025 + hp],
                          start=True, stop=True),
                      inc=(f"{tag}a{qb}w{wvl[0]}_cbc" if i == nw - 1
                           else None),
                      waits=(f"{tag}a{qb}w{wvl[0]}_cexp",) if i == 0 else ())
            em.op("vector",
                  lambda e, nw=nw, negbase=negbase: e.tensor_scalar(
                      ev[:, negbase:negbase + nw], B.bank(6, c0=64, w=nw),
                      -1.0, None, ALU.mult),
                  inc=f"{tag}a{qb}w{wvl[0]}_negc",
                  waits=(f"{tag}a{qb}w{wvl[0]}_cbc",))
            if em.mode == "count":
                em.mark[f"{tag}a{qb}w{wvl[0]}_negcrd"] = \
                    em.mark[f"{tag}a{qb}w{wvl[0]}_negc"]

        emit_wave(waves[0])

        pending = [None]

        def emit_epi1(hp, qb=qb, ktmax=ktmax):
            p = pname(qb, hp)
            ctxe = f"{p}_ctxq{ktmax-1}"

            em.op("scalar",
                  lambda e: e.activation(ev[64:65, 0:1024],
                                         B.psum_t.ap()[64:65, 2048:3072],
                                         AF.Ln, bias=0.0, scale=1.0),
                  inc=f"{p}_lnz", waits=(ctxe,))
            em.op("scalar",
                  lambda e: e.activation(B.s32.ap()[64:65, 0:1024],
                                         ev[64:65, 0:1024], AF.Exp,
                                         bias=0.0, scale=-1.0),
                  inc=f"{p}_r0", waits=(f"{p}_lnz",))
            em.op("vector",
                  lambda e: e.tensor_tensor(
                      ev[64:65, 0:1024], B.psum_t.ap()[64:65, 2048:3072],
                      B.s32.ap()[64:65, 0:1024], ALU.mult),
                  inc=f"{p}_nu", waits=(f"{p}_r0",))
            em.op("vector",
                  lambda e: e.tensor_scalar(
                      ev[64:65, 0:1024], ev[64:65, 0:1024], 2.0, -1.0,
                      ALU.subtract, ALU.mult),
                  inc=f"{p}_ng", waits=(f"{p}_nu",))
            def _r1(e):
                with em.nc.allow_low_precision(reason="bf16 recipZ bcast"):
                    return e.tensor_tensor(
                        B.rows.ap()[64:65, 0:1024], ev[64:65, 0:1024],
                        B.s32.ap()[64:65, 0:1024], ALU.mult)
            em.op("vector", _r1, inc=f"{p}_recip", waits=(f"{p}_ng",))
            wbc = [f"{p}_recip",
                   f"{tag}a{qb}w{4 if (n_hp == 8 and hp >= 4) else 0}_negcrd"]
            em.op("tensor",
                  lambda e: e.matmul(B.bank(6, p0=0, p1=64),
                                     B.ones_bf.ap()[64:65, 0:64],
                                     B.rows.ap()[64:65, 0:512],
                                     start=True, stop=True),
                  inc=f"{p}_bcA", waits=tuple(wbc))
            em.op("tensor",
                  lambda e: e.matmul(B.bank(7, p0=0, p1=64),
                                     B.ones_bf.ap()[64:65, 0:64],
                                     B.rows.ap()[64:65, 512:1024],
                                     start=True, stop=True),
                  inc=f"{p}_bcB")
            em.op("scalar",
                  lambda e: e.copy(ev[0:64, 0:1024],
                                   B.psum_t.ap()[0:64, 3072:4096]),
                  inc=f"{p}_cp", waits=(f"{p}_bcB",))
            em.op("vector",
                  lambda e, hp=hp: e.tensor_tensor(
                      B.pl("ctxt", hp * 512, 512, p0=0, p1=64),
                      B.bank(4, p0=0, p1=64), ev[0:64, 0:512], ALU.mult),
                  inc=f"{p}_evA", waits=(f"{p}_cp", ctxe))
            em.op("vector",
                  lambda e: e.tensor_tensor(
                      B.rows.ap()[0:64, 2048:2560],
                      B.bank(5, p0=0, p1=64), ev[0:64, 512:1024], ALU.mult),
                  inc=f"{p}_evB")

        def emit_epi2(hp, qb=qb):
            p = pname(qb, hp)
            em.op("tensor",
                  lambda e: e.matmul(B.bank(5, p0=64, p1=128),
                                     B.id_bf.ap()[0:64, 0:64],
                                     B.rows.ap()[0:64, 2048:2560],
                                     start=True, stop=True,
                                     tile_position=(0, 64)),
                  inc=f"{p}_mv", waits=(f"{p}_evB",))
            em.op("vector",
                  lambda e, hp=hp: e.tensor_copy(
                      B.pl("ctxt", hp * 512, 512, p0=64, p1=128),
                      B.bank(5, p0=64, p1=128)),
                  inc=f"{p}_cpB2", waits=(f"{p}_mv",))

        def flush_epi():
            if pending[0] is not None:
                emit_epi1(pending[0])
                emit_epi2(pending[0])
                pending[0] = None

        for hp in range(n_hp):
            if n_hp == 8 and hp == 4:
                flush_epi()
                emit_wave(waves[1])
            p = pname(qb, hp)
            wavebase = 0 if hp < 4 else 4
            negbase = 1052 if wavebase == 0 else 1056
            negcol = negbase + (hp - wavebase)
            prevp = pname(qb, hp - 1) if hp > 0 else \
                (pname(qb - 1, n_hp - 1) if qb > 0 else None)
            prev_ktmax = (ktmax if hp > 0 else ktmax_f(qb - 1)) \
                if prevp else 0
            def emit_ctx(kt, hp=hp, p=p, prevp=prevp, ktmax=ktmax):
                first, last = (kt == 0), (kt == ktmax - 1)
                for head, cbnk in (("A", 4), ("B", 5)):
                    slot = 2 * (kt % 2) + (0 if head == "A" else 1)
                    wctx = [f"{p}_expg{kt}"]
                    if kt == 0 and prevp:
                        wctx += [f"{prevp}_evA", f"{prevp}_cpB2"]
                    em.op("tensor",
                          lambda e, kt=kt, hp=hp, head=head, cbnk=cbnk,
                          slot=slot, first=first, last=last: e.matmul(
                              B.bank(cbnk, p0=0, p1=65),
                              v_slice(hp, kt, head),
                              B.pl("attn", slot * 512, 512),
                              start=first, stop=last),
                          inc=(f"{p}_ctxq{kt}" if head == "B" else None),
                          waits=tuple(wctx) if head == "A" else ())

            for kt in range(ktmax):
                bA = 2 * (kt % 2)
                bB = bA + 1
                if kt == 0 and pending[0] is not None:
                    emit_epi1(pending[0])
                for hi, (head, pp0, pp1, bnk) in enumerate(
                        (("A", 0, 64, bA), ("B", 64, 128, bB))):
                    wsc = []
                    if hi == 0:
                        if kt >= 2:
                            wsc = [f"{p}_expg{kt-2}"]
                        else:
                            wsc = [f"{p}_qtev"]
                            if hp in (0, 4):
                                wsc.append(f"{pname(qb, min(hp + 3, n_hp - 1))}_qtev")
                            if prevp:
                                wsc.append(
                                    f"{prevp}_expg{prev_ktmax - 2 + kt}")
                    em.op("tensor",
                          lambda e, kt=kt, hp=hp, head=head, pp0=pp0,
                          pp1=pp1, bnk=bnk: e.matmul(
                              B.bank(bnk), kt_slice(hp, kt, head),
                              B.pl("qt", (hp % 4) * 512, 512,
                                   p0=pp0, p1=pp1),
                              start=True, stop=True,
                              tile_position=(pp0, 0)),
                          inc=(f"{p}_scq{kt}" if hi == 1 else None),
                          waits=tuple(wsc))
                if kt == ktmax - 1 and em.mode == "count":
                    em.mark[f"{p}_scl"] = em.mark[f"{p}_scq{kt}"]
                rel = (kt - 4 * qb) if causal else -1
                straddle = causal and rel >= 0
                expw = [f"{p}_scq{kt}", f"{tag}a{qb}w{wavebase}_negc"]
                if kt >= 2:
                    expw.append(f"{p}_ctxq{kt-2}")
                elif prevp:
                    expw.append(f"{prevp}_ctxq{prev_ktmax - 2 + kt}")
                if not straddle:
                    em.op("scalar",
                          lambda e, bA=bA: e.activation(
                              B.pl("attn", bA * 512, 1024),
                              B.psum_t.ap()[:, bA * 512:(bA + 2) * 512],
                              AF.Exp, bias=fm_col(ev, negcol), scale=1.0),
                          inc=f"{p}_expg{kt}", waits=tuple(expw))
                else:
                    for hi2, bnk in enumerate((bA, bB)):
                        em.op("vector",
                              lambda e, bnk=bnk, rel=rel: e.tensor_tensor(
                                  B.bank(bnk, c0=rel * 128, w=128),
                                  B.bank(bnk, c0=rel * 128, w=128),
                                  B.caus.ap()[:], ALU.add),
                              inc=(f"{p}_msk{kt}" if hi2 == 1 else None),
                              waits=(f"{p}_scq{kt}",) if hi2 == 0 else ())
                    msw = [f"{p}_msk{kt}"] + expw[1:]
                    if rel > 0:
                        for hi2, bnk in enumerate((bA, bB)):
                            em.op("vector",
                                  lambda e, bnk=bnk, rel=rel: e.memset(
                                      B.pl("attn", bnk * 512, rel * 128),
                                      0.0),
                                  inc=f"{p}_ms{kt}",
                                  waits=tuple(msw) if hi2 == 0 else ())
                    expw2 = [f"{p}_msk{kt}"] + expw[1:] + \
                        ([f"{p}_ms{kt}"] if rel > 0 else [])
                    c0 = max(rel, 0) * 128
                    for hi2, bnk in enumerate((bA, bB)):
                        em.op("scalar",
                              lambda e, bnk=bnk, c0=c0: e.activation(
                                  B.pl("attn", bnk * 512 + c0, 512 - c0),
                                  B.bank(bnk, c0=c0, w=512 - c0),
                                  AF.Exp, bias=fm_col(ev, negcol),
                                  scale=1.0),
                              inc=(f"{p}_expg{kt}" if hi2 == 1 else None),
                              waits=tuple(expw2) if hi2 == 0 else ())
                if kt == 1 and pending[0] is not None:
                    emit_epi2(pending[0])
                    pending[0] = None
                if kt >= 1:
                    emit_ctx(kt - 1)
            emit_ctx(ktmax - 1)
            pending[0] = hp
        flush_epi()
        hps_prev_tail[0] = pname(qb, n_hp - 1)
        lastp = pname(qb, n_hp - 1)
        for i in range(8):
            rt, oh = i // 2, i % 2
            bnk = 6 + i % 2
            wsj = [f"{lastp}_evA", f"{lastp}_cpB2", f"{lastp}_cp"] + \
                list(wo_marks)
            if i >= 2:
                wsj.append(f"{tag}a{qb}_xadd{i-2}")
            brows = bias_rows(oh)
            nmm = n_wo_chunks + len(brows)
            k = 0
            for ci in range(n_wo_chunks):
                em.op("tensor",
                      lambda e, ci=ci, rt=rt, oh=oh, bnk=bnk, k=k, nmm=nmm:
                      e.matmul(
                          B.bank(bnk), B.pl("ctxt", ci * 512 + rt * 128, 128),
                          wo_slice(ci, oh), start=(k == 0),
                          stop=(k == nmm - 1)),
                      inc=(f"{tag}a{qb}_opm{i}" if k == nmm - 1 else None),
                      waits=tuple(wsj) if k == 0 else ())
                k += 1
            for br in brows:
                em.op("tensor",
                      lambda e, br=br, bnk=bnk, k=k, nmm=nmm: e.matmul(
                          B.bank(bnk), B.ones_bf.ap()[0:1, 0:128], br,
                          start=(k == 0), stop=(k == nmm - 1)),
                      inc=(f"{tag}a{qb}_opm{i}" if k == nmm - 1 else None))
                k += 1
            xsl = B.xres.ap()[:, (qb * 4 + rt) * DM + oh * 512:
                              (qb * 4 + rt) * DM + oh * 512 + 512]
            em.op("vector",
                  lambda e, xsl=xsl, bnk=bnk: e.tensor_tensor(
                      xsl, xsl, B.bank(bnk), ALU.add),
                  inc=f"{tag}a{qb}_xadd{i}", waits=(f"{tag}a{qb}_opm{i}",))


def emit_l0a_attn(em, B, tag, wo_marks):
    def qt_emit(em, p, ws, qb, hp):
        ws = ws + ("l0a_w5", f"{tag}_cev3")
        for cc in range(2):
            em.op("tensor",
                  lambda e, cc=cc, hp=hp, qb=qb: e.matmul(
                      B.bank(hp % 4),
                      B.pl("w5", 2 * 2048 + cc * 1024 + hp * 128, 128),
                      B.pl("cqt", cc * 2048 + qb * 512, 512),
                      start=(cc == 0), stop=(cc == 1)),
                  inc=(f"{p}_qtm" if cc == 1 else None),
                  waits=ws if cc == 0 else ())

    emit_attn_core(
        em, B, tag, causal=True, n_hp=8, ktmax_f=lambda qb: 4 * (qb + 1),
        kt_slice=lambda hp, kt, part: B.pl(
            "kt", hp * 2048 + kt * 128, 128,
            p0=0 if part == "A" else 64, p1=64 if part == "A" else 128),
        v_slice=lambda hp, kt, part: B.pl(
            "v", (kt * 16 + hp * 2 + (0 if part == "A" else 1)) * 65, 65),
        qt_emit=qt_emit, qt_bias=lambda hp: 0.0,
        wo_slice=lambda ci, oh: B.pl("wo_a", ci * 1024 + oh * 512, 512)
        if ci < 4 else B.pl("wo_b", (ci - 4) * 1024 + oh * 512, 512),
        n_wo_chunks=8, bias_rows=lambda oh: [], wo_marks=wo_marks,
        xadd_acc=True)


WSPECS_L0A = [
    ("L0_attn_W_dq", "w5", 0, 8),
    ("L0_attn_W_dkv", "w5", 2048, 8),
    ("L0_attn_W_uq", "w5", 2 * 2048, 2),
    ("L0_attn_W_uk", "w5", 3 * 2048, 2),
    ("L0_attn_W_uv", "w5", 4 * 2048, 2),
]


def build_core_program(n_seq=SEQ_PER_CORE,
                       phases=("l0a", "l0f", "l1a", "l1f")):
    nc = bass.Bass("TRN2", target_bir_lowering=False)
    x_d = nc.dram_tensor("x", [n_seq, S, DM], F32, kind="ExternalInput")
    out_d = nc.dram_tensor("out", [n_seq, S, DM], F32, kind="ExternalOutput")
    aux = {k: nc.dram_tensor(k, list(v.shape), F32, kind="ExternalInput")
           for k, v in _aux_arrays().items()}
    W = {}
    wshapes = {"L0_attn_W_dq": [DM, DC], "L0_attn_W_dkv": [DM, DC],
               "L0_attn_W_uq": [DC, DM], "L0_attn_W_uk": [DC, DM],
               "L0_attn_W_uv": [DC, DM], "L0_attn_W_o": [DM, DM],
               "L0_ffn_gate_w": [DM, NE], "L0_ffn_gate_b": [NE],
               "L0_ffn_expert_w": [NE, DM, DM], "L0_ffn_expert_b": [NE, DM],
               "L1_attn_ln_gamma": [DM], "L1_attn_ln_beta": [DM],
               "L1_attn_W_qkv": [3 * DM, DM], "L1_attn_b_qkv": [3 * DM],
               "L1_attn_W_out": [DM, DM], "L1_attn_b_out": [DM],
               "L1_ffn_Wq": [DM, DM], "L1_ffn_Wk": [DM, DM],
               "L1_ffn_Wv": [DM, DM], "L1_ffn_Wo": [DM, DM],
               "L1_ffn_Ws1": [DM, DC], "L1_ffn_bs1": [DC],
               "L1_ffn_Ws2": [DC, 1], "L1_ffn_bs2": [1]}
    need = set()
    if "l0a" in phases:
        need |= {"L0_attn_W_dq", "L0_attn_W_uq", "L0_attn_W_dkv",
                 "L0_attn_W_uk", "L0_attn_W_uv", "L0_attn_W_o"}
    if "l0f" in phases:
        need |= {"L0_ffn_gate_w", "L0_ffn_gate_b", "L0_ffn_expert_w",
                 "L0_ffn_expert_b"}
    if "l1a" in phases:
        need |= {"L1_attn_ln_gamma", "L1_attn_ln_beta", "L1_attn_W_qkv",
                 "L1_attn_b_qkv", "L1_attn_W_out", "L1_attn_b_out"}
    if "l1f" in phases:
        need |= {"L1_ffn_Wq", "L1_ffn_Wk", "L1_ffn_Wv", "L1_ffn_Wo",
                 "L1_ffn_Ws1", "L1_ffn_bs1", "L1_ffn_Ws2", "L1_ffn_bs2"}
    for name in sorted(need):
        W[name] = nc.dram_tensor(name, wshapes[name], F32,
                                 kind="ExternalInput")

    B = Bufs(nc)
    em = Em(nc)

    def load_aux(em):
        em.op("vector", lambda e: e.memset(B.consts.ap()[:, 0:1], float(EPS)))
        em.op("vector", lambda e: e.memset(B.consts.ap()[:, 1:2],
                                           float(EPS * EPS)),
              inc="k_consts")
        em.op("gpsimd", lambda e: e.dma_start(B.id_bf.ap(), aux["aux_id"].ap()),
              inc="aux_id", dma=True)
        em.op("gpsimd", lambda e: e.dma_start(B.ones_bf.ap(),
                                              aux["aux_ones"].ap()),
              inc="aux_ones", dma=True)
        em.op("sync", lambda e: e.dma_start(B.caus.ap(), aux["aux_caus"].ap()),
              inc="aux_caus", dma=True)
        em.op("sync", lambda e: e.dma_start(B.tril.ap(), aux["aux_tril"].ap()),
              inc="aux_tril", dma=True)
        em.op("gpsimd", lambda e: e.dma_start(B.auxg.ap(), aux["aux_g"].ap()),
              inc="aux_g", dma=True)
        em.op("gpsimd", lambda e: e.dma_start(B.auxe.ap(), aux["aux_e"].ap()),
              inc="aux_e", dma=True)

    def load_w5(em):
        for name, slot, coff, nchunk in WSPECS_L0A:
            src = W[name].ap().rearrange("(c p) n -> p c n", p=128)
            dst = B.pl(slot, coff, nchunk * (2048 // nchunk)).rearrange(
                "p (c n) -> p c n", c=nchunk)
            em.op("gpsimd", lambda e, src=src, dst=dst: e.dma_start(dst, src),
                  inc=("l0a_w5" if name == "L0_attn_W_uv" else None), dma=True,
                  grp="w5")

    def run_seq(seq):
        tag = f"s{seq}"
        B.lay = LAY_L0A

        def ph_load(em):
            emit_load_x(em, B, x_d.ap()[seq], tag,
                        waits=(f"s{seq-1}_xout",) if seq > 0 else ())
            if seq == 0:
                load_aux(em)
            if "l0a" in phases:
                load_w5(em)
        em.run_block(ph_load)

        if "l0a" in phases:
            t0 = tag + "l0a"

            def ph_ln(em):
                em.gate(("aux_id", "aux_ones", "aux_caus", "aux_tril",
                         "aux_g", "aux_e"))
                emit_ln_stats(em, B, t0, waits=(f"{tag}_xin", "k_consts"))
            em.run_block(ph_ln)
            em.run_block(lambda em: emit_l0_cproj(em, B, t0))

            def ph_kv(em):
                emit_l0_kv(em, B, t0)
                src = W["L0_attn_W_o"].ap().rearrange("(c p) n -> p c n", p=128)
                em.op("gpsimd",
                      lambda e: e.dma_start(
                          B.pl("wo_a").rearrange("p (c n) -> p c n", c=4),
                          src[:, 0:4]),
                      inc="l0a_woa", waits=(f"{t0}_cmm3_11",), dma=True)
                em.op("gpsimd",
                      lambda e: e.dma_start(
                          B.pl("wo_b").rearrange("p (c n) -> p c n", c=4),
                          src[:, 4:8]),
                      inc="l0a_wob", waits=(f"{t0}_ktm31", f"{t0}_vm31"),
                      dma=True)
            em.run_block(ph_kv)
            em.run_block(lambda em: emit_l0a_attn(em, B, t0,
                                                  ("l0a_woa", "l0a_wob")))

        if "l0f" in phases:
            B.lay = LAY_L0F
            emit_l0_ffn(em, B, tag + "l0f", W, f"{tag}_xin")

        if "l1a" in phases:
            B.lay = LAY_L1A
            emit_l1_attn(em, B, tag + "l1a", W, f"{tag}_xin")

        if "l1f" in phases:
            B.lay = LAY_L1F
            emit_l1_ffn(em, B, tag + "l1f", W, f"{tag}_xin")

        em.run_block(lambda em: emit_store_x(em, B, out_d.ap()[seq], tag, ()))

    for seq in range(n_seq):
        run_seq(seq)

    with nc.Block() as blk:
        @blk.sync
        def _(e):
            for grp, (sem, qc) in em.dma_sems.items():
                if qc == "hw":
                    e.wait_ge(sem, em.dma_cnt[grp] * 16)
        @blk.gpsimd
        def _(e):
            for grp, (sem, qc) in em.dma_sems.items():
                if qc == "sw":
                    e.wait_ge(sem, em.dma_cnt[grp] * 16)
    return nc


def _shard_inputs(inputs, n_seq=SEQ_PER_CORE):
    aux = _aux_arrays()
    maps = []
    x = np.ascontiguousarray(inputs["L0_attn_x"])
    for c in range(N_CORES):
        m = {"x": x[c * n_seq:(c + 1) * n_seq]}
        for k, v in inputs.items():
            if k != "L0_attn_x":
                m[k] = np.ascontiguousarray(v)
        m.update(aux)
        maps.append(m)
    return maps


def kernel(**inputs):
    nc = build_core_program()
    maps = _shard_inputs(inputs)
    declared = set()
    for a in nc.m.functions[0].allocations:
        try:
            declared.add(a.memorylocations[0].name)
        except AttributeError:
            pass
    maps = [{k: v for k, v in m.items() if k in declared} for m in maps]
    res = run_bass_kernel_spmd(nc, maps, core_ids=list(range(N_CORES)))
    return np.concatenate([r["out"] for r in res.results], axis=0)


LAY_L0F = _lay([
    ("xnt", 8 * 2048),
    ("xnt_lo", 8 * 2048),
    ("gw", 2 * 64),
    ("wt8", 2048),
    ("be8", 2 * 1024),
    ("wfb", 128),
])
LAY_L0F["ew"] = (LAY_L0F["xnt_lo"][0], 2 * 8192)


def emit_l0_ffn(em, B, tag, Wd, xin_mark):
    ev = B.ev.ap()
    s32 = B.s32.ap()
    def ph_ln(em):
        em.gate(("aux_id", "aux_ones", "aux_caus", "aux_tril",
                 "aux_g", "aux_e"))
        emit_ln_stats(em, B, tag, waits=(xin_mark, "k_consts"))
    em.run_block(ph_ln)

    def ph_xnt(em):
        emit_xnt_tiles(em, B, f"{tag}x", list(range(NT)), "xnt", 2048,
                       waits=(f"{tag}_rstd",), lo=True)
    em.run_block(ph_xnt)

    def ph_gate(em):
        src = Wd["L0_ffn_gate_w"].ap().rearrange("(c p) n -> p c n", p=128)
        em.op("sync", lambda e: e.dma_start(
            s32[:, 0:64].rearrange("p (c n) -> p c n", c=8), src),
            inc=f"{tag}_gwin", dma=True)
        em.op("scalar", lambda e: e.copy(B.pl("gw", 0, 64), s32[:, 0:64]),
              inc=f"{tag}_gwh", waits=(f"{tag}_gwin",))
        em.op("vector", lambda e: e.tensor_tensor(
            B.pl("gw", 64, 64), s32[:, 0:64], B.pl("gw", 0, 64), ALU.subtract),
            inc=f"{tag}_gwl", waits=(f"{tag}_gwh",))
        srcb = Wd["L0_ffn_gate_b"].ap().unsqueeze(1)
        em.op("sync", lambda e: e.dma_start(ev[0:8, 1984:1985], srcb),
              inc=f"{tag}_gbin", dma=True)
        for qb in range(4):
            n = 0
            for slot, wi in (("xnt", 0), ("xnt", 64), ("xnt_lo", 0)):
                for c in range(8):
                    last = (n == 23)
                    em.op("tensor",
                          lambda e, qb=qb, slot=slot, wi=wi, c=c, n=n,
                          last=last: e.matmul(
                              B.bank(qb, p1=8),
                              B.pl("gw", wi + c * 8, 8),
                              B.pl(slot, c * 2048 + qb * 512, 512),
                              start=(n == 0), stop=last),
                          inc=(f"{tag}_glm{qb}" if last else None),
                          waits=((f"{tag}_gwl", f"{tag}x_ev_lo{NT-1}")
                                 if n == 0 else ()))
                    n += 1
        for qb in range(4):
            em.op("vector",
                  lambda e, qb=qb: e.tensor_scalar(
                      ev[0:8, qb * 512:(qb + 1) * 512], B.bank(qb, p1=8),
                      ev[0:8, 1984:1985], None, ALU.add),
                  inc=f"{tag}_glev{qb}",
                  waits=(f"{tag}_glm{qb}", f"{tag}_gbin"))
        em.op("scalar", lambda e: e.copy(B.sbf.ap()[0:8, 0:2048],
                                         ev[0:8, 0:2048]),
              inc=f"{tag}_glh", waits=tuple(f"{tag}_glev{q}" for q in range(4)))
        em.op("vector", lambda e: e.tensor_tensor(
            B.rows.ap()[0:8, 0:2048], ev[0:8, 0:2048],
            B.sbf.ap()[0:8, 0:2048], ALU.subtract),
            inc=f"{tag}_gll", waits=(f"{tag}_glh",))
        for t in range(NT):
            bnk = 4 + t % 2
            em.op("tensor",
                  lambda e, t=t, bnk=bnk: e.matmul(
                      B.bank(bnk, c0=0, w=8), B.sbf.ap()[0:8, t * 128:(t + 1) * 128],
                      B.id_bf.ap()[0:8, 0:8], start=True, stop=False),
                  waits=(f"{tag}_gll",) + ((f"{tag}_gw{t-2}",) if t >= 2 else ())),
            em.op("tensor",
                  lambda e, t=t, bnk=bnk: e.matmul(
                      B.bank(bnk, c0=0, w=8), B.rows.ap()[0:8, t * 128:(t + 1) * 128],
                      B.id_bf.ap()[0:8, 0:8], start=False, stop=True),
                  inc=f"{tag}_gtr{t}")
            co = 1024 + t * 8
            em.op("vector",
                  lambda e, t=t, bnk=bnk: e.tensor_reduce(
                      ev[:, 1990:1991], B.bank(bnk, c0=0, w=8), AX.X, ALU.max,
                      negate=True),
                  inc=f"{tag}_gmx{t}",
                  waits=(f"{tag}_gtr{t}",) + ((f"{tag}_gex{t-1}",) if t else ()))
            em.op("scalar",
                  lambda e, t=t, bnk=bnk: e.activation(
                      s32[:, 960:968], B.bank(bnk, c0=0, w=8), AF.Exp,
                      bias=ev[:, 1990:1991], scale=1.0),
                  inc=f"{tag}_gex{t}",
                  waits=(f"{tag}_gmx{t}",) + ((f"{tag}_gpr{t-1}",) if t else ()))
            em.op("vector",
                  lambda e: e.tensor_reduce(ev[:, 1991:1992], s32[:, 960:968],
                                            AX.X, ALU.add),
                  inc=f"{tag}_gsm{t}", waits=(f"{tag}_gex{t}",))
            em.op("vector",
                  lambda e: e.reciprocal(ev[:, 1992:1993], ev[:, 1991:1992]),
                  inc=f"{tag}_grc{t}", waits=(f"{tag}_gsm{t}",))
            em.op("vector",
                  lambda e: e.tensor_scalar(s32[:, 968:976], s32[:, 960:968],
                                            ev[:, 1992:1993], None, ALU.mult),
                  inc=f"{tag}_gpr{t}", waits=(f"{tag}_grc{t}",))
            em.op("vector",
                  lambda e: e.tensor_scalar(s32[:, 976:984], s32[:, 968:976],
                                            0.1, None, ALU.is_ge),
                  inc=f"{tag}_gsel{t}", waits=(f"{tag}_gpr{t}",))
            em.op("vector",
                  lambda e: e.tensor_tensor(s32[:, 984:992], s32[:, 968:976],
                                            s32[:, 976:984], ALU.mult),
                  inc=f"{tag}_gw0{t}", waits=(f"{tag}_gsel{t}",))
            em.op("vector",
                  lambda e: e.tensor_reduce(ev[:, 1993:1994], s32[:, 984:992],
                                            AX.X, ALU.add),
                  inc=f"{tag}_gws{t}", waits=(f"{tag}_gw0{t}",))
            em.op("vector",
                  lambda e: e.tensor_scalar(ev[:, 1994:1995], ev[:, 1993:1994],
                                            0.0, ev[:, 1993:1994], ALU.is_equal,
                                            ALU.add),
                  inc=f"{tag}_grw{t}", waits=(f"{tag}_gws{t}",))
            em.op("vector",
                  lambda e: e.reciprocal(ev[:, 1995:1996], ev[:, 1994:1995]),
                  inc=f"{tag}_grr{t}", waits=(f"{tag}_grw{t}",))
            em.op("vector",
                  lambda e, t=t: e.tensor_scalar(s32[:, t * 8:(t + 1) * 8],
                                                 s32[:, 984:992],
                                                 ev[:, 1995:1996], None,
                                                 ALU.mult),
                  inc=f"{tag}_gw{t}", waits=(f"{tag}_grr{t}",))
            em.op("vector",
                  lambda e, t=t: e.tensor_copy(B.pl("wfb", t * 8, 8),
                                               s32[:, t * 8:(t + 1) * 8]),
                  inc=f"{tag}_gwb{t}", waits=(f"{tag}_gw{t}",))
        for t in range(NT):
            bnk = 6 + t % 2
            em.op("tensor",
                  lambda e, t=t, bnk=bnk: e.matmul(
                      B.bank(bnk, p1=8, c0=0, w=128),
                      B.pl("wfb", t * 8, 8),
                      B.id_bf.ap()[:], start=True, stop=True),
                  inc=f"{tag}_wtm{t}",
                  waits=(f"{tag}_gwb{t}",) + ((f"{tag}_wte{t-2}",) if t >= 2
                                              else ())),
            em.op("vector",
                  lambda e, t=t, bnk=bnk: e.tensor_copy(
                      B.pl("wt8", t * 128, 128, p0=0, p1=8),
                      B.bank(bnk, p0=0, p1=8, c0=0, w=128)),
                  inc=f"{tag}_wte{t}", waits=(f"{tag}_wtm{t}",))
        em.op("sync", lambda e: e.dma_start(
            ev[0:8, 0:1024], Wd["L0_ffn_expert_b"].ap()),
            inc=f"{tag}_bein", dma=True, grp=f"{tag}_bein",
            waits=(f"{tag}_gll",))
        em.op("scalar", lambda e: e.copy(B.pl("be8", 0, 1024, p0=0, p1=8),
                                         ev[0:8, 0:1024]),
              inc=f"{tag}_beh", waits=(f"{tag}_bein", f"{tag}_gtr{NT-1}")),
        em.op("vector", lambda e: e.tensor_tensor(
            B.pl("be8", 1024, 1024, p0=0, p1=8), ev[0:8, 0:1024],
            B.pl("be8", 0, 1024, p0=0, p1=8), ALU.subtract),
            inc=f"{tag}_bel", waits=(f"{tag}_beh",))
    em.run_block(ph_gate)

    def ph_exp(em):
        for e_ in range(NE):
            src = Wd["L0_ffn_expert_w"].ap()[e_].rearrange(
                "(c p) n -> p c n", p=128)
            ws = []
            if e_ >= 2:
                ws.append(f"{tag}_em{e_-2}")
            em.op("gpsimd",
                  lambda e, e_=e_, src=src: e.dma_start(
                      B.pl("ew", (e_ % 2) * 8192, 8192).rearrange(
                          "p (c n) -> p c n", c=8), src),
                  inc=f"{tag}_ewin{e_}", waits=tuple(ws), dma=True,
                  grp=f"ew{e_ % 2}")
        em.op("vector", lambda e: e.drain() or 1)
        for e_ in range(NE):
            for i in range(32):
                tt, nh = i // 2, i % 2
                bnk = i % 4
                last_mm = (i == 31)
                ws = [f"{tag}_ewin{e_}"]
                if e_ > 0 or i >= 4:
                    pj = (e_ * 32 + i) - 4
                    ws.append(f"{tag}_stt{pj}")
                for c in range(8):
                    em.op("tensor",
                          lambda e, e_=e_, tt=tt, nh=nh, c=c, bnk=bnk: e.matmul(
                              B.bank(bnk),
                              B.pl("xnt", c * 2048 + tt * 128, 128),
                              B.pl("ew", (e_ % 2) * 8192 + c * 1024 + nh * 512,
                                   512),
                              start=(c == 0), stop=(c == 7)),
                          inc=(f"{tag}_em{e_}" if last_mm and c == 7 else
                               (f"{tag}_pm{e_*32+i}" if c == 7 else None)),
                          waits=tuple(ws) if c == 0 else ())
                mmark = f"{tag}_em{e_}" if last_mm else f"{tag}_pm{e_*32+i}"
                xsl = B.xres.ap()[:, tt * DM + nh * 512: tt * DM + nh * 512 + 512]
                em.op("vector",
                      lambda e, xsl=xsl, bnk=bnk, tt=tt, e_=e_: e.scalar_tensor_tensor(
                          xsl, B.bank(bnk), s32[:, tt * 8 + e_:tt * 8 + e_ + 1],
                          xsl, ALU.mult, ALU.add),
                      inc=f"{tag}_stt{e_*32+i}", waits=(mmark,))
        for i in range(32):
            tt, nh = i // 2, i % 2
            bnk = 4 + i % 2
            ws = [f"{tag}_stt{NE*32-1}"]
            if i >= 2:
                ws.append(f"{tag}_bad{i-2}")
            em.op("tensor",
                  lambda e, tt=tt, nh=nh, bnk=bnk: e.matmul(
                      B.bank(bnk), B.pl("wt8", tt * 128, 128, p0=0, p1=8),
                      B.pl("be8", nh * 512, 512, p0=0, p1=8),
                      start=True, stop=False),
                  waits=tuple(ws)),
            em.op("tensor",
                  lambda e, tt=tt, nh=nh, bnk=bnk: e.matmul(
                      B.bank(bnk), B.pl("wt8", tt * 128, 128, p0=0, p1=8),
                      B.pl("be8", 1024 + nh * 512, 512, p0=0, p1=8),
                      start=False, stop=True),
                  inc=f"{tag}_bm{i}"),
            xsl = B.xres.ap()[:, tt * DM + nh * 512: tt * DM + nh * 512 + 512]
            em.op("vector",
                  lambda e, xsl=xsl, bnk=bnk: e.tensor_tensor(
                      xsl, xsl, B.bank(bnk), ALU.add),
                  inc=f"{tag}_bad{i}", waits=(f"{tag}_bm{i}",))
    em.run_block(ph_exp)


LAY_L1A = _lay([
    ("ut", 8 * 2048),
    ("kt", 4 * 2048),
    ("v", NT * 8 * 65),
    ("ctxt", 8 * 512),
    ("attn", 8 * 512),
    ("wstr", 2 * 1024),
    ("wnat", 2 * 1024),
    ("wqt", 4 * 1024),
    ("wot", 8 * 1024),
])
LAY_L1A["qt"] = (LAY_L1A["wstr"][0], 4 * 512)


def emit_l1_attn(em, B, tag, Wd, xin_mark):
    ev = B.ev.ap()
    st = B.stats.ap()

    def ph_ln(em):
        em.gate(("aux_id", "aux_ones", "aux_caus", "aux_tril",
                 "aux_g", "aux_e"))
        emit_ln_stats(em, B, tag, rstd2=True, waits=(xin_mark, "k_consts"))
        em.op("sync", lambda e: e.dma_start(
            B.gbfm.ap()[:, 0:8],
            Wd["L1_attn_ln_gamma"].ap().rearrange("(c p) -> p c", p=128),
            allow_slow_non_contiguous=True),
            inc=f"{tag}_gam", dma=True)
        em.op("sync", lambda e: e.dma_start(
            B.gbfm.ap()[:, 8:16],
            Wd["L1_attn_ln_beta"].ap().rearrange("(c p) -> p c", p=128),
            allow_slow_non_contiguous=True),
            inc=f"{tag}_bet", dma=True)
        em.op("sync", lambda e: e.dma_start(
            B.bqkv.ap()[:, 0:24],
            Wd["L1_attn_b_qkv"].ap().rearrange("(c p) -> p c", p=128),
            allow_slow_non_contiguous=True),
            inc=f"{tag}_bq", dma=True)
        em.op("sync", lambda e: e.dma_start(
            ev[0:1, 0:1024], Wd["L1_attn_b_out"].ap().unsqueeze(0)),
            inc=f"{tag}_bo", dma=True)
        em.op("vector", lambda e: e.tensor_scalar(
            B.bqkv.ap()[:, 24:32], B.bqkv.ap()[:, 0:8], 0.125, None, ALU.mult),
            inc=f"{tag}_bqs", waits=(f"{tag}_bq",))
        em.op("scalar", lambda e: e.copy(B.rows.ap()[0:1, 0:1024],
                                         ev[0:1, 0:1024]),
              inc=f"{tag}_boh", waits=(f"{tag}_bo",))
        em.op("vector", lambda e: e.tensor_tensor(
            B.rows.ap()[0:1, 1024:2048], ev[0:1, 0:1024],
            B.rows.ap()[0:1, 0:1024], ALU.subtract),
            inc=f"{tag}_bol", waits=(f"{tag}_boh",))
        em.op("sync", lambda e: e.dma_start(
            ev[32:33, 0:1024], Wd["L1_attn_b_qkv"].ap()[2048:3072].unsqueeze(0)),
            inc=f"{tag}_bv", dma=True)
        em.op("scalar", lambda e: e.copy(B.sbf.ap()[32:33, 0:1024],
                                         ev[32:33, 0:1024]),
              inc=f"{tag}_bvh", waits=(f"{tag}_bv",))
        em.op("vector", lambda e: e.tensor_tensor(
            B.sbf.ap()[32:33, 1024:2048], ev[32:33, 0:1024],
            B.sbf.ap()[32:33, 0:1024], ALU.subtract),
            inc=f"{tag}_bvl", waits=(f"{tag}_bvh",))
        em.op("vector", lambda e: e.tensor_copy(B.rows.ap()[32:33, 0:2048],
                                                B.sbf.ap()[32:33, 0:2048]),
              inc=f"{tag}_bvr", waits=(f"{tag}_bvl",))
    em.run_block(ph_ln)

    def ph_ut(em):
        emit_xnt_tiles(em, B, f"{tag}u", list(range(NT)), "ut", 2048,
                       rstd_col=64, gb=True,
                       waits=(f"{tag}_rstd2", f"{tag}_gam", f"{tag}_bet"))
        em.op("vector", lambda e: e.tensor_scalar(
            st[:, 80:96], st[:, 48:64], 1.0, None, ALU.add),
            inc=f"{tag}_hs1", waits=(f"{tag}_rstd2",))
        em.op("vector", lambda e: e.tensor_tensor(
            st[:, 96:112], st[:, 32:48], st[:, 48:64], ALU.mult),
            inc=f"{tag}_hs2", waits=(f"{tag}_hs1",))
        for t in range(NT):
            x_t = B.xres.ap()[:, t * DM:(t + 1) * DM]
            em.op("vector",
                  lambda e, x_t=x_t, t=t: e.tensor_scalar(
                      x_t, x_t, fm_col(st, 80 + t), fm_col(st, 96 + t),
                      ALU.mult, ALU.subtract),
                  inc=f"{tag}_hf{t}",
                  waits=(f"{tag}_hs2", f"{tag}u_xn{t}"))
    em.run_block(ph_ut)

    for half in range(2):
        tagH = f"{tag}h{half}"

        def ph_qkv(em, half=half, tagH=tagH):
            seqs = [("q", half * 4 + j) for j in range(4)] + \
                   [("k", 8 + half * 4 + j) for j in range(4)] + \
                   [("v", 16 + half * 4 + j) for j in range(4)] + \
                   ([("o", j) for j in range(8)] if half == 0 else [])
            for si, (kind, oc) in enumerate(seqs):
                par = si % 2
                wsrc = Wd["L1_attn_W_qkv"] if kind != "o" else Wd["L1_attn_W_out"]
                src = wsrc.ap()[oc * 128:(oc + 1) * 128, :]
                wsl = []
                if si >= 2:
                    wsl.append(f"{tagH}_wtr{si-2}")
                em.op("gpsimd",
                      lambda e, src=src, par=par: e.dma_start(
                          B.pl("wnat", par * 1024, 1024), src),
                      inc=f"{tagH}_wn{si}", waits=tuple(wsl), dma=True,
                      grp=f"{tag}_wn{par}")
                for c in range(8):
                    bnk = 6 + c // 4
                    ws2 = []
                    if c == 0:
                        ws2 = [f"{tagH}_wn{si}"]
                        if si >= 1:
                            ws2.append(f"{tagH}_wev{si-1}")
                    em.op("tensor",
                          lambda e, c=c, bnk=bnk, par=par: e.matmul(
                              B.bank(bnk, c0=(c % 4) * 128, w=128),
                              B.pl("wnat", par * 1024 + c * 128, 128),
                              B.id_bf.ap()[:], start=True, stop=True),
                          inc=(f"{tagH}_wtr{si}" if c == 7 else None),
                          waits=tuple(ws2))
                if kind == "q":
                    dst = B.pl("wqt", (oc - half * 4) * 1024, 1024)
                elif kind == "o":
                    dst = None
                else:
                    dst = B.pl("wstr", par * 1024, 1024)
                if kind != "o":
                    em.op("vector",
                          lambda e, dst=dst: e.tensor_copy(
                              dst.rearrange("p (c n) -> p c n", c=8),
                              B.bank(6, w=1024).rearrange("p (c n) -> p c n",
                                                          c=8)),
                          inc=f"{tagH}_wev{si}", waits=(f"{tagH}_wtr{si}",))
                else:
                    d3 = B.pl("wot").rearrange("p (c n) -> p c n", c=8)[
                        :, :, oc * 128:(oc + 1) * 128]
                    em.op("vector",
                          lambda e, d3=d3: e.tensor_copy(
                              d3, B.bank(6, w=1024).rearrange(
                                  "p (c n) -> p c n", c=8)),
                          inc=f"{tagH}_wev{si}", waits=(f"{tagH}_wtr{si}",))
                if kind == "k":
                    gp = oc - 8
                    lp = gp - half * 4
                    for qblk in range(4):
                        bnk = qblk % 2
                        for c in range(8):
                            ws3 = []
                            if c == 0:
                                ws3 = [f"{tagH}_wev{si}"]
                                if qblk >= 2:
                                    ws3.append(f"{tagH}_ktev{lp}_{qblk-2}")
                                elif lp > 0 or half > 0:
                                    ws3.append(f"{tagH}_ktev{lp-1}_{2+qblk}"
                                               if lp > 0 else
                                               f"{tag}h{half-1}_ktev3_{2+qblk}")
                            em.op("tensor",
                                  lambda e, c=c, qblk=qblk, bnk=bnk, par=par:
                                  e.matmul(
                                      B.bank(bnk),
                                      B.pl("wstr", par * 1024 + c * 128, 128),
                                      B.pl("ut", c * 2048 + qblk * 512, 512),
                                      start=(c == 0), stop=(c == 7)),
                                  inc=(f"{tagH}_ktm{lp}_{qblk}" if c == 7
                                       else None),
                                  waits=tuple(ws3))
                        em.op("vector",
                              lambda e, lp=lp, qblk=qblk, bnk=bnk, gp=gp:
                              e.tensor_scalar(
                                  B.pl("kt", lp * 2048 + qblk * 512, 512),
                                  B.bank(bnk), fm_col(B.bqkv.ap(), 8 + gp),
                                  None, ALU.add),
                              inc=f"{tagH}_ktev{lp}_{qblk}",
                              waits=(f"{tagH}_ktm{lp}_{qblk}",))
                elif kind == "v":
                    ocr = oc - 16 - half * 4
                    for t in range(NT):
                        bnk = 2 + t % 2
                        i = ocr * NT + t
                        for c in range(8):
                            ws4 = []
                            if c == 0:
                                ws4 = [f"{tagH}_wev{si}", f"{tag}_bvr"]
                                if i >= 2:
                                    ws4.append(f"{tagH}_vev{i-2}")
                                elif half > 0:
                                    ws4.append(f"{tag}h{half-1}_vev{4*NT-2+i}")
                            em.op("tensor",
                                  lambda e, c=c, t=t, bnk=bnk, par=par:
                                  e.matmul(
                                      B.bank(bnk, w=128),
                                      B.pl("ut", c * 2048 + t * 128, 128),
                                      B.pl("wstr", par * 1024 + c * 128, 128),
                                      start=(c == 0), stop=False),
                                  waits=tuple(ws4))
                        em.op("tensor",
                              lambda e, bnk=bnk, ocr=ocr: e.matmul(
                                  B.bank(bnk, w=128),
                                  B.ones_bf.ap()[32:33, 0:128],
                                  B.rows.ap()[32:33, (half * 4 + ocr) * 128:
                                              (half * 4 + ocr) * 128 + 128],
                                  start=False, stop=False))
                        em.op("tensor",
                              lambda e, bnk=bnk, ocr=ocr: e.matmul(
                                  B.bank(bnk, w=128),
                                  B.ones_bf.ap()[32:33, 0:128],
                                  B.rows.ap()[32:33, 1024 + (half * 4 + ocr) * 128:
                                              1024 + (half * 4 + ocr) * 128 + 128],
                                  start=False, stop=True),
                              inc=f"{tagH}_vm{i}")
                        em.op("vector",
                              lambda e, t=t, ocr=ocr, bnk=bnk: e.tensor_copy(
                                  B.pl("v", (t * 8 + ocr * 2) * 65,
                                       2 * 65).rearrange(
                                      "p (h n) -> p h n", h=2)[:, :, 0:64],
                                  B.bank(bnk, w=128).rearrange(
                                      "p (h n) -> p h n", h=2)),
                              inc=f"{tagH}_vev{i}", waits=(f"{tagH}_vm{i}",))
            em.op("vector",
                  lambda e: e.memset(B.pl("v").rearrange(
                      "p (g n) -> p g n", n=65)[:, :, 64:65], 1.0),
                  inc=f"{tagH}_vones", waits=(f"{tagH}_vev{4*NT-1}",))
            for lp in range(4):
                em.op("vector",
                      lambda e, lp=lp: e.tensor_tensor(
                          B.sbf.ap()[:, 0:2048], B.pl("kt", lp * 2048, 2048),
                          B.pl("kt", lp * 2048, 2048), ALU.mult),
                      inc=f"{tagH}_ksq{lp}",
                      waits=(f"{tagH}_ktev{lp}_3", f"{tagH}_kred{lp-1}") if lp
                      else (f"{tagH}_ktev{lp}_3",))
                for qblk in range(4):
                    em.op("tensor",
                          lambda e, qblk=qblk: e.matmul(
                              B.bank(4 + qblk, p1=1), B.ones_bf.ap()[:, 0:1],
                              B.sbf.ap()[:, qblk * 512:(qblk + 1) * 512],
                              start=True, stop=True),
                          inc=(f"{tagH}_knm{lp}" if qblk == 3 else None),
                          waits=(f"{tagH}_ksq{lp}",) if qblk == 0 else ())
                em.op("vector",
                      lambda e, lp=lp: e.tensor_reduce(
                          ev[0:1, 1024 + lp:1025 + lp],
                          B.psum_t.ap()[0:1, 4 * 512:8 * 512], AX.X, ALU.max),
                      inc=f"{tagH}_kred{lp}", waits=(f"{tagH}_knm{lp}",))
            em.op("scalar", lambda e: e.activation(
                ev[0:1, 1032:1036], ev[0:1, 1024:1028], AF.Ln,
                bias=0.0, scale=1.0),
                inc=f"{tagH}_lnk", waits=(f"{tagH}_kred3",))
        em.run_block(ph_qkv)

        def qt_emit(em, p, ws, qb, hp, tagH=tagH):
            for c in range(8):
                em.op("tensor",
                      lambda e, c=c, hp=hp, qb=qb: e.matmul(
                          B.bank(hp % 4),
                          B.pl("wqt", hp * 1024 + c * 128, 128),
                          B.pl("ut", c * 2048 + qb * 512, 512),
                          start=(c == 0), stop=(c == 7)),
                      inc=(f"{p}_qtm" if c == 7 else None),
                      waits=ws if c == 0 else ())

        def ph_attn(em, half=half, tagH=tagH):
            emit_attn_core(
                em, B, tagH, causal=False, n_hp=4, ktmax_f=lambda qb: NT,
                kt_slice=lambda hp, kt, part: B.pl(
                    "kt", hp * 2048 + kt * 128, 128,
                    p0=0 if part == "A" else 64,
                    p1=64 if part == "A" else 128),
                v_slice=lambda hp, kt, part: B.pl(
                    "v", (kt * 8 + hp * 2 + (0 if part == "A" else 1)) * 65,
                    65),
                qt_emit=qt_emit,
                qt_bias=lambda hp: fm_col(B.bqkv.ap(), half * 4 + hp),
                wo_slice=lambda ci, oh: B.pl(
                    "wot", (half * 4 + ci) * 1024 + oh * 512, 512),
                n_wo_chunks=4,
                bias_rows=(lambda oh: [B.rows.ap()[0:1, oh * 512:
                                                   oh * 512 + 512],
                                       B.rows.ap()[0:1, 1024 + oh * 512:
                                                   1024 + oh * 512 + 512]])
                if half == 0 else (lambda oh: []),
                wo_marks=(f"{tag}h0_wev{19}", f"{tag}_bol"),
                xadd_acc=True)
        em.run_block(ph_attn)


LAY_L1F = _lay([
    ("wq", 8 * 1024), ("wk", 8 * 1024), ("wv", 8 * 1024), ("wo", 8 * 1024),
    ("ws1", 2 * 2048),
    ("gt", 2 * 1024),
    ("vg", 16 * HD),
    ("ctxtg", 8 * 128),
    ("attng", 4 * 128),
    ("o128", 2 * 1024),
    ("ktg", 8 * 128),
    ("qtg", 8 * 128),
])


def emit_l1_ffn(em, B, tag, Wd, xin_mark):
    ev = B.ev.ap()
    st = B.stats.ap()
    s32 = B.s32.ap()

    def ph_ln(em):
        em.gate(("aux_id", "aux_ones", "aux_caus", "aux_tril",
                 "aux_g", "aux_e"))
        emit_ln_stats(em, B, tag, waits=(xin_mark, "k_consts"))
        for nm, slot in (("L1_ffn_Wq", "wq"), ("L1_ffn_Wk", "wk"),
                         ("L1_ffn_Wv", "wv"), ("L1_ffn_Wo", "wo")):
            src = Wd[nm].ap().rearrange("(c p) n -> p c n", p=128)
            em.op("gpsimd", lambda e, src=src, slot=slot: e.dma_start(
                B.pl(slot).rearrange("p (c n) -> p c n", c=8), src),
                inc=f"{tag}_{slot}in", dma=True)
    em.run_block(ph_ln)

    def ph_groups(em):
        for t in range(NT):
            x_t = B.xres.ap()[:, t * DM:(t + 1) * DM]
            wx = [f"{tag}_rstd"]
            if t:
                wx.append(f"{tag}_gm{t-1}")
            em.op("vector",
                  lambda e, x_t=x_t, t=t: e.tensor_scalar(
                      s32[:, 0:1024], x_t, fm_col(st, 32 + t),
                      fm_col(st, 48 + t), ALU.subtract, ALU.mult),
                  inc=f"{tag}_x32{t}", waits=tuple(wx))
            em.op("vector", lambda e: e.tensor_copy(B.sbf.ap()[:, 0:1024],
                                                    s32[:, 0:1024]),
                  inc=f"{tag}_xh{t}", waits=(f"{tag}_x32{t}",))
            em.op("vector",
                  lambda e: e.tensor_tensor(B.sbf.ap()[:, 1024:2048],
                                            s32[:, 0:1024],
                                            B.sbf.ap()[:, 0:1024],
                                            ALU.subtract),
                  inc=f"{tag}_xl{t}", waits=(f"{tag}_xh{t}",))
            gsl = B.auxg.ap()[:, 120 - 8 * t:248 - 8 * t]
            for dh in range(2):
                for so in (0, 1024):
                    em.op("tensor",
                          lambda e, gsl=gsl, dh=dh, so=so, t=t: e.matmul(
                              B.bank(dh), gsl,
                              B.sbf.ap()[:, so + dh * 512:so + dh * 512 + 512],
                              start=(t == 0 and so == 0),
                              stop=(t == NT - 1 and so == 1024)),
                          inc=(f"{tag}_gm{t}" if dh == 1 and so == 1024
                               else None),
                          waits=(f"{tag}_xl{t}",) if dh == 0 and so == 0
                          else ())
        em.op("vector", lambda e: e.tensor_copy(ev[:, 0:512], B.bank(0)),
              inc=f"{tag}_gev0", waits=(f"{tag}_gm{NT-1}",))
        em.op("vector", lambda e: e.tensor_copy(ev[:, 512:1024], B.bank(1)),
              inc=f"{tag}_gev1")
        em.op("scalar", lambda e: e.copy(B.sbf.ap()[:, 0:1024],
                                         ev[:, 0:1024]),
              inc=f"{tag}_gh", waits=(f"{tag}_gev0", f"{tag}_gev1",
                                      f"{tag}_xl{NT-1}"))
        em.op("vector", lambda e: e.tensor_tensor(
            B.sbf.ap()[:, 1024:2048], ev[:, 0:1024], B.sbf.ap()[:, 0:1024],
            ALU.subtract),
            inc=f"{tag}_gl", waits=(f"{tag}_gh",))
        for i in range(16):
            hl, c = i // 8, i % 8
            bnk = 2 + i % 2
            em.op("tensor",
                  lambda e, hl=hl, c=c, bnk=bnk: e.matmul(
                      B.bank(bnk, w=128),
                      B.sbf.ap()[:, hl * 1024 + c * 128:hl * 1024 + (c + 1) * 128],
                      B.id_bf.ap()[:], start=True, stop=True),
                  inc=f"{tag}_gtr{i}",
                  waits=((f"{tag}_gl",) if i == 0 else ()) +
                  ((f"{tag}_gte{i-2}",) if i >= 2 else ()))
            em.op("vector",
                  lambda e, hl=hl, c=c, bnk=bnk: e.tensor_copy(
                      B.pl("gt", hl * 1024 + c * 128, 128),
                      B.bank(bnk, w=128)),
                  inc=f"{tag}_gte{i}", waits=(f"{tag}_gtr{i}",))
    em.run_block(ph_groups)

    def ph_route(em):
        for sg in range(2):
            src = Wd["L1_ffn_Ws1"].ap().rearrange(
                "(c p) n -> p c n", p=128)[:, sg * 4:(sg + 1) * 4]
            em.op("sync", lambda e, src=src: e.dma_start(
                s32.rearrange("p (c n) -> p c n", c=4), src),
                inc=f"{tag}_w1in{sg}", dma=True,
                waits=(f"{tag}_w1l{sg-1}",) if sg else ())
            em.op("scalar",
                  lambda e, sg=sg: e.copy(B.pl("ws1", sg * 1024, 1024),
                                          s32[:, 0:1024]),
                  inc=f"{tag}_w1h{sg}", waits=(f"{tag}_w1in{sg}",))
            em.op("vector",
                  lambda e, sg=sg: e.tensor_tensor(
                      B.pl("ws1", 2048 + sg * 1024, 1024), s32[:, 0:1024],
                      B.pl("ws1", sg * 1024, 1024), ALU.subtract),
                  inc=f"{tag}_w1l{sg}", waits=(f"{tag}_w1h{sg}",))
        em.op("sync", lambda e: e.dma_start(
            ev[0:1, 1100:1356], Wd["L1_ffn_bs1"].ap().unsqueeze(0)),
            inc=f"{tag}_b1in", dma=True)
        em.op("scalar", lambda e: e.copy(B.rows.ap()[0:1, 0:256],
                                         ev[0:1, 1100:1356]),
              inc=f"{tag}_b1h", waits=(f"{tag}_b1in",))
        em.op("vector", lambda e: e.tensor_tensor(
            B.rows.ap()[0:1, 256:512], ev[0:1, 1100:1356],
            B.rows.ap()[0:1, 0:256], ALU.subtract),
            inc=f"{tag}_b1l", waits=(f"{tag}_b1h",))
        em.op("sync", lambda e: e.dma_start(
            ev[:, 1400:1402], Wd["L1_ffn_Ws2"].ap().rearrange(
                "(c p) n -> p c n", p=128),
            allow_slow_non_contiguous=True),
            inc=f"{tag}_w2in", dma=True)
        em.op("sync", lambda e: e.dma_start(
            ev[0:1, 1402:1403], Wd["L1_ffn_bs2"].ap().unsqueeze(0)),
            inc=f"{tag}_b2in", dma=True)
        em.op("scalar", lambda e: e.copy(B.sbf.ap()[:, 1024:1026],
                                         ev[:, 1400:1402]),
              inc=f"{tag}_w2h", waits=(f"{tag}_w2in",))
        em.op("vector", lambda e: e.tensor_tensor(
            B.sbf.ap()[:, 1026:1028], ev[:, 1400:1402],
            B.sbf.ap()[:, 1024:1026], ALU.subtract),
            inc=f"{tag}_w2l", waits=(f"{tag}_w2h",))
        em.op("scalar", lambda e: e.copy(B.sbf.ap()[0:1, 1028:1029],
                                         ev[0:1, 1402:1403]),
              inc=f"{tag}_b2h", waits=(f"{tag}_b2in",))
        em.op("vector", lambda e: e.tensor_tensor(
            B.sbf.ap()[0:1, 1029:1030], ev[0:1, 1402:1403],
            B.sbf.ap()[0:1, 1028:1029], ALU.subtract),
            inc=f"{tag}_b2l", waits=(f"{tag}_b2h",))
        n = 0
        for gof, wof in ((0, 0), (0, 2048), (1024, 0)):
            for c in range(8):
                em.op("tensor",
                      lambda e, gof=gof, wof=wof, c=c, n=n: e.matmul(
                          B.bank(4, w=256),
                          B.pl("gt", gof + c * 128, 128),
                          B.pl("ws1", wof + c * 256, 256),
                          start=(n == 0), stop=False),
                      waits=(f"{tag}_w1l1", f"{tag}_b1l") if n == 0 else ())
                n += 1
        em.op("tensor", lambda e: e.matmul(
            B.bank(4, w=256), B.ones_bf.ap()[0:1, 0:128],
            B.rows.ap()[0:1, 0:256], start=False, stop=False))
        em.op("tensor", lambda e: e.matmul(
            B.bank(4, w=256), B.ones_bf.ap()[0:1, 0:128],
            B.rows.ap()[0:1, 256:512], start=False, stop=True),
            inc=f"{tag}_hm")
        em.op("scalar", lambda e: e.activation(s32[:, 0:256], B.bank(4, w=256),
                                               AF.Relu, bias=0.0, scale=1.0),
              inc=f"{tag}_hrel", waits=(f"{tag}_hm",))
        em.op("vector", lambda e: e.tensor_copy(B.sbf.ap()[:, 0:256],
                                                s32[:, 0:256]),
              inc=f"{tag}_hh", waits=(f"{tag}_hrel",))
        em.op("vector", lambda e: e.tensor_tensor(
            B.sbf.ap()[:, 256:512], s32[:, 0:256], B.sbf.ap()[:, 0:256],
            ALU.subtract),
            inc=f"{tag}_hl", waits=(f"{tag}_hh",))
        for i in range(4):
            hl, c = i // 2, i % 2
            bnk = 5
            em.op("tensor",
                  lambda e, hl=hl, c=c: e.matmul(
                      B.bank(5, c0=i * 0, w=128) if False else
                      B.bank(5, c0=(i % 4) * 128, w=128),
                      B.sbf.ap()[:, hl * 256 + c * 128:hl * 256 + (c + 1) * 128],
                      B.id_bf.ap()[:], start=True, stop=True),
                  inc=(f"{tag}_htr" if i == 3 else None),
                  waits=(f"{tag}_hl",) if i == 0 else ())
        em.op("vector", lambda e: e.tensor_copy(B.sbf.ap()[:, 512:1024],
                                                B.bank(5, w=512)),
              inc=f"{tag}_hte", waits=(f"{tag}_htr",))
        n = 0
        for hof, wof in ((512, 1024), (512, 1026), (768, 1024)):
            for c in range(2):
                em.op("tensor",
                      lambda e, hof=hof, wof=wof, c=c, n=n: e.matmul(
                          B.bank(6, w=1),
                          B.sbf.ap()[:, hof + c * 128:hof + (c + 1) * 128],
                          B.sbf.ap()[c * 128:(c + 1) * 128, wof + c:wof + c + 1]
                          if False else
                          B.sbf.ap()[:, wof + c:wof + c + 1],
                          start=(n == 0), stop=False),
                      waits=(f"{tag}_hte", f"{tag}_w2l", f"{tag}_b2l")
                      if n == 0 else ())
                n += 1
        em.op("tensor", lambda e: e.matmul(
            B.bank(6, w=1), B.ones_bf.ap()[0:1, 0:128],
            B.sbf.ap()[0:1, 1028:1029], start=False, stop=False))
        em.op("tensor", lambda e: e.matmul(
            B.bank(6, w=1), B.ones_bf.ap()[0:1, 0:128],
            B.sbf.ap()[0:1, 1029:1030], start=False, stop=True),
            inc=f"{tag}_sm")
        em.op("vector", lambda e: e.tensor_copy(B.sbf.ap()[:, 1030:1031],
                                                B.bank(6, w=1)),
              inc=f"{tag}_sh", waits=(f"{tag}_sm",))
        em.op("vector", lambda e: e.drain() or 1)
        em.op("vector", lambda e: e.scalar_tensor_tensor(
            B.sbf.ap()[:, 1031:1032], B.bank(6, w=1), 1.0,
            B.sbf.ap()[:, 1030:1031], ALU.mult, ALU.subtract),
            inc=f"{tag}_sl", waits=(f"{tag}_sh",))
        for i, col in enumerate((1030, 1031)):
            em.op("tensor",
                  lambda e, col=col, i=i: e.matmul(
                      B.bank(7, p1=1, w=128), B.sbf.ap()[:, col:col + 1],
                      B.id_bf.ap()[:], start=(i == 0), stop=(i == 1)),
                  inc=(f"{tag}_str" if i == 1 else None),
                  waits=(f"{tag}_sl",) if i == 0 else ())
        em.op("vector", lambda e: e.tensor_reduce(
            ev[0:1, 1403:1404], B.bank(7, p1=1, w=128), AX.X, ALU.max,
            negate=True),
            inc=f"{tag}_smx", waits=(f"{tag}_str",))
        em.op("scalar", lambda e: e.activation(
            ev[0:1, 1404:1532], B.bank(7, p1=1, w=128), AF.Exp,
            bias=ev[0:1, 1403:1404], scale=1.0),
            inc=f"{tag}_sex", waits=(f"{tag}_smx",))
        em.op("vector", lambda e: e.tensor_reduce(
            ev[0:1, 1532:1533], ev[0:1, 1404:1532], AX.X, ALU.add),
            inc=f"{tag}_ssum", waits=(f"{tag}_sex",))
        em.op("vector", lambda e: e.reciprocal(ev[0:1, 1533:1534],
                                               ev[0:1, 1532:1533]),
              inc=f"{tag}_src", waits=(f"{tag}_ssum",))
        em.op("vector", lambda e: e.tensor_scalar(
            ev[0:1, 1404:1532], ev[0:1, 1404:1532], ev[0:1, 1533:1534], None,
            ALU.mult),
            inc=f"{tag}_spr", waits=(f"{tag}_src",))
        em.op("vector", lambda e: e.tensor_copy(B.sbf.ap()[0:1, 1032:1160],
                                                ev[0:1, 1404:1532]),
              inc=f"{tag}_sph", waits=(f"{tag}_spr",))
        em.op("vector", lambda e: e.tensor_tensor(
            B.sbf.ap()[0:1, 1160:1288], ev[0:1, 1404:1532],
            B.sbf.ap()[0:1, 1032:1160], ALU.subtract),
            inc=f"{tag}_spl", waits=(f"{tag}_sph",))
        for i, co in enumerate((1032, 1160)):
            em.op("tensor",
                  lambda e, co=co, i=i: e.matmul(
                      B.bank(4, w=1), B.sbf.ap()[0:1, co:co + 128],
                      B.ones_bf.ap()[0:1, 0:1], start=(i == 0), stop=(i == 1)),
                  inc=(f"{tag}_pcm" if i == 1 else None),
                  waits=(f"{tag}_spl", f"{tag}_hrel") if i == 0 else ())
        for i, co in enumerate((1032, 1160)):
            em.op("tensor",
                  lambda e, co=co, i=i: e.matmul(
                      B.bank(5, w=128), B.ones_bf.ap()[0:1, 0:128],
                      B.sbf.ap()[0:1, co:co + 128], start=(i == 0),
                      stop=(i == 1)),
                  inc=(f"{tag}_prm" if i == 1 else None),
                  waits=(f"{tag}_hte",) if i == 0 else ())
        em.op("vector", lambda e: e.tensor_copy(ev[:, 1536:1537],
                                                B.bank(4, w=1)),
              inc=f"{tag}_pce", waits=(f"{tag}_pcm",))
        em.op("vector", lambda e: e.tensor_copy(ev[:, 1537:1665],
                                                B.bank(5, w=128)),
              inc=f"{tag}_pre", waits=(f"{tag}_prm",))
        em.op("vector", lambda e: e.tensor_scalar(
            s32[:, 0:128], ev[:, 1537:1665], ev[:, 1536:1537], None,
            ALU.is_gt, ALU.add, accum_out=ev[:, 1665:1666]),
            inc=f"{tag}_gcnt", waits=(f"{tag}_pce", f"{tag}_pre"))
        em.op("vector", lambda e: e.tensor_scalar(
            s32[:, 128:256], ev[:, 1537:1665], ev[:, 1536:1537], None,
            ALU.is_equal),
            inc=f"{tag}_aeq", waits=(f"{tag}_gcnt",))
        em.op("vector", lambda e: e.tensor_tensor(
            s32[:, 256:384], s32[:, 128:256], B.tril.ap()[:], ALU.mult),
            inc=f"{tag}_eqt", waits=(f"{tag}_aeq",))
        em.op("vector", lambda e: e.tensor_scalar(
            s32[:, 384:512], s32[:, 256:384], 0.0, None, ALU.add, ALU.add,
            accum_out=ev[:, 1666:1667]),
            inc=f"{tag}_eqlw", waits=(f"{tag}_eqt",))
        em.op("vector", lambda e: e.tensor_tensor(
            ev[:, 1667:1668], ev[:, 1665:1666], ev[:, 1666:1667], ALU.add),
            inc=f"{tag}_rank", waits=(f"{tag}_eqlw",))
        em.op("vector", lambda e: e.tensor_scalar(
            ev[:, 1668:1669], ev[:, 1667:1668], 64.0, None, ALU.is_lt),
            inc=f"{tag}_msel", waits=(f"{tag}_rank",))
        em.op("vector", lambda e: e.tensor_scalar(
            ev[:, 1669:1670], ev[:, 1668:1669], float(-NEG), float(NEG),
            ALU.mult, ALU.add),
            inc=f"{tag}_mbias", waits=(f"{tag}_msel",))
    em.run_block(ph_route)

    def ph_gattn(em):
        for pair in range(8):
            bnk = pair % 2
            for c in range(8):
                em.op("tensor",
                      lambda e, pair=pair, c=c, bnk=bnk: e.matmul(
                          B.bank(bnk, w=128),
                          B.pl("wq", c * 1024 + pair * 128, 128),
                          B.pl("gt", c * 128, 128),
                          start=(c == 0), stop=(c == 7)),
                      inc=(f"{tag}_qm{pair}" if c == 7 else None),
                      waits=(f"{tag}_wqin",) + ((f"{tag}_qe{pair-2}",)
                                                if pair >= 2 else ())
                      if c == 0 else ())
            em.op("scalar",
                  lambda e, pair=pair, bnk=bnk: e.activation(
                      B.pl("qtg", pair * 128, 128), B.bank(bnk, w=128),
                      AF.Copy, bias=0.0, scale=0.125),
                  inc=f"{tag}_qe{pair}", waits=(f"{tag}_qm{pair}",))
            em.op("tensor",
                  lambda e, pair=pair, c=0, bnk=bnk: None, waits=()) \
                if False else None
            bnk2 = 2 + pair % 2
            for c in range(8):
                em.op("tensor",
                      lambda e, pair=pair, c=c, bnk2=bnk2: e.matmul(
                          B.bank(bnk2, w=128),
                          B.pl("wk", c * 1024 + pair * 128, 128),
                          B.pl("gt", c * 128, 128),
                          start=(c == 0), stop=(c == 7)),
                      inc=(f"{tag}_km{pair}" if c == 7 else None),
                      waits=(f"{tag}_wkin",) + ((f"{tag}_ke{pair-2}",)
                                                if pair >= 2 else ())
                      if c == 0 else ())
            em.op("vector",
                  lambda e, pair=pair, bnk2=bnk2: e.tensor_copy(
                      B.pl("ktg", pair * 128, 128), B.bank(bnk2, w=128)),
                  inc=f"{tag}_ke{pair}", waits=(f"{tag}_km{pair}",))
        for vh in range(2):
            bnk = 4 + vh
            for c in range(8):
                em.op("tensor",
                      lambda e, vh=vh, c=c, bnk=bnk: e.matmul(
                          B.bank(bnk),
                          B.pl("gt", c * 128, 128),
                          B.pl("wv", c * 1024 + vh * 512, 512),
                          start=(c == 0), stop=(c == 7)),
                      inc=(f"{tag}_vm{vh}" if c == 7 else None),
                      waits=(f"{tag}_wvin",) if c == 0 else ())
            em.op("vector",
                  lambda e, vh=vh, bnk=bnk: e.tensor_copy(
                      B.pl("vg", vh * 512, 512), B.bank(bnk)),
                  inc=f"{tag}_vge{vh}", waits=(f"{tag}_vm{vh}",))
        for pair in range(8):
            p = f"{tag}g{pair}"
            prevp = f"{tag}g{pair-1}" if pair else None
            em.op("vector",
                  lambda e, pair=pair: e.tensor_tensor(
                      B.sbf.ap()[:, 0:128], B.pl("ktg", pair * 128, 128),
                      B.pl("ktg", pair * 128, 128), ALU.mult),
                  inc=f"{p}_ksq",
                  waits=(f"{tag}_ke{pair}",) + ((f"{prevp}_qred",)
                                                if prevp else ())),
            em.op("tensor",
                  lambda e: e.matmul(B.bank(6, p1=1, w=128),
                                     B.ones_bf.ap()[:, 0:1],
                                     B.sbf.ap()[:, 0:128],
                                     start=True, stop=True),
                  inc=f"{p}_knm",
                  waits=(f"{p}_ksq",) + ((f"{prevp}_cpA",) if prevp else ())),
            em.op("vector",
                  lambda e: e.tensor_reduce(ev[0:1, 1700:1701],
                                            B.bank(6, p1=1, w=128), AX.X,
                                            ALU.max),
                  inc=f"{p}_kred", waits=(f"{p}_knm",))
            em.op("vector",
                  lambda e, pair=pair: e.tensor_tensor(
                      B.sbf.ap()[:, 0:128], B.pl("qtg", pair * 128, 128),
                      B.pl("qtg", pair * 128, 128), ALU.mult),
                  inc=f"{p}_qsq", waits=(f"{p}_kred", f"{tag}_qe{pair}")),
            em.op("tensor",
                  lambda e: e.matmul(B.bank(6, p1=1, c0=128, w=128),
                                     B.ones_bf.ap()[:, 0:1],
                                     B.sbf.ap()[:, 0:128],
                                     start=True, stop=True),
                  inc=f"{p}_qnm", waits=(f"{p}_qsq",)),
            em.op("vector",
                  lambda e: e.tensor_reduce(ev[0:1, 1701:1702],
                                            B.bank(6, p1=1, c0=128, w=128),
                                            AX.X, ALU.max),
                  inc=f"{p}_qred", waits=(f"{p}_qnm",))
            em.op("scalar",
                  lambda e: e.activation(ev[0:1, 1703:1705],
                                         ev[0:1, 1700:1702], AF.Ln,
                                         bias=0.0, scale=1.0),
                  inc=f"{p}_lnp", waits=(f"{p}_qred",))
            em.op("vector",
                  lambda e: e.tensor_reduce(ev[0:1, 1705:1706],
                                            ev[0:1, 1703:1705], AX.X, ALU.add),
                  inc=f"{p}_lns", waits=(f"{p}_lnp",))
            em.op("scalar",
                  lambda e: e.activation(B.sbf.ap()[0:1, 1024:1025],
                                         ev[0:1, 1705:1706], AF.Exp,
                                         bias=0.0, scale=0.5),
                  inc=f"{p}_cexp", waits=(f"{p}_lns",))
            em.op("tensor",
                  lambda e: e.matmul(B.bank(6, c0=256, w=1),
                                     B.ones_bf.ap()[0:1, 0:128],
                                     B.sbf.ap()[0:1, 1024:1025],
                                     start=True, stop=True),
                  inc=f"{p}_cbc", waits=(f"{p}_cexp",)),
            em.op("vector",
                  lambda e: e.scalar_tensor_tensor(
                      ev[:, 1706:1707], B.bank(6, c0=256, w=1), -1.0,
                      ev[:, 1669:1670], ALU.mult, ALU.add),
                  inc=f"{p}_negc", waits=(f"{p}_cbc", f"{tag}_mbias")),
            par = pair % 2
            sA, sB = par, 2 + par
            wsc = [f"{p}_negc"]
            if prevp:
                wsc.append(f"{prevp}_exp")
            if pair < 2:
                wsc += [f"{tag}_qe{6+par}", f"{tag}_ke{6+par}"]
            em.op("tensor",
                  lambda e, pair=pair, sA=sA: e.matmul(
                      B.bank(sA, w=128),
                      B.pl("ktg", pair * 128, 128, p0=0, p1=64),
                      B.pl("qtg", pair * 128, 128, p0=0, p1=64),
                      start=True, stop=True, tile_position=(0, 0)),
                  waits=tuple(wsc)),
            em.op("tensor",
                  lambda e, pair=pair, sB=sB: e.matmul(
                      B.bank(sB, w=128),
                      B.pl("ktg", pair * 128, 128, p0=64, p1=128),
                      B.pl("qtg", pair * 128, 128, p0=64, p1=128),
                      start=True, stop=True, tile_position=(64, 0)),
                  inc=f"{p}_sc"),
            slA = B.pl("attng", par * 128, 128)
            slB = B.pl("attng", (2 + par) * 128, 128)
            em.op("scalar",
                  lambda e, sA=sA, slA=slA: e.activation(
                      slA, B.bank(sA, w=128), AF.Exp,
                      bias=ev[:, 1706:1707], scale=1.0),
                  waits=(f"{p}_sc",) + ((f"{prevp}_ctx",) if prevp else ())),
            em.op("scalar",
                  lambda e, sB=sB, slB=slB: e.activation(
                      slB, B.bank(sB, w=128), AF.Exp,
                      bias=ev[:, 1706:1707], scale=1.0),
                  inc=f"{p}_exp")
            wctx = [f"{p}_exp"]
            if prevp:
                wctx += [f"{prevp}_mlA", f"{prevp}_mlB"]
            else:
                wctx += [f"{tag}_vge0", f"{tag}_vge1"]
            em.op("tensor",
                  lambda e, pair=pair, slA=slA: e.matmul(
                      B.bank(4, p0=0, p1=64, w=128),
                      B.pl("vg", (pair * 2) * HD, HD), slA,
                      start=True, stop=True, tile_position=(0, 0)),
                  waits=tuple(wctx)),
            em.op("tensor",
                  lambda e, slA=slA: e.matmul(
                      B.bank(6, p0=64, p1=65, w=128),
                      B.ones_bf.ap()[:, 0:1], slA,
                      start=True, stop=True, tile_position=(0, 64))),
            em.op("tensor",
                  lambda e, pair=pair, slB=slB: e.matmul(
                      B.bank(5, p0=64, p1=128, w=128),
                      B.pl("vg", (pair * 2 + 1) * HD, HD), slB,
                      start=True, stop=True, tile_position=(0, 64))),
            em.op("tensor",
                  lambda e, slB=slB: e.matmul(
                      B.bank(7, p0=32, p1=33, w=128),
                      B.ones_bf.ap()[:, 0:1], slB,
                      start=True, stop=True, tile_position=(0, 32)),
                  inc=f"{p}_ctx")
            def _recip(e, dst, srcp):
                with em.nc.allow_low_precision(reason="bf16 recipZ"):
                    return e.reciprocal(dst, srcp)
            em.op("vector",
                  lambda e: _recip(e, B.sbf.ap()[64:65, 512:640],
                                   B.bank(6, p0=64, p1=65, w=128)),
                  inc=f"{p}_rA", waits=(f"{p}_ctx",)),
            em.op("vector",
                  lambda e: _recip(e, B.sbf.ap()[32:33, 512:640],
                                   B.bank(7, p0=32, p1=33, w=128)),
                  inc=f"{p}_rB"),
            em.op("tensor",
                  lambda e: e.matmul(B.bank(6, p0=0, p1=64, w=128),
                                     B.ones_bf.ap()[64:65, 0:64],
                                     B.sbf.ap()[64:65, 512:640],
                                     start=True, stop=True),
                  inc=f"{p}_bcA", waits=(f"{p}_rA",)),
            em.op("tensor",
                  lambda e: e.matmul(B.bank(7, p0=64, p1=128, w=128),
                                     B.ones_bf.ap()[32:33, 0:64],
                                     B.sbf.ap()[32:33, 512:640],
                                     start=True, stop=True),
                  inc=f"{p}_bcB", waits=(f"{p}_rB",)),
            em.op("scalar",
                  lambda e: e.copy(ev[0:64, 1792:1920],
                                   B.bank(6, p0=0, p1=64, w=128)),
                  inc=f"{p}_cpA", waits=(f"{p}_bcA",)),
            em.op("scalar",
                  lambda e: e.copy(ev[64:128, 1792:1920],
                                   B.bank(7, p0=64, p1=128, w=128)),
                  inc=f"{p}_cpB", waits=(f"{p}_bcB",)),
            em.op("vector",
                  lambda e, pair=pair: e.tensor_tensor(
                      B.pl("ctxtg", pair * 128, 128, p0=0, p1=64),
                      B.bank(4, p0=0, p1=64, w=128), ev[0:64, 1792:1920],
                      ALU.mult),
                  inc=f"{p}_mlA", waits=(f"{p}_cpA",)),
            em.op("vector",
                  lambda e, pair=pair: e.tensor_tensor(
                      B.pl("ctxtg", pair * 128, 128, p0=64, p1=128),
                      B.bank(5, p0=64, p1=128, w=128), ev[64:128, 1792:1920],
                      ALU.mult),
                  inc=f"{p}_mlB", waits=(f"{p}_cpB",))
        for oh in range(2):
            bnk = oh
            for c in range(8):
                em.op("tensor",
                      lambda e, oh=oh, c=c, bnk=bnk: e.matmul(
                          B.bank(bnk),
                          B.pl("ctxtg", c * 128, 128),
                          B.pl("wo", c * 1024 + oh * 512, 512),
                          start=(c == 0), stop=(c == 7)),
                      inc=(f"{tag}_om{oh}" if c == 7 else None),
                      waits=(f"{tag}g7_mlA", f"{tag}g7_mlB", f"{tag}_woin")
                      if c == 0 else ())
            em.op("vector",
                  lambda e, oh=oh, bnk=bnk: e.tensor_scalar(
                      B.pl("o128", oh * 512, 512), B.bank(bnk),
                      ev[:, 1668:1669], None, ALU.mult),
                  inc=f"{tag}_oh{oh}", waits=(f"{tag}_om{oh}",))
            em.op("vector", lambda e: e.drain() or 1)
            em.op("vector",
                  lambda e, oh=oh, bnk=bnk: e.scalar_tensor_tensor(
                      B.pl("o128", 1024 + oh * 512, 512), B.bank(bnk),
                      ev[:, 1668:1669], B.pl("o128", oh * 512, 512),
                      ALU.mult, ALU.subtract),
                  inc=f"{tag}_ol{oh}", waits=(f"{tag}_oh{oh}",))
        for i in range(32):
            t, dh = i // 2, i % 2
            bnk = 2 + i % 4
            esl = B.auxe.ap()[:, t * 128:(t + 1) * 128]
            ws5 = [f"{tag}_ol0", f"{tag}_ol1"]
            if i >= 4:
                ws5.append(f"{tag}_xa{i-4}")
            em.op("tensor",
                  lambda e, esl=esl, dh=dh, bnk=bnk: e.matmul(
                      B.bank(bnk), esl, B.pl("o128", dh * 512, 512),
                      start=True, stop=False),
                  waits=tuple(ws5)),
            em.op("tensor",
                  lambda e, esl=esl, dh=dh, bnk=bnk: e.matmul(
                      B.bank(bnk), esl, B.pl("o128", 1024 + dh * 512, 512),
                      start=False, stop=True),
                  inc=f"{tag}_xm{i}")
            xsl = B.xres.ap()[:, t * DM + dh * 512: t * DM + dh * 512 + 512]
            em.op("vector",
                  lambda e, xsl=xsl, bnk=bnk: e.tensor_tensor(
                      xsl, xsl, B.bank(bnk), ALU.add),
                  inc=f"{tag}_xa{i}", waits=(f"{tag}_xm{i}",))
    em.run_block(ph_gattn)


# version 14
# speedup vs baseline: 1.0625x; 1.0625x over previous
import math
from collections import defaultdict

import numpy as np
import concourse.bass as bass
import concourse.mybir as mybir
from concourse.bass_utils import run_bass_kernel_spmd

F32 = mybir.dt.float32
BF16 = mybir.dt.bfloat16
AF = mybir.ActivationFunctionType
ALU = mybir.AluOpType
AX = mybir.AxisListType

S = 2048
DM = 1024
NH = 16
HD = 64
DC = 256
NE = 8
GS = 16
KSEL = 64
NT = S // 128
NB = S // 512
EPS = 1e-5
NEG = -1.0e30

N_CORES = 8
SEQ_PER_CORE = 2


def _aux_arrays():
    idm = np.eye(128, dtype=np.float32)
    ones = np.ones((128, 128), dtype=np.float32)
    caus = np.where(np.arange(128)[:, None] > np.arange(128)[None, :], NEG,
                    0.0).astype(np.float32)
    tril01 = (np.arange(128)[None, :] < np.arange(128)[:, None]).astype(np.float32)
    t = np.arange(128)
    u = np.arange(248)
    auxg = ((u[None, :] == (120 + t[:, None] // GS)).astype(np.float32) / GS)
    g = np.arange(128)
    tok = np.arange(S)
    auxe = (g[:, None] == (tok[None, :] // GS)).astype(np.float32)
    return {"aux_id": idm, "aux_ones": ones, "aux_caus": caus,
            "aux_tril": tril01, "aux_g": auxg, "aux_e": auxe}


DEBUG_DRAIN = False
DEBUG_DUMP = False
DEBUG_NB1 = False
ENGINES = ("tensor", "scalar", "vector", "gpsimd", "sync")


class Em:

    def __init__(self, nc):
        self.nc = nc
        self.sems = {e: nc.alloc_semaphore(f"sem_{e}") for e in ENGINES}
        self.dma_sems = {}
        self.mode = "count"
        self.cur = None
        self.cnt = defaultdict(int)
        self.dma_cnt = defaultdict(int)
        self.mark = {}
        self.water = defaultdict(int)
        self.emit_cnt = defaultdict(int)
        self.blk_no = 0

    def _dma_group(self, grp, eng):
        qc = "sw" if eng == "gpsimd" else "hw"
        if grp not in self.dma_sems:
            self.dma_sems[grp] = (self.nc.alloc_semaphore(f"dg_{grp}"), qc)
        assert self.dma_sems[grp][1] == qc, (grp, qc)
        return self.dma_sems[grp][0]

    def op(self, eng, fn, inc=None, waits=(), dma=False, grp=None):
        if dma and grp is None:
            grp = inc or "misc_" + eng
        if self.mode == "count":
            if dma:
                self._dma_group(grp, eng)
                self.dma_cnt[grp] += 1
                if inc is not None:
                    self.mark[inc] = (grp + "!dma", self.dma_cnt[grp] * 16)
            elif inc is not None:
                self.cnt[eng] += 1
                self.mark[inc] = (eng, self.cnt[eng])
            return None
        name, obj = self.cur
        if name != eng:
            return None
        for w in waits:
            prod, val = self.mark[w]
            if self.water[(eng, prod)] >= val:
                continue
            if prod.endswith("!dma"):
                obj.wait_ge(self.dma_sems[prod[:-4]][0], val)
            else:
                obj.wait_ge(self.sems[prod], val)
            self.water[(eng, prod)] = val
        instr = fn(obj)
        if dma:
            instr.then_inc(self.dma_sems[grp][0], 16)
            self.emit_cnt[grp + "!d"] += 1
        elif inc is not None:
            instr.then_inc(self.sems[eng], 1)
            self.emit_cnt[eng] += 1
        return instr

    def gate(self, waits, engines=("tensor", "scalar", "vector")):
        if self.mode == "count":
            return
        name, obj = self.cur
        if name not in engines:
            return
        for w in waits:
            prod, val = self.mark[w]
            if self.water[(name, prod)] >= val:
                continue
            if prod.endswith("!dma"):
                obj.wait_ge(self.dma_sems[prod[:-4]][0], val)
            else:
                obj.wait_ge(self.sems[prod], val)
            self.water[(name, prod)] = val

    def run_block(self, body):
        self.mode = "count"
        self.cur = None
        body(self)
        with self.nc.Block() as blk:
            for eng_name in ENGINES:
                def mk(_n):
                    def section(obj):
                        self.mode = "emit"
                        self.cur = (_n, obj)
                        body(self)
                    return section
                getattr(blk, eng_name)(mk(eng_name))
        self.blk_no += 1
        for e in ENGINES:
            assert self.emit_cnt[e] == self.cnt[e], (
                "count/emit mismatch", self.blk_no, e,
                self.emit_cnt[e], self.cnt[e])
        for key in list(self.water):
            cons, prod = key
            if not prod.endswith("!dma"):
                self.water[key] = self.cnt[prod]


POOL_COLS = 58624


def _lay(entries, extra=()):
    out, off = {}, 0
    for name, n in entries:
        out[name] = (off, n)
        off += n
    assert off <= POOL_COLS, (off, POOL_COLS)
    for name, o, n in extra:
        out[name] = (o, n)
    return out


LAY_L0A = _lay([
    ("w5", 5 * 2048),
    ("kt", 8 * 2048),
    ("v", NT * NH * 65),
    ("cqt", 2 * 2048),
    ("ckvt", 2 * 2048),
    ("xnt", 8 * 512),
    ("qt", 4 * 512),
])
LAY_L0A["attn"] = (LAY_L0A["xnt"][0], 8 * 512)
LAY_L0A["ctxt"] = (LAY_L0A["ckvt"][0], 8 * 512)
LAY_L0A["wo_a"] = (LAY_L0A["w5"][0], 4 * 1024)
LAY_L0A["wo_b"] = (LAY_L0A["w5"][0] + 3 * 2048, 4 * 1024)
_ = LAY_L0A

LAY_L0F = None
LAY_L1A = None
LAY_L1F = None


class Bufs:
    def __init__(self, nc):
        a = nc.alloc_sbuf_tensor
        self.xres = a("xres", [128, NT * DM], F32)
        self.id_bf = a("id_bf", [128, 128], BF16)
        self.ones_bf = a("ones_bf", [128, 128], BF16)
        self.caus = a("caus", [128, 128], F32)
        self.tril = a("tril", [128, 128], F32)
        self.auxg = a("auxg", [128, 248], BF16)
        self.auxe = a("auxe", [128, S], BF16)
        self.stats = a("stats", [128, 128], F32)
        self.bn12 = a("bn12", [128, 12], F32)
        self.bqkv = a("bqkv", [128, 32], F32)
        self.gbfm = a("gbfm", [128, 16], F32)
        self.rows = a("rows", [128, 3072], BF16)
        self.ev = a("ev", [128, 2048], F32)
        self.s32 = a("s32", [128, 1024], F32)
        self.sbf = a("sbf", [128, 2048], BF16)
        self.consts = a("consts", [128, 8], F32)
        self.pool = a("pool", [128, POOL_COLS], BF16)
        self.psum_t = nc.alloc_psum_tensor("ps", [128, 4096], F32)
        self.lay = None

    def bank(self, i, p0=0, p1=128, c0=0, w=512):
        return self.psum_t.ap()[p0:p1, i * 512 + c0: i * 512 + c0 + w]

    def bank_bf(self, i, p0=0, p1=128, c0=0, w=1024):
        return self.psum_t.ap().bitcast(BF16)[p0:p1, i * 1024 + c0: i * 1024 + c0 + w]

    def pl(self, name, c0=0, w=None, p0=0, p1=128):
        off, n = self.lay[name]
        if w is None:
            w = n - c0
        assert c0 + w <= n, (name, c0, w, n)
        return self.pool.ap()[p0:p1, off + c0: off + c0 + w]


def fm_col(ap_2d, c):
    return ap_2d[:, c:c + 1]


def emit_load_x(em, B, x_dram_seq, tag, waits=()):
    src = x_dram_seq.rearrange("(t p) d -> p t d", p=128)
    dst = B.xres.ap().rearrange("p (t d) -> p t d", d=DM)
    em.op("sync", lambda e: e.dma_start(dst, src), inc=f"{tag}_xin",
          waits=waits, dma=True)


def emit_store_x(em, B, out_dram_seq, tag, waits):
    src = B.xres.ap().rearrange("p (t d) -> p t d", d=DM)
    dst = out_dram_seq.rearrange("(t p) d -> p t d", p=128)
    em.op("sync", lambda e: e.dma_start(dst, src), inc=f"{tag}_xout",
          waits=waits, dma=True)


def emit_ln_stats(em, B, tag, rstd2=False, waits=()):
    st = B.stats.ap()
    for t in range(NT):
        x_t = B.xres.ap()[:, t * DM:(t + 1) * DM]
        w0 = list(waits) if t == 0 else [f"{tag}_ag{t-1}"]
        em.op("vector", lambda e, x_t=x_t: e.bn_stats(B.bn12.ap()[:, 0:6],
                                                      x_t[:, 0:512]),
              inc=f"{tag}_bna{t}", waits=tuple(w0))
        em.op("vector", lambda e, x_t=x_t: e.bn_stats(B.bn12.ap()[:, 6:12],
                                                      x_t[:, 512:1024]),
              inc=f"{tag}_bnb{t}")
        em.op("vector", lambda e, t=t: e.bn_aggr(st[:, 2 * t:2 * t + 2],
                                                 B.bn12.ap()),
              inc=f"{tag}_ag{t}", waits=(f"{tag}_bna{t}", f"{tag}_bnb{t}"))
    mv = B.stats.ap().rearrange("p (t two) -> p t two", two=2)
    em.op("vector", lambda e: e.tensor_copy(
        st[:, 32:48].unsqueeze(2), mv[:, 0:16, 0:1]),
        inc=f"{tag}_m", waits=(f"{tag}_ag{NT-1}",))
    em.op("scalar", lambda e: e.activation(
        st[:, 80:96].unsqueeze(2), mv[:, 0:16, 1:2], AF.Ln,
        bias=fm_col(B.consts.ap(), 0), scale=1.0),
        inc=f"{tag}_lnv", waits=(f"{tag}_ag{NT-1}", "k_consts"))
    em.op("scalar", lambda e: e.activation(
        st[:, 48:64], st[:, 80:96], AF.Exp, bias=0.0, scale=-0.5),
        inc=f"{tag}_rstd", waits=(f"{tag}_lnv",))
    if rstd2:
        em.op("scalar", lambda e: e.activation(
            st[:, 80:96].unsqueeze(2), mv[:, 0:16, 1:2], AF.Ln,
            bias=fm_col(B.consts.ap(), 1), scale=float(1.0 + EPS)),
            inc=f"{tag}_lnv2", waits=(f"{tag}_rstd",))
        em.op("scalar", lambda e: e.activation(
            st[:, 64:80], st[:, 80:96], AF.Exp, bias=0.0, scale=-0.5),
            inc=f"{tag}_rstd2", waits=(f"{tag}_lnv2",))


def emit_xnt_tiles(em, B, tag, tiles, dst_name, blockw, rstd_col=48, gb=False,
                   waits=(), lo=False):
    st = B.stats.ap()
    tpb = blockw // 128
    prev = None
    for t in tiles:
        rt = t % tpb
        x_t = B.xres.ap()[:, t * DM:(t + 1) * DM]
        wx = list(waits) + ([prev] if prev else [])
        if not lo:
            em.op("vector",
                  lambda e, x_t=x_t, t=t: e.tensor_scalar(
                      B.sbf.ap()[:, 0:1024], x_t, fm_col(st, 32 + t),
                      fm_col(st, rstd_col + t), ALU.subtract, ALU.mult),
                  inc=f"{tag}_xn{t}", waits=tuple(wx))
        else:
            em.op("vector",
                  lambda e, x_t=x_t, t=t: e.tensor_scalar(
                      B.s32.ap()[:, 0:1024], x_t, fm_col(st, 32 + t),
                      fm_col(st, rstd_col + t), ALU.subtract, ALU.mult),
                  inc=f"{tag}_x32{t}", waits=tuple(wx))
            em.op("vector", lambda e: e.tensor_copy(B.sbf.ap()[:, 0:1024],
                                                    B.s32.ap()[:, 0:1024]),
                  inc=f"{tag}_xn{t}", waits=(f"{tag}_x32{t}",))
            em.op("vector",
                  lambda e: e.tensor_tensor(B.sbf.ap()[:, 1024:2048],
                                            B.s32.ap()[:, 0:1024],
                                            B.sbf.ap()[:, 0:1024],
                                            ALU.subtract),
                  inc=f"{tag}_xlo{t}", waits=(f"{tag}_xn{t}",))
        parts = [("", 0, [f"{tag}_xn{t}"])]
        if lo:
            parts.append(("_lo", 1024, [f"{tag}_xlo{t}", f"{tag}_ev{t}"]))
        for sfx, srcoff, wnames in parts:
            evw = f"{tag}_ev{sfx}{t}"
            trw = f"{tag}_tr{sfx}{t}"
            for c in range(8):
                bnk = 6 + c // 4
                ws = []
                if c == 0:
                    ws = list(wnames)
                    pe = f"{tag}_evprev{t}"
                    if sfx == "" and pe in em.mark:
                        ws.append(pe)
                em.op("tensor",
                      lambda e, c=c, bnk=bnk, srcoff=srcoff: e.matmul(
                          B.bank(bnk, c0=(c % 4) * 128, w=128),
                          B.sbf.ap()[:, srcoff + c * 128:srcoff + (c + 1) * 128],
                          B.id_bf.ap()[:], start=True, stop=True),
                      inc=(trw if c == 7 else None), waits=tuple(ws))
            dst = B.pl(dst_name + sfx).rearrange(
                "p (c n) -> p c n", c=8)[:, :, rt * 128:rt * 128 + 128]
            if not gb:
                em.op("vector", lambda e, dst=dst: e.tensor_copy(
                    dst[:, 0:4],
                    B.bank(6).rearrange("p (c n) -> p c n", c=4)),
                    waits=(trw,), inc=f"{tag}_evh{sfx}{t}")
                em.op("vector", lambda e, dst=dst: e.tensor_copy(
                    dst[:, 4:8],
                    B.bank(7).rearrange("p (c n) -> p c n", c=4)),
                    inc=evw)
            else:
                for c in range(8):
                    bnk = 6 + c // 4
                    em.op("vector",
                          lambda e, c=c, bnk=bnk, dst=dst: e.tensor_scalar(
                              dst[:, c], B.bank(bnk, c0=(c % 4) * 128, w=128),
                              fm_col(B.gbfm.ap(), c),
                              fm_col(B.gbfm.ap(), 8 + c), ALU.mult, ALU.add),
                          waits=((trw,) if c == 0 else ()),
                          inc=(evw if c == 7 else None))
        idx = tiles.index(t)
        nxt = tiles[idx + 1] if idx + 1 < len(tiles) else None
        if nxt is not None and em.mode == "count":
            em.mark[f"{tag}_evprev{nxt}"] = em.mark[
                f"{tag}_ev_lo{t}" if lo else f"{tag}_ev{t}"]
        prev = f"{tag}_tr{'_lo' if lo else ''}{t}"


def emit_l0_cproj(em, B, tag):
    for b in range(NB):
        emit_xnt_tiles(em, B, f"{tag}b{b}", [b * 4 + i for i in range(4)],
                       "xnt", 512, waits=(f"{tag}_rstd", f"{tag}_cev{b-1}")
                       if b else (f"{tag}_rstd",))
        for wi, (wslot, dstn) in enumerate((("wdq", "cqt"), ("wdkv", "ckvt"))):
            for cc in range(2):
                bnk = wi * 2 + cc
                for c in range(8):
                    em.op("tensor",
                          lambda e, wi=wi, cc=cc, c=c, b=b, bnk=bnk: e.matmul(
                              B.bank(bnk),
                              B.pl("w5", wi * 2048 + c * 256 + cc * 128, 128),
                              B.pl("xnt", c * 512, 512),
                              start=(c == 0), stop=(c == 7)),
                          inc=(f"{tag}_cmm{b}_{wi}{cc}" if c == 7 else None),
                          waits=(tuple([f"{tag}b{b}_ev{b*4+3}", "l0a_w5"] +
                                       ([f"{tag}_cev{b-1}"] if b else []))
                                 if c == 0 and cc == 0 and wi == 0 else ()))
                dst = "cqt" if wi == 0 else "ckvt"
                em.op("vector",
                      lambda e, dst=dst, cc=cc, b=b, bnk=bnk: e.tensor_copy(
                          B.pl(dst, cc * 2048 + b * 512, 512), B.bank(bnk)),
                      inc=(f"{tag}_cev{b}" if wi == 1 and cc == 1 else None),
                      waits=(f"{tag}_cmm{b}_{wi}{cc}",))


def emit_l0_kv(em, B, tag):
    for j in range(32):
        hp, qb = j // 4, j % 4
        bnk = j % 2
        ws = [f"{tag}_cev3"]
        if j >= 2:
            ws.append(f"{tag}_ktev{j-2}")
        for cc in range(2):
            em.op("tensor",
                  lambda e, hp=hp, qb=qb, cc=cc, bnk=bnk: e.matmul(
                      B.bank(bnk),
                      B.pl("w5", 3 * 2048 + cc * 1024 + hp * 128, 128),
                      B.pl("ckvt", cc * 2048 + qb * 512, 512),
                      start=(cc == 0), stop=(cc == 1)),
                  inc=(f"{tag}_ktm{j}" if cc == 1 else None),
                  waits=tuple(ws) if cc == 0 else ())
        em.op("vector",
              lambda e, hp=hp, qb=qb, bnk=bnk: e.tensor_copy(
                  B.pl("kt", hp * 2048 + qb * 512, 512), B.bank(bnk)),
              inc=f"{tag}_ktev{j}", waits=(f"{tag}_ktm{j}",))
    for i in range(NT * 2):
        t, h8 = i // 2, i % 2
        bnk = 2 + i % 2
        ws = [f"{tag}_cev3"]
        if i >= 2:
            ws.append(f"{tag}_vev{i-2}")
        for cc in range(2):
            em.op("tensor",
                  lambda e, t=t, h8=h8, cc=cc, bnk=bnk: e.matmul(
                      B.bank(bnk),
                      B.pl("ckvt", cc * 2048 + t * 128, 128),
                      B.pl("w5", 4 * 2048 + cc * 1024 + h8 * 512, 512),
                      start=(cc == 0), stop=(cc == 1)),
                  inc=(f"{tag}_vm{i}" if cc == 1 else None),
                  waits=tuple(ws) if cc == 0 else ())
        em.op("vector",
              lambda e, t=t, h8=h8, bnk=bnk: e.tensor_copy(
                  B.pl("v", (t * 16 + h8 * 8) * 65, 8 * 65).rearrange(
                      "p (h n) -> p h n", h=8)[:, :, 0:64],
                  B.bank(bnk).rearrange("p (h n) -> p h n", h=8)),
              inc=f"{tag}_vev{i}", waits=(f"{tag}_vm{i}",))
    em.op("vector",
          lambda e: e.memset(B.pl("v").rearrange(
              "p (g n) -> p g n", n=65)[:, :, 64:65], 1.0),
          inc=f"{tag}_vones", waits=(f"{tag}_vev{NT*2-1}",))
    for hp in range(8):
        em.op("vector",
              lambda e, hp=hp: e.tensor_tensor(
                  B.sbf.ap()[:, 0:2048], B.pl("kt", hp * 2048, 2048),
                  B.pl("kt", hp * 2048, 2048), ALU.mult),
              inc=f"{tag}_ksq{hp}",
              waits=(f"{tag}_ktev{hp*4+3}", f"{tag}_kred{hp-1}") if hp
              else (f"{tag}_ktev{hp*4+3}",))
        for qb in range(4):
            em.op("tensor",
                  lambda e, qb=qb: e.matmul(
                      B.bank(4 + qb, p1=1),
                      B.ones_bf.ap()[:, 0:1], B.sbf.ap()[:, qb * 512:(qb + 1) * 512],
                      start=True, stop=True),
                  inc=(f"{tag}_knm{hp}" if qb == 3 else None),
                  waits=(f"{tag}_ksq{hp}",) if qb == 0 else ())
        em.op("vector",
              lambda e, hp=hp: e.tensor_reduce(
                  B.ev.ap()[0:1, 1024 + hp:1025 + hp],
                  B.psum_t.ap()[0:1, 4 * 512:8 * 512], AX.X, ALU.max),
              inc=f"{tag}_kred{hp}", waits=(f"{tag}_knm{hp}",))
    em.op("scalar", lambda e: e.activation(
        B.ev.ap()[0:1, 1032:1040], B.ev.ap()[0:1, 1024:1032], AF.Ln,
        bias=0.0, scale=1.0),
        inc=f"{tag}_lnk", waits=(f"{tag}_kred7",))


def emit_attn_core(em, B, tag, *, causal, n_hp, ktmax_f, kt_slice, v_slice,
                   qt_emit, qt_bias, wo_slice, n_wo_chunks, bias_rows,
                   wo_marks, xadd_acc):
    ev = B.ev.ap()
    st_attn = lambda slot: B.pl("attn", slot * 512, 512)

    def pname(qb, hp):
        return f"{tag}a{qb}_{hp}"

    hps_prev_tail = [None]

    nb_eff = 1 if (DEBUG_DUMP and DEBUG_NB1 and tag == "s0l1ah0") else NB
    for qb in range(nb_eff):
        ktmax = ktmax_f(qb)
        assert ktmax % 2 == 0
        npair = ktmax // 2
        waves = [range(0, min(4, n_hp))] + \
            ([range(4, 8)] if n_hp == 8 else [])

        def emit_wave(wv, qb=qb, ktmax=ktmax):
            wvl = list(wv)
            negbase = 1052 if wvl[0] == 0 else 1056
            for i, hp in enumerate(wvl):
                p = pname(qb, hp)
                qslot = hp % 4
                ws = []
                if hp == wvl[0]:
                    if hp >= 4:
                        ws.append(f"{pname(qb, hp-1)}_expg{ktmax-1}")
                    elif qb > 0:
                        ws.append(f"{pname(qb-1, n_hp-1)}_expg"
                                  f"{ktmax_f(qb-1)-1}")
                qt_emit(em, p, tuple(ws), qb, hp)
                prevu = None
                if n_hp == 8:
                    prevu = pname(qb, hp - 4) if hp >= 4 else (
                        pname(qb - 1, hp + 4) if qb > 0 else None)
                else:
                    prevu = pname(qb - 1, hp) if qb > 0 else None
                qbias = qt_bias(hp)
                qtw = [f"{p}_qtm"]
                if prevu:
                    qtw.append(f"{prevu}_scl")
                if isinstance(qbias, float):
                    em.op("scalar",
                          lambda e, qslot=qslot, hp=hp, qbias=qbias:
                          e.activation(
                              B.pl("qt", qslot * 512, 512), B.bank(hp % 4),
                              AF.Copy, bias=qbias, scale=0.125),
                          inc=f"{p}_qtev", waits=tuple(qtw))
                else:
                    em.op("vector",
                          lambda e, qslot=qslot, hp=hp, qbias=qbias:
                          e.tensor_scalar(
                              B.pl("qt", qslot * 512, 512), B.bank(hp % 4),
                              qbias, 0.125, ALU.add, ALU.mult),
                          inc=f"{p}_qtev", waits=tuple(qtw))
                em.op("vector",
                      lambda e, qslot=qslot: e.tensor_tensor(
                          B.sbf.ap()[:, qslot * 512:(qslot + 1) * 512],
                          B.pl("qt", qslot * 512, 512),
                          B.pl("qt", qslot * 512, 512), ALU.mult),
                      inc=f"{p}_qsq", waits=(f"{p}_qtev",))
                bnk = 6 + i // 2
                row = 32 * (i % 2)
                wqn = [f"{p}_qsq"]
                if hp == wvl[0]:
                    lastp = hps_prev_tail[0]
                    if lastp:
                        wqn.append(f"{lastp}_cp")
                    if wvl[0] == 4:
                        wqn.append(f"{pname(qb, 3)}_cp")
                    if qb > 0 and wvl[0] == 0:
                        wqn += [f"{tag}a{qb-1}_xadd6", f"{tag}a{qb-1}_xadd7"]
                em.op("tensor",
                      lambda e, qslot=qslot, bnk=bnk, row=row: e.matmul(
                          B.bank(bnk, p0=row, p1=row + 1),
                          B.ones_bf.ap()[:, 0:1],
                          B.sbf.ap()[:, qslot * 512:(qslot + 1) * 512],
                          start=True, stop=True,
                          tile_position=(0, row)),
                      inc=f"{p}_qnm", waits=tuple(wqn))
                em.op("vector",
                      lambda e, bnk=bnk, row=row, i=i: e.tensor_reduce(
                          ev[row:row + 1, 1040 + i:1041 + i],
                          B.bank(bnk, p0=row, p1=row + 1), AX.X, ALU.max),
                      inc=f"{p}_qred", waits=(f"{p}_qnm",))
            for i, hp in enumerate(wvl):
                row = 32 * (i % 2)
                if row:
                    em.op("vector",
                          lambda e, i=i: e.tensor_copy(
                              ev[0:1, 1040 + i:1041 + i],
                              ev[32:33, 1040 + i:1041 + i]),
                          inc=f"{pname(qb, wvl[i])}_qmv",
                          waits=(f"{pname(qb, wvl[i])}_qred",))
            last_red = [f"{pname(qb, h)}_qred" for h in wvl] + \
                [f"{pname(qb, wvl[i])}_qmv" for i in range(len(wvl))
                 if 32 * (i % 2)]
            nw = len(wvl)
            em.op("scalar",
                  lambda e, nw=nw: e.activation(
                      ev[0:1, 1044:1044 + nw], ev[0:1, 1040:1040 + nw],
                      AF.Ln, bias=0.0, scale=1.0),
                  inc=f"{tag}a{qb}w{wvl[0]}_lnq", waits=tuple(last_red))
            em.op("vector",
                  lambda e, nw=nw, w0=wvl[0]: e.tensor_tensor(
                      ev[0:1, 1048:1048 + nw], ev[0:1, 1044:1044 + nw],
                      ev[0:1, 1032 + w0:1032 + w0 + nw], ALU.add),
                  inc=f"{tag}a{qb}w{wvl[0]}_lns",
                  waits=(f"{tag}a{qb}w{wvl[0]}_lnq", f"{tag}_lnk"))
            em.op("scalar",
                  lambda e, nw=nw, w0=wvl[0]: e.activation(
                      B.rows.ap()[64:65, 1024 + w0:1024 + w0 + nw],
                      ev[0:1, 1048:1048 + nw], AF.Exp, bias=0.0, scale=0.5),
                  inc=f"{tag}a{qb}w{wvl[0]}_cexp",
                  waits=(f"{tag}a{qb}w{wvl[0]}_lns",))
            for i, hp in enumerate(wvl):
                em.op("tensor",
                      lambda e, i=i, hp=hp: e.matmul(
                          B.bank(6, c0=64 + i, w=1),
                          B.ones_bf.ap()[64:65, 0:128],
                          B.rows.ap()[64:65, 1024 + hp:1025 + hp],
                          start=True, stop=True),
                      inc=(f"{tag}a{qb}w{wvl[0]}_cbc" if i == nw - 1
                           else None),
                      waits=(f"{tag}a{qb}w{wvl[0]}_cexp",) if i == 0 else ())
            em.op("vector",
                  lambda e, nw=nw, negbase=negbase: e.tensor_scalar(
                      ev[:, negbase:negbase + nw], B.bank(6, c0=64, w=nw),
                      -1.0, None, ALU.mult),
                  inc=f"{tag}a{qb}w{wvl[0]}_negc",
                  waits=(f"{tag}a{qb}w{wvl[0]}_cbc",))
            if em.mode == "count":
                em.mark[f"{tag}a{qb}w{wvl[0]}_negcrd"] = \
                    em.mark[f"{tag}a{qb}w{wvl[0]}_negc"]

        emit_wave(waves[0])

        pending = [None]

        def emit_epi1(hp, qb=qb, ktmax=ktmax):
            p = pname(qb, hp)
            ctxe = f"{p}_ctxq{ktmax-1}"

            em.op("scalar",
                  lambda e: e.activation(ev[64:65, 0:1024],
                                         B.psum_t.ap()[64:65, 2048:3072],
                                         AF.Ln, bias=0.0,
                                         scale=7.1700422e15),
                  inc=f"{p}_lnz", waits=(ctxe,))
            em.op("scalar",
                  lambda e: e.activation(B.rows.ap()[64:65, 0:1024],
                                         ev[64:65, 0:1024], AF.Exp,
                                         bias=B.consts.ap()[64:65, 2:3],
                                         scale=-1.0),
                  inc=f"{p}_recip", waits=(f"{p}_lnz",))
            wbc = [f"{p}_recip",
                   f"{tag}a{qb}w{4 if (n_hp == 8 and hp >= 4) else 0}_negcrd"]
            em.op("tensor",
                  lambda e: e.matmul(B.bank(6, p0=0, p1=64),
                                     B.ones_bf.ap()[64:65, 0:64],
                                     B.rows.ap()[64:65, 0:512],
                                     start=True, stop=True),
                  inc=f"{p}_bcA", waits=tuple(wbc))
            em.op("tensor",
                  lambda e: e.matmul(B.bank(7, p0=0, p1=64),
                                     B.ones_bf.ap()[64:65, 0:64],
                                     B.rows.ap()[64:65, 512:1024],
                                     start=True, stop=True),
                  inc=f"{p}_bcB")
            em.op("scalar",
                  lambda e: e.copy(ev[0:64, 0:1024],
                                   B.psum_t.ap()[0:64, 3072:4096]),
                  inc=f"{p}_cp", waits=(f"{p}_bcB",))
            em.op("vector",
                  lambda e, hp=hp: e.tensor_tensor(
                      B.pl("ctxt", hp * 512, 512, p0=0, p1=64),
                      B.bank(4, p0=0, p1=64), ev[0:64, 0:512], ALU.mult),
                  inc=f"{p}_evA", waits=(f"{p}_cp", ctxe))
            em.op("vector",
                  lambda e: e.tensor_tensor(
                      B.rows.ap()[0:64, 2048:2560],
                      B.bank(5, p0=0, p1=64), ev[0:64, 512:1024], ALU.mult),
                  inc=f"{p}_evB")

        def emit_epi2(hp, qb=qb):
            p = pname(qb, hp)
            em.op("tensor",
                  lambda e: e.matmul(B.bank(5, p0=64, p1=128),
                                     B.id_bf.ap()[0:64, 0:64],
                                     B.rows.ap()[0:64, 2048:2560],
                                     start=True, stop=True,
                                     tile_position=(0, 64)),
                  inc=f"{p}_mv", waits=(f"{p}_evB",))
            em.op("vector",
                  lambda e, hp=hp: e.tensor_copy(
                      B.pl("ctxt", hp * 512, 512, p0=64, p1=128),
                      B.bank(5, p0=64, p1=128)),
                  inc=f"{p}_cpB2", waits=(f"{p}_mv",))

        def flush_epi():
            if pending[0] is not None:
                emit_epi1(pending[0])
                emit_epi2(pending[0])
                pending[0] = None

        for hp in range(n_hp):
            if n_hp == 8 and hp == 4:
                flush_epi()
                emit_wave(waves[1])
            p = pname(qb, hp)
            wavebase = 0 if hp < 4 else 4
            negbase = 1052 if wavebase == 0 else 1056
            negcol = negbase + (hp - wavebase)
            prevp = pname(qb, hp - 1) if hp > 0 else \
                (pname(qb - 1, n_hp - 1) if qb > 0 else None)
            prev_ktmax = (ktmax if hp > 0 else ktmax_f(qb - 1)) \
                if prevp else 0
            def emit_ctx(kt, hp=hp, p=p, prevp=prevp, ktmax=ktmax):
                first, last = (kt == 0), (kt == ktmax - 1)
                for head, cbnk in (("A", 4), ("B", 5)):
                    slot = 2 * (kt % 2) + (0 if head == "A" else 1)
                    wctx = [f"{p}_expg{kt}"]
                    if kt == 0 and prevp:
                        wctx += [f"{prevp}_evA", f"{prevp}_cpB2"]
                    em.op("tensor",
                          lambda e, kt=kt, hp=hp, head=head, cbnk=cbnk,
                          slot=slot, first=first, last=last: e.matmul(
                              B.bank(cbnk, p0=0, p1=65),
                              v_slice(hp, kt, head),
                              B.pl("attn", slot * 512, 512),
                              start=first, stop=last),
                          inc=(f"{p}_ctxq{kt}" if head == "B" else None),
                          waits=tuple(wctx) if head == "A" else ())

            for kt in range(ktmax):
                bA = 2 * (kt % 2)
                bB = bA + 1
                if kt == 0 and pending[0] is not None:
                    emit_epi1(pending[0])
                for hi, (head, pp0, pp1, bnk) in enumerate(
                        (("A", 0, 64, bA), ("B", 64, 128, bB))):
                    wsc = []
                    if hi == 0:
                        if kt >= 2:
                            wsc = [f"{p}_expg{kt-2}"]
                        else:
                            wsc = [f"{p}_qtev"]
                            if hp in (0, 4):
                                wsc.append(f"{pname(qb, min(hp + 3, n_hp - 1))}_qtev")
                            if prevp:
                                wsc.append(
                                    f"{prevp}_expg{prev_ktmax - 2 + kt}")
                    em.op("tensor",
                          lambda e, kt=kt, hp=hp, head=head, pp0=pp0,
                          pp1=pp1, bnk=bnk: e.matmul(
                              B.bank(bnk), kt_slice(hp, kt, head),
                              B.pl("qt", (hp % 4) * 512, 512,
                                   p0=pp0, p1=pp1),
                              start=True, stop=True,
                              tile_position=(pp0, 0)),
                          inc=(f"{p}_scq{kt}" if hi == 1 else None),
                          waits=tuple(wsc))
                if kt == ktmax - 1 and em.mode == "count":
                    em.mark[f"{p}_scl"] = em.mark[f"{p}_scq{kt}"]
                rel = (kt - 4 * qb) if causal else -1
                straddle = causal and rel >= 0
                expw = [f"{p}_scq{kt}", f"{tag}a{qb}w{wavebase}_negc"]
                if kt >= 2:
                    expw.append(f"{p}_ctxq{kt-2}")
                elif prevp:
                    expw.append(f"{prevp}_ctxq{prev_ktmax - 2 + kt}")
                if not straddle:
                    em.op("scalar",
                          lambda e, bA=bA: e.activation(
                              B.pl("attn", bA * 512, 1024),
                              B.psum_t.ap()[:, bA * 512:(bA + 2) * 512],
                              AF.Exp, bias=fm_col(ev, negcol), scale=1.0),
                          inc=f"{p}_expg{kt}", waits=tuple(expw))
                else:
                    for hi2, bnk in enumerate((bA, bB)):
                        em.op("vector",
                              lambda e, bnk=bnk, rel=rel: e.tensor_tensor(
                                  B.bank(bnk, c0=rel * 128, w=128),
                                  B.bank(bnk, c0=rel * 128, w=128),
                                  B.caus.ap()[:], ALU.add),
                              inc=(f"{p}_msk{kt}" if hi2 == 1 else None),
                              waits=(f"{p}_scq{kt}",) if hi2 == 0 else ())
                    msw = [f"{p}_msk{kt}"] + expw[1:]
                    if rel > 0:
                        for hi2, bnk in enumerate((bA, bB)):
                            em.op("vector",
                                  lambda e, bnk=bnk, rel=rel: e.memset(
                                      B.pl("attn", bnk * 512, rel * 128),
                                      0.0),
                                  inc=f"{p}_ms{kt}",
                                  waits=tuple(msw) if hi2 == 0 else ())
                    expw2 = [f"{p}_msk{kt}"] + expw[1:] + \
                        ([f"{p}_ms{kt}"] if rel > 0 else [])
                    c0 = max(rel, 0) * 128
                    for hi2, bnk in enumerate((bA, bB)):
                        em.op("scalar",
                              lambda e, bnk=bnk, c0=c0: e.activation(
                                  B.pl("attn", bnk * 512 + c0, 512 - c0),
                                  B.bank(bnk, c0=c0, w=512 - c0),
                                  AF.Exp, bias=fm_col(ev, negcol),
                                  scale=1.0),
                              inc=(f"{p}_expg{kt}" if hi2 == 1 else None),
                              waits=tuple(expw2) if hi2 == 0 else ())
                if kt == 1 and pending[0] is not None:
                    emit_epi2(pending[0])
                    pending[0] = None
                if kt >= 1:
                    emit_ctx(kt - 1)
            emit_ctx(ktmax - 1)
            pending[0] = hp
        flush_epi()
        hps_prev_tail[0] = pname(qb, n_hp - 1)
        if DEBUG_DUMP and tag == "s0l1ah0" and qb == 0:
            _lp = pname(qb, n_hp - 1)
            if not hasattr(em, "_dbg"):
                em._dbg = {}
                em._dbg["c"] = em.nc.dram_tensor(
                    "dbg_ctxt", [128, 4096], F32, kind="ExternalOutput")
                em._dbg["q"] = em.nc.dram_tensor(
                    "dbg_qt", [128, 2048], F32, kind="ExternalOutput")
                em._dbg["r"] = em.nc.dram_tensor(
                    "dbg_rz", [128, 3072], F32, kind="ExternalOutput")
            dbg_c, dbg_q, dbg_r = em._dbg["c"], em._dbg["q"], em._dbg["r"]
            em.op("gpsimd", lambda e: e.dma_start(
                dbg_c.ap(), B.pl("ctxt", 0, 4096)),
                inc="dbgct", dma=True, grp="dbg",
                waits=(f"{_lp}_evA", f"{_lp}_cpB2"))
            em.op("gpsimd", lambda e: e.dma_start(
                dbg_q.ap(), B.pl("qt", 0, 2048)),
                inc="dbgqt", dma=True, grp="dbg",
                waits=(f"{_lp}_qtev",))
            em.op("gpsimd", lambda e: e.dma_start(
                dbg_r.ap(), B.rows.ap()),
                inc="dbgrz", dma=True, grp="dbg",
                waits=(f"{_lp}_recip",))
        lastp = pname(qb, n_hp - 1)
        for i in range(8):
            rt, oh = i // 2, i % 2
            bnk = 6 + i % 2
            wsj = [f"{lastp}_evA", f"{lastp}_cpB2", f"{lastp}_cp"] + \
                list(wo_marks)
            if i >= 2:
                wsj.append(f"{tag}a{qb}_xadd{i-2}")
            brows = bias_rows(oh)
            nmm = n_wo_chunks + len(brows)
            k = 0
            for ci in range(n_wo_chunks):
                em.op("tensor",
                      lambda e, ci=ci, rt=rt, oh=oh, bnk=bnk, k=k, nmm=nmm:
                      e.matmul(
                          B.bank(bnk), B.pl("ctxt", ci * 512 + rt * 128, 128),
                          wo_slice(ci, oh), start=(k == 0),
                          stop=(k == nmm - 1)),
                      inc=(f"{tag}a{qb}_opm{i}" if k == nmm - 1 else None),
                      waits=tuple(wsj) if k == 0 else ())
                k += 1
            for br in brows:
                em.op("tensor",
                      lambda e, br=br, bnk=bnk, k=k, nmm=nmm: e.matmul(
                          B.bank(bnk), B.ones_bf.ap()[0:1, 0:128], br,
                          start=(k == 0), stop=(k == nmm - 1)),
                      inc=(f"{tag}a{qb}_opm{i}" if k == nmm - 1 else None))
                k += 1
            xsl = B.xres.ap()[:, (qb * 4 + rt) * DM + oh * 512:
                              (qb * 4 + rt) * DM + oh * 512 + 512]
            em.op("vector",
                  lambda e, xsl=xsl, bnk=bnk: e.tensor_tensor(
                      xsl, xsl, B.bank(bnk), ALU.add),
                  inc=f"{tag}a{qb}_xadd{i}", waits=(f"{tag}a{qb}_opm{i}",))


def emit_l0a_attn(em, B, tag, wo_marks):
    def qt_emit(em, p, ws, qb, hp):
        ws = ws + ("l0a_w5", f"{tag}_cev3")
        for cc in range(2):
            em.op("tensor",
                  lambda e, cc=cc, hp=hp, qb=qb: e.matmul(
                      B.bank(hp % 4),
                      B.pl("w5", 2 * 2048 + cc * 1024 + hp * 128, 128),
                      B.pl("cqt", cc * 2048 + qb * 512, 512),
                      start=(cc == 0), stop=(cc == 1)),
                  inc=(f"{p}_qtm" if cc == 1 else None),
                  waits=ws if cc == 0 else ())

    emit_attn_core(
        em, B, tag, causal=True, n_hp=8, ktmax_f=lambda qb: 4 * (qb + 1),
        kt_slice=lambda hp, kt, part: B.pl(
            "kt", hp * 2048 + kt * 128, 128,
            p0=0 if part == "A" else 64, p1=64 if part == "A" else 128),
        v_slice=lambda hp, kt, part: B.pl(
            "v", (kt * 16 + hp * 2 + (0 if part == "A" else 1)) * 65, 65),
        qt_emit=qt_emit, qt_bias=lambda hp: 0.0,
        wo_slice=lambda ci, oh: B.pl("wo_a", ci * 1024 + oh * 512, 512)
        if ci < 4 else B.pl("wo_b", (ci - 4) * 1024 + oh * 512, 512),
        n_wo_chunks=8, bias_rows=lambda oh: [], wo_marks=wo_marks,
        xadd_acc=True)


WSPECS_L0A = [
    ("L0_attn_W_dq", "w5", 0, 8),
    ("L0_attn_W_dkv", "w5", 2048, 8),
    ("L0_attn_W_uq", "w5", 2 * 2048, 2),
    ("L0_attn_W_uk", "w5", 3 * 2048, 2),
    ("L0_attn_W_uv", "w5", 4 * 2048, 2),
]


def build_core_program(n_seq=SEQ_PER_CORE,
                       phases=("l0a", "l0f", "l1a", "l1f")):
    nc = bass.Bass("TRN2", target_bir_lowering=False)
    x_d = nc.dram_tensor("x", [n_seq, S, DM], F32, kind="ExternalInput")
    out_d = nc.dram_tensor("out", [n_seq, S, DM], F32, kind="ExternalOutput")
    aux = {k: nc.dram_tensor(k, list(v.shape), F32, kind="ExternalInput")
           for k, v in _aux_arrays().items()}
    W = {}
    wshapes = {"L0_attn_W_dq": [DM, DC], "L0_attn_W_dkv": [DM, DC],
               "L0_attn_W_uq": [DC, DM], "L0_attn_W_uk": [DC, DM],
               "L0_attn_W_uv": [DC, DM], "L0_attn_W_o": [DM, DM],
               "L0_ffn_gate_w": [DM, NE], "L0_ffn_gate_b": [NE],
               "L0_ffn_expert_w": [NE, DM, DM], "L0_ffn_expert_b": [NE, DM],
               "L1_attn_ln_gamma": [DM], "L1_attn_ln_beta": [DM],
               "L1_attn_W_qkv": [3 * DM, DM], "L1_attn_b_qkv": [3 * DM],
               "L1_attn_W_out": [DM, DM], "L1_attn_b_out": [DM],
               "L1_ffn_Wq": [DM, DM], "L1_ffn_Wk": [DM, DM],
               "L1_ffn_Wv": [DM, DM], "L1_ffn_Wo": [DM, DM],
               "L1_ffn_Ws1": [DM, DC], "L1_ffn_bs1": [DC],
               "L1_ffn_Ws2": [DC, 1], "L1_ffn_bs2": [1]}
    need = set()
    if "l0a" in phases:
        need |= {"L0_attn_W_dq", "L0_attn_W_uq", "L0_attn_W_dkv",
                 "L0_attn_W_uk", "L0_attn_W_uv", "L0_attn_W_o"}
    if "l0f" in phases:
        need |= {"L0_ffn_gate_w", "L0_ffn_gate_b", "L0_ffn_expert_w",
                 "L0_ffn_expert_b"}
    if "l1a" in phases:
        need |= {"L1_attn_ln_gamma", "L1_attn_ln_beta", "L1_attn_W_qkv",
                 "L1_attn_b_qkv", "L1_attn_W_out", "L1_attn_b_out"}
    if "l1f" in phases:
        need |= {"L1_ffn_Wq", "L1_ffn_Wk", "L1_ffn_Wv", "L1_ffn_Wo",
                 "L1_ffn_Ws1", "L1_ffn_bs1", "L1_ffn_Ws2", "L1_ffn_bs2"}
    for name in sorted(need):
        W[name] = nc.dram_tensor(name, wshapes[name], F32,
                                 kind="ExternalInput")

    B = Bufs(nc)
    em = Em(nc)

    def load_aux(em):
        em.op("vector", lambda e: e.memset(B.consts.ap()[:, 0:1], float(EPS)))
        em.op("vector", lambda e: e.memset(B.consts.ap()[:, 1:2],
                                           float(EPS * EPS)))
        em.op("vector", lambda e: e.memset(B.consts.ap()[:, 2:3], 36.5),
              inc="k_consts")
        em.op("gpsimd", lambda e: e.dma_start(B.id_bf.ap(), aux["aux_id"].ap()),
              inc="aux_id", dma=True)
        em.op("gpsimd", lambda e: e.dma_start(B.ones_bf.ap(),
                                              aux["aux_ones"].ap()),
              inc="aux_ones", dma=True)
        em.op("sync", lambda e: e.dma_start(B.caus.ap(), aux["aux_caus"].ap()),
              inc="aux_caus", dma=True)
        em.op("sync", lambda e: e.dma_start(B.tril.ap(), aux["aux_tril"].ap()),
              inc="aux_tril", dma=True)
        em.op("gpsimd", lambda e: e.dma_start(B.auxg.ap(), aux["aux_g"].ap()),
              inc="aux_g", dma=True)
        em.op("gpsimd", lambda e: e.dma_start(B.auxe.ap(), aux["aux_e"].ap()),
              inc="aux_e", dma=True)

    def load_w5(em):
        for name, slot, coff, nchunk in WSPECS_L0A:
            src = W[name].ap().rearrange("(c p) n -> p c n", p=128)
            dst = B.pl(slot, coff, nchunk * (2048 // nchunk)).rearrange(
                "p (c n) -> p c n", c=nchunk)
            em.op("gpsimd", lambda e, src=src, dst=dst: e.dma_start(dst, src),
                  inc=("l0a_w5" if name == "L0_attn_W_uv" else None), dma=True,
                  grp="w5")

    def run_seq(seq):
        tag = f"s{seq}"
        B.lay = LAY_L0A

        def ph_load(em):
            emit_load_x(em, B, x_d.ap()[seq], tag,
                        waits=(f"s{seq-1}_xout",) if seq > 0 else ())
            if seq == 0:
                load_aux(em)
            if "l0a" in phases:
                load_w5(em)
        em.run_block(ph_load)

        if "l0a" in phases:
            t0 = tag + "l0a"

            def ph_ln(em):
                em.gate(("aux_id", "aux_ones", "aux_caus", "aux_tril",
                         "aux_g", "aux_e"))
                emit_ln_stats(em, B, t0, waits=(f"{tag}_xin", "k_consts"))
            em.run_block(ph_ln)
            em.run_block(lambda em: emit_l0_cproj(em, B, t0))

            def ph_kv(em):
                emit_l0_kv(em, B, t0)
                src = W["L0_attn_W_o"].ap().rearrange("(c p) n -> p c n", p=128)
                em.op("gpsimd",
                      lambda e: e.dma_start(
                          B.pl("wo_a").rearrange("p (c n) -> p c n", c=4),
                          src[:, 0:4]),
                      inc="l0a_woa", waits=(f"{t0}_cmm3_11",), dma=True)
                em.op("gpsimd",
                      lambda e: e.dma_start(
                          B.pl("wo_b").rearrange("p (c n) -> p c n", c=4),
                          src[:, 4:8]),
                      inc="l0a_wob", waits=(f"{t0}_ktm31", f"{t0}_vm31"),
                      dma=True)
            em.run_block(ph_kv)
            em.run_block(lambda em: emit_l0a_attn(em, B, t0,
                                                  ("l0a_woa", "l0a_wob")))

        if "l0f" in phases:
            B.lay = LAY_L0F
            emit_l0_ffn(em, B, tag + "l0f", W, f"{tag}_xin")

        if "l1a" in phases:
            B.lay = LAY_L1A
            if DEBUG_DRAIN:
                def ph_drain(em):
                    if em.mode == "count":
                        return
                    name, obj = em.cur
                    for g, (sem, qc) in em.dma_sems.items():
                        obj.wait_ge(sem, em.dma_cnt[g] * 16)
                em.run_block(ph_drain)
            emit_l1_attn(em, B, tag + "l1a", W, f"{tag}_xin")

        if "l1f" in phases:
            B.lay = LAY_L1F
            emit_l1_ffn(em, B, tag + "l1f", W, f"{tag}_xin")

        if DEBUG_DUMP and seq == 0:
            dbg_pool = nc.dram_tensor("dbg_pool", [128, POOL_COLS], F32,
                                      kind="ExternalOutput")
            dbg_rows = nc.dram_tensor("dbg_rows", [128, 3072], F32,
                                      kind="ExternalOutput")
            dbg_misc = nc.dram_tensor("dbg_misc", [128, 2048 + 1024 + 128 + 32
                                                   + 16 + 2048], F32,
                                      kind="ExternalOutput")

            def ph_dump(em):
                em.op("gpsimd", lambda e: e.dma_start(
                    dbg_pool.ap(), B.pool.ap()), inc="dbg0", dma=True,
                    grp="dbg")
                em.op("gpsimd", lambda e: e.dma_start(
                    dbg_rows.ap(), B.rows.ap()), inc="dbg1", dma=True,
                    grp="dbg")
                m = dbg_misc.ap()
                em.op("gpsimd", lambda e: e.dma_start(
                    m[:, 0:2048], B.ev.ap()), dma=True, grp="dbg")
                em.op("gpsimd", lambda e: e.dma_start(
                    m[:, 2048:3072], B.s32.ap()), dma=True, grp="dbg")
                em.op("gpsimd", lambda e: e.dma_start(
                    m[:, 3072:3200], B.stats.ap()), dma=True, grp="dbg")
                em.op("gpsimd", lambda e: e.dma_start(
                    m[:, 3200:3232], B.bqkv.ap()), dma=True, grp="dbg")
                em.op("gpsimd", lambda e: e.dma_start(
                    m[:, 3232:3248], B.gbfm.ap()), dma=True, grp="dbg")
                em.op("gpsimd", lambda e: e.dma_start(
                    m[:, 3248:5296], B.sbf.ap()), inc="dbg2", dma=True,
                    grp="dbg")
            em.run_block(ph_dump)

        em.run_block(lambda em: emit_store_x(em, B, out_d.ap()[seq], tag, ()))

    for seq in range(n_seq):
        run_seq(seq)

    with nc.Block() as blk:
        @blk.sync
        def _(e):
            for grp, (sem, qc) in em.dma_sems.items():
                if qc == "hw":
                    e.wait_ge(sem, em.dma_cnt[grp] * 16)
        @blk.gpsimd
        def _(e):
            for grp, (sem, qc) in em.dma_sems.items():
                if qc == "sw":
                    e.wait_ge(sem, em.dma_cnt[grp] * 16)
    return nc


def _shard_inputs(inputs, n_seq=SEQ_PER_CORE):
    aux = _aux_arrays()
    maps = []
    x = np.ascontiguousarray(inputs["L0_attn_x"])
    for c in range(N_CORES):
        m = {"x": x[c * n_seq:(c + 1) * n_seq]}
        for k, v in inputs.items():
            if k != "L0_attn_x":
                m[k] = np.ascontiguousarray(v)
        m.update(aux)
        maps.append(m)
    return maps


def kernel(**inputs):
    nc = build_core_program()
    maps = _shard_inputs(inputs)
    declared = set()
    for a in nc.m.functions[0].allocations:
        try:
            declared.add(a.memorylocations[0].name)
        except AttributeError:
            pass
    maps = [{k: v for k, v in m.items() if k in declared} for m in maps]
    res = run_bass_kernel_spmd(nc, maps, core_ids=list(range(N_CORES)))
    return np.concatenate([r["out"] for r in res.results], axis=0)


LAY_L0F = _lay([
    ("xnt", 8 * 2048),
    ("xnt_lo", 8 * 2048),
    ("gw", 2 * 64),
    ("wt8", 2048),
    ("be8", 2 * 1024),
    ("wfb", 128),
])
LAY_L0F["ew"] = (LAY_L0F["xnt_lo"][0], 2 * 8192)


def emit_l0_ffn(em, B, tag, Wd, xin_mark):
    ev = B.ev.ap()
    s32 = B.s32.ap()
    def ph_ln(em):
        em.gate(("aux_id", "aux_ones", "aux_caus", "aux_tril",
                 "aux_g", "aux_e"))
        emit_ln_stats(em, B, tag, waits=(xin_mark, "k_consts"))
    em.run_block(ph_ln)

    def ph_xnt(em):
        emit_xnt_tiles(em, B, f"{tag}x", list(range(NT)), "xnt", 2048,
                       waits=(f"{tag}_rstd",), lo=True)
    em.run_block(ph_xnt)

    def ph_gate(em):
        src = Wd["L0_ffn_gate_w"].ap().rearrange("(c p) n -> p c n", p=128)
        em.op("sync", lambda e: e.dma_start(
            s32[:, 0:64].rearrange("p (c n) -> p c n", c=8), src),
            inc=f"{tag}_gwin", dma=True)
        em.op("scalar", lambda e: e.copy(B.pl("gw", 0, 64), s32[:, 0:64]),
              inc=f"{tag}_gwh", waits=(f"{tag}_gwin",))
        em.op("vector", lambda e: e.tensor_tensor(
            B.pl("gw", 64, 64), s32[:, 0:64], B.pl("gw", 0, 64), ALU.subtract),
            inc=f"{tag}_gwl", waits=(f"{tag}_gwh",))
        srcb = Wd["L0_ffn_gate_b"].ap().unsqueeze(1)
        em.op("sync", lambda e: e.dma_start(ev[0:8, 1984:1985], srcb),
              inc=f"{tag}_gbin", dma=True)
        for qb in range(4):
            n = 0
            for slot, wi in (("xnt", 0), ("xnt", 64), ("xnt_lo", 0)):
                for c in range(8):
                    last = (n == 23)
                    em.op("tensor",
                          lambda e, qb=qb, slot=slot, wi=wi, c=c, n=n,
                          last=last: e.matmul(
                              B.bank(qb, p1=8),
                              B.pl("gw", wi + c * 8, 8),
                              B.pl(slot, c * 2048 + qb * 512, 512),
                              start=(n == 0), stop=last),
                          inc=(f"{tag}_glm{qb}" if last else None),
                          waits=((f"{tag}_gwl", f"{tag}x_ev_lo{NT-1}")
                                 if n == 0 else ()))
                    n += 1
        for qb in range(4):
            em.op("vector",
                  lambda e, qb=qb: e.tensor_scalar(
                      ev[0:8, qb * 512:(qb + 1) * 512], B.bank(qb, p1=8),
                      ev[0:8, 1984:1985], None, ALU.add),
                  inc=f"{tag}_glev{qb}",
                  waits=(f"{tag}_glm{qb}", f"{tag}_gbin"))
        em.op("scalar", lambda e: e.copy(B.sbf.ap()[0:8, 0:2048],
                                         ev[0:8, 0:2048]),
              inc=f"{tag}_glh", waits=tuple(f"{tag}_glev{q}" for q in range(4)))
        em.op("vector", lambda e: e.tensor_tensor(
            B.rows.ap()[0:8, 0:2048], ev[0:8, 0:2048],
            B.sbf.ap()[0:8, 0:2048], ALU.subtract),
            inc=f"{tag}_gll", waits=(f"{tag}_glh",))
        for t in range(NT):
            bnk = 4 + t % 2
            em.op("tensor",
                  lambda e, t=t, bnk=bnk: e.matmul(
                      B.bank(bnk, c0=0, w=8), B.sbf.ap()[0:8, t * 128:(t + 1) * 128],
                      B.id_bf.ap()[0:8, 0:8], start=True, stop=False),
                  waits=(f"{tag}_gll",) + ((f"{tag}_gw{t-2}",) if t >= 2 else ())),
            em.op("tensor",
                  lambda e, t=t, bnk=bnk: e.matmul(
                      B.bank(bnk, c0=0, w=8), B.rows.ap()[0:8, t * 128:(t + 1) * 128],
                      B.id_bf.ap()[0:8, 0:8], start=False, stop=True),
                  inc=f"{tag}_gtr{t}")
            co = 1024 + t * 8
            em.op("vector",
                  lambda e, t=t, bnk=bnk: e.tensor_reduce(
                      ev[:, 1990:1991], B.bank(bnk, c0=0, w=8), AX.X, ALU.max,
                      negate=True),
                  inc=f"{tag}_gmx{t}",
                  waits=(f"{tag}_gtr{t}",) + ((f"{tag}_gex{t-1}",) if t else ()))
            em.op("scalar",
                  lambda e, t=t, bnk=bnk: e.activation(
                      s32[:, 960:968], B.bank(bnk, c0=0, w=8), AF.Exp,
                      bias=ev[:, 1990:1991], scale=1.0),
                  inc=f"{tag}_gex{t}",
                  waits=(f"{tag}_gmx{t}",) + ((f"{tag}_gpr{t-1}",) if t else ()))
            em.op("vector",
                  lambda e: e.tensor_reduce(ev[:, 1991:1992], s32[:, 960:968],
                                            AX.X, ALU.add),
                  inc=f"{tag}_gsm{t}", waits=(f"{tag}_gex{t}",))
            em.op("vector",
                  lambda e: e.reciprocal(ev[:, 1992:1993], ev[:, 1991:1992]),
                  inc=f"{tag}_grc{t}", waits=(f"{tag}_gsm{t}",))
            em.op("vector",
                  lambda e: e.tensor_scalar(s32[:, 968:976], s32[:, 960:968],
                                            ev[:, 1992:1993], None, ALU.mult),
                  inc=f"{tag}_gpr{t}", waits=(f"{tag}_grc{t}",))
            em.op("vector",
                  lambda e: e.tensor_scalar(s32[:, 976:984], s32[:, 968:976],
                                            0.1, None, ALU.is_ge),
                  inc=f"{tag}_gsel{t}", waits=(f"{tag}_gpr{t}",))
            em.op("vector",
                  lambda e: e.tensor_tensor(s32[:, 984:992], s32[:, 968:976],
                                            s32[:, 976:984], ALU.mult),
                  inc=f"{tag}_gw0{t}", waits=(f"{tag}_gsel{t}",))
            em.op("vector",
                  lambda e: e.tensor_reduce(ev[:, 1993:1994], s32[:, 984:992],
                                            AX.X, ALU.add),
                  inc=f"{tag}_gws{t}", waits=(f"{tag}_gw0{t}",))
            em.op("vector",
                  lambda e: e.tensor_scalar(ev[:, 1994:1995], ev[:, 1993:1994],
                                            0.0, ev[:, 1993:1994], ALU.is_equal,
                                            ALU.add),
                  inc=f"{tag}_grw{t}", waits=(f"{tag}_gws{t}",))
            em.op("vector",
                  lambda e: e.reciprocal(ev[:, 1995:1996], ev[:, 1994:1995]),
                  inc=f"{tag}_grr{t}", waits=(f"{tag}_grw{t}",))
            em.op("vector",
                  lambda e, t=t: e.tensor_scalar(s32[:, t * 8:(t + 1) * 8],
                                                 s32[:, 984:992],
                                                 ev[:, 1995:1996], None,
                                                 ALU.mult),
                  inc=f"{tag}_gw{t}", waits=(f"{tag}_grr{t}",))
            em.op("vector",
                  lambda e, t=t: e.tensor_copy(B.pl("wfb", t * 8, 8),
                                               s32[:, t * 8:(t + 1) * 8]),
                  inc=f"{tag}_gwb{t}", waits=(f"{tag}_gw{t}",))
        for t in range(NT):
            bnk = 6 + t % 2
            em.op("tensor",
                  lambda e, t=t, bnk=bnk: e.matmul(
                      B.bank(bnk, p1=8, c0=0, w=128),
                      B.pl("wfb", t * 8, 8),
                      B.id_bf.ap()[:], start=True, stop=True),
                  inc=f"{tag}_wtm{t}",
                  waits=(f"{tag}_gwb{t}",) + ((f"{tag}_wte{t-2}",) if t >= 2
                                              else ())),
            em.op("vector",
                  lambda e, t=t, bnk=bnk: e.tensor_copy(
                      B.pl("wt8", t * 128, 128, p0=0, p1=8),
                      B.bank(bnk, p0=0, p1=8, c0=0, w=128)),
                  inc=f"{tag}_wte{t}", waits=(f"{tag}_wtm{t}",))
        em.op("sync", lambda e: e.dma_start(
            ev[0:8, 0:1024], Wd["L0_ffn_expert_b"].ap()),
            inc=f"{tag}_bein", dma=True, grp=f"{tag}_bein",
            waits=(f"{tag}_gll",))
        em.op("scalar", lambda e: e.copy(B.pl("be8", 0, 1024, p0=0, p1=8),
                                         ev[0:8, 0:1024]),
              inc=f"{tag}_beh", waits=(f"{tag}_bein", f"{tag}_gtr{NT-1}")),
        em.op("vector", lambda e: e.tensor_tensor(
            B.pl("be8", 1024, 1024, p0=0, p1=8), ev[0:8, 0:1024],
            B.pl("be8", 0, 1024, p0=0, p1=8), ALU.subtract),
            inc=f"{tag}_bel", waits=(f"{tag}_beh",))
    em.run_block(ph_gate)

    def ph_exp(em):
        for e_ in range(NE):
            src = Wd["L0_ffn_expert_w"].ap()[e_].rearrange(
                "(c p) n -> p c n", p=128)
            ws = []
            if e_ >= 2:
                ws.append(f"{tag}_em{e_-2}")
            em.op("gpsimd",
                  lambda e, e_=e_, src=src: e.dma_start(
                      B.pl("ew", (e_ % 2) * 8192, 8192).rearrange(
                          "p (c n) -> p c n", c=8), src),
                  inc=f"{tag}_ewin{e_}", waits=tuple(ws), dma=True,
                  grp=f"ew{e_ % 2}")
        em.op("vector", lambda e: e.drain() or 1)
        for e_ in range(NE):
            for i in range(32):
                tt, nh = i // 2, i % 2
                bnk = i % 4
                last_mm = (i == 31)
                ws = [f"{tag}_ewin{e_}"]
                if e_ > 0 or i >= 4:
                    pj = (e_ * 32 + i) - 4
                    ws.append(f"{tag}_stt{pj}")
                for c in range(8):
                    em.op("tensor",
                          lambda e, e_=e_, tt=tt, nh=nh, c=c, bnk=bnk: e.matmul(
                              B.bank(bnk),
                              B.pl("xnt", c * 2048 + tt * 128, 128),
                              B.pl("ew", (e_ % 2) * 8192 + c * 1024 + nh * 512,
                                   512),
                              start=(c == 0), stop=(c == 7)),
                          inc=(f"{tag}_em{e_}" if last_mm and c == 7 else
                               (f"{tag}_pm{e_*32+i}" if c == 7 else None)),
                          waits=tuple(ws) if c == 0 else ())
                mmark = f"{tag}_em{e_}" if last_mm else f"{tag}_pm{e_*32+i}"
                xsl = B.xres.ap()[:, tt * DM + nh * 512: tt * DM + nh * 512 + 512]
                em.op("vector",
                      lambda e, xsl=xsl, bnk=bnk, tt=tt, e_=e_: e.scalar_tensor_tensor(
                          xsl, B.bank(bnk), s32[:, tt * 8 + e_:tt * 8 + e_ + 1],
                          xsl, ALU.mult, ALU.add),
                      inc=f"{tag}_stt{e_*32+i}", waits=(mmark,))
        for i in range(32):
            tt, nh = i // 2, i % 2
            bnk = 4 + i % 2
            ws = [f"{tag}_stt{NE*32-1}"]
            if i >= 2:
                ws.append(f"{tag}_bad{i-2}")
            em.op("tensor",
                  lambda e, tt=tt, nh=nh, bnk=bnk: e.matmul(
                      B.bank(bnk), B.pl("wt8", tt * 128, 128, p0=0, p1=8),
                      B.pl("be8", nh * 512, 512, p0=0, p1=8),
                      start=True, stop=False),
                  waits=tuple(ws)),
            em.op("tensor",
                  lambda e, tt=tt, nh=nh, bnk=bnk: e.matmul(
                      B.bank(bnk), B.pl("wt8", tt * 128, 128, p0=0, p1=8),
                      B.pl("be8", 1024 + nh * 512, 512, p0=0, p1=8),
                      start=False, stop=True),
                  inc=f"{tag}_bm{i}"),
            xsl = B.xres.ap()[:, tt * DM + nh * 512: tt * DM + nh * 512 + 512]
            em.op("vector",
                  lambda e, xsl=xsl, bnk=bnk: e.tensor_tensor(
                      xsl, xsl, B.bank(bnk), ALU.add),
                  inc=f"{tag}_bad{i}", waits=(f"{tag}_bm{i}",))
    em.run_block(ph_exp)


LAY_L1A = _lay([
    ("ut", 8 * 2048),
    ("kt", 4 * 2048),
    ("v", NT * 8 * 65),
    ("ctxt", 8 * 512),
    ("attn", 8 * 512),
    ("wstr", 2 * 1024),
    ("wnat", 2 * 1024),
    ("wqt", 4 * 1024),
    ("wot", 8 * 1024),
])
LAY_L1A["qt"] = (LAY_L1A["wstr"][0], 4 * 512)


def emit_l1_attn(em, B, tag, Wd, xin_mark):
    ev = B.ev.ap()
    st = B.stats.ap()

    def ph_ln(em):
        em.gate(("aux_id", "aux_ones", "aux_caus", "aux_tril",
                 "aux_g", "aux_e"))
        emit_ln_stats(em, B, tag, rstd2=True, waits=(xin_mark, "k_consts"))
        em.op("sync", lambda e: e.dma_start(
            B.gbfm.ap()[:, 0:8],
            Wd["L1_attn_ln_gamma"].ap().rearrange("(c p) -> p c", p=128),
            allow_slow_non_contiguous=True),
            inc=f"{tag}_gam", dma=True)
        em.op("sync", lambda e: e.dma_start(
            B.gbfm.ap()[:, 8:16],
            Wd["L1_attn_ln_beta"].ap().rearrange("(c p) -> p c", p=128),
            allow_slow_non_contiguous=True),
            inc=f"{tag}_bet", dma=True)
        em.op("sync", lambda e: e.dma_start(
            B.bqkv.ap()[:, 0:24],
            Wd["L1_attn_b_qkv"].ap().rearrange("(c p) -> p c", p=128),
            allow_slow_non_contiguous=True),
            inc=f"{tag}_bq", dma=True)
        em.op("sync", lambda e: e.dma_start(
            ev[0:1, 0:1024], Wd["L1_attn_b_out"].ap().unsqueeze(0)),
            inc=f"{tag}_bo", dma=True)
        em.op("vector", lambda e: e.tensor_scalar(
            B.bqkv.ap()[:, 24:32], B.bqkv.ap()[:, 0:8], 0.125, None, ALU.mult),
            inc=f"{tag}_bqs", waits=(f"{tag}_bq",))
        em.op("scalar", lambda e: e.copy(B.rows.ap()[0:1, 0:1024],
                                         ev[0:1, 0:1024]),
              inc=f"{tag}_boh", waits=(f"{tag}_bo",))
        em.op("vector", lambda e: e.tensor_tensor(
            B.rows.ap()[0:1, 1024:2048], ev[0:1, 0:1024],
            B.rows.ap()[0:1, 0:1024], ALU.subtract),
            inc=f"{tag}_bol", waits=(f"{tag}_boh",))
        em.op("sync", lambda e: e.dma_start(
            ev[32:33, 0:1024], Wd["L1_attn_b_qkv"].ap()[2048:3072].unsqueeze(0)),
            inc=f"{tag}_bv", dma=True)
        em.op("scalar", lambda e: e.copy(B.sbf.ap()[32:33, 0:1024],
                                         ev[32:33, 0:1024]),
              inc=f"{tag}_bvh", waits=(f"{tag}_bv",))
        em.op("vector", lambda e: e.tensor_tensor(
            B.sbf.ap()[32:33, 1024:2048], ev[32:33, 0:1024],
            B.sbf.ap()[32:33, 0:1024], ALU.subtract),
            inc=f"{tag}_bvl", waits=(f"{tag}_bvh",))
        em.op("vector", lambda e: e.tensor_copy(B.rows.ap()[32:33, 0:2048],
                                                B.sbf.ap()[32:33, 0:2048]),
              inc=f"{tag}_bvr", waits=(f"{tag}_bvl",))
    em.run_block(ph_ln)

    def ph_ut(em):
        emit_xnt_tiles(em, B, f"{tag}u", list(range(NT)), "ut", 2048,
                       rstd_col=64, gb=True,
                       waits=(f"{tag}_rstd2", f"{tag}_gam", f"{tag}_bet"))
        em.op("vector", lambda e: e.tensor_scalar(
            st[:, 80:96], st[:, 48:64], 1.0, None, ALU.add),
            inc=f"{tag}_hs1", waits=(f"{tag}_rstd2",))
        em.op("vector", lambda e: e.tensor_tensor(
            st[:, 96:112], st[:, 32:48], st[:, 48:64], ALU.mult),
            inc=f"{tag}_hs2", waits=(f"{tag}_hs1",))
        for t in range(NT):
            x_t = B.xres.ap()[:, t * DM:(t + 1) * DM]
            em.op("vector",
                  lambda e, x_t=x_t, t=t: e.tensor_scalar(
                      x_t, x_t, fm_col(st, 80 + t), fm_col(st, 96 + t),
                      ALU.mult, ALU.subtract),
                  inc=f"{tag}_hf{t}",
                  waits=(f"{tag}_hs2", f"{tag}u_xn{t}"))
    em.run_block(ph_ut)

    if DEBUG_DUMP and tag.startswith("s0"):
        dbg_xu = em.nc.dram_tensor("dbg_xu", [128, 16384], F32,
                                   kind="ExternalOutput")
        em.run_block(lambda em: em.op(
            "gpsimd", lambda e: e.dma_start(dbg_xu.ap(), B.xres.ap()),
            inc="dbgxu", dma=True, grp="dbg"))

    for half in range(2):
        tagH = f"{tag}h{half}"

        def ph_qkv(em, half=half, tagH=tagH):
            seqs = [("q", half * 4 + j) for j in range(4)] + \
                   [("k", 8 + half * 4 + j) for j in range(4)] + \
                   [("v", 16 + half * 4 + j) for j in range(4)] + \
                   ([("o", j) for j in range(8)] if half == 0 else [])
            for si, (kind, oc) in enumerate(seqs):
                par = si % 2
                wsrc = Wd["L1_attn_W_qkv"] if kind != "o" else Wd["L1_attn_W_out"]
                src = wsrc.ap()[oc * 128:(oc + 1) * 128, :]
                wsl = []
                if si >= 2:
                    wsl.append(f"{tagH}_wtr{si-2}")
                em.op("gpsimd",
                      lambda e, src=src, par=par: e.dma_start(
                          B.pl("wnat", par * 1024, 1024), src),
                      inc=f"{tagH}_wn{si}", waits=tuple(wsl), dma=True,
                      grp=f"{tag}_wn{par}")
                for c in range(8):
                    bnk = 6 + c // 4
                    ws2 = []
                    if c == 0:
                        ws2 = [f"{tagH}_wn{si}"]
                        if si >= 1:
                            ws2.append(f"{tagH}_wev{si-1}")
                    em.op("tensor",
                          lambda e, c=c, bnk=bnk, par=par: e.matmul(
                              B.bank(bnk, c0=(c % 4) * 128, w=128),
                              B.pl("wnat", par * 1024 + c * 128, 128),
                              B.id_bf.ap()[:], start=True, stop=True),
                          inc=(f"{tagH}_wtr{si}" if c == 7 else None),
                          waits=tuple(ws2))
                if kind == "q":
                    dst = B.pl("wqt", (oc - half * 4) * 1024, 1024)
                elif kind == "o":
                    dst = None
                else:
                    dst = B.pl("wstr", par * 1024, 1024)
                if kind != "o":
                    em.op("vector",
                          lambda e, dst=dst: e.tensor_copy(
                              dst.rearrange("p (c n) -> p c n", c=8),
                              B.bank(6, w=1024).rearrange("p (c n) -> p c n",
                                                          c=8)),
                          inc=f"{tagH}_wev{si}", waits=(f"{tagH}_wtr{si}",))
                else:
                    d3 = B.pl("wot").rearrange("p (c n) -> p c n", c=8)[
                        :, :, oc * 128:(oc + 1) * 128]
                    em.op("vector",
                          lambda e, d3=d3: e.tensor_copy(
                              d3, B.bank(6, w=1024).rearrange(
                                  "p (c n) -> p c n", c=8)),
                          inc=f"{tagH}_wev{si}", waits=(f"{tagH}_wtr{si}",))
                if kind == "k":
                    gp = oc - 8
                    lp = gp - half * 4
                    for qblk in range(4):
                        bnk = qblk % 2
                        for c in range(8):
                            ws3 = []
                            if c == 0:
                                ws3 = [f"{tagH}_wev{si}"]
                                if qblk >= 2:
                                    ws3.append(f"{tagH}_ktev{lp}_{qblk-2}")
                                elif lp > 0 or half > 0:
                                    ws3.append(f"{tagH}_ktev{lp-1}_{2+qblk}"
                                               if lp > 0 else
                                               f"{tag}h{half-1}_ktev3_{2+qblk}")
                            em.op("tensor",
                                  lambda e, c=c, qblk=qblk, bnk=bnk, par=par:
                                  e.matmul(
                                      B.bank(bnk),
                                      B.pl("wstr", par * 1024 + c * 128, 128),
                                      B.pl("ut", c * 2048 + qblk * 512, 512),
                                      start=(c == 0), stop=(c == 7)),
                                  inc=(f"{tagH}_ktm{lp}_{qblk}" if c == 7
                                       else None),
                                  waits=tuple(ws3))
                        em.op("vector",
                              lambda e, lp=lp, qblk=qblk, bnk=bnk, gp=gp:
                              e.tensor_scalar(
                                  B.pl("kt", lp * 2048 + qblk * 512, 512),
                                  B.bank(bnk), fm_col(B.bqkv.ap(), 8 + gp),
                                  None, ALU.add),
                              inc=f"{tagH}_ktev{lp}_{qblk}",
                              waits=(f"{tagH}_ktm{lp}_{qblk}",))
                elif kind == "v":
                    ocr = oc - 16 - half * 4
                    for t in range(NT):
                        bnk = 2 + t % 2
                        i = ocr * NT + t
                        for c in range(8):
                            ws4 = []
                            if c == 0:
                                ws4 = [f"{tagH}_wev{si}", f"{tag}_bvr"]
                                if i >= 2:
                                    ws4.append(f"{tagH}_vev{i-2}")
                                elif half > 0:
                                    ws4.append(f"{tag}h{half-1}_vev{4*NT-2+i}")
                            em.op("tensor",
                                  lambda e, c=c, t=t, bnk=bnk, par=par:
                                  e.matmul(
                                      B.bank(bnk, w=128),
                                      B.pl("ut", c * 2048 + t * 128, 128),
                                      B.pl("wstr", par * 1024 + c * 128, 128),
                                      start=(c == 0), stop=False),
                                  waits=tuple(ws4))
                        em.op("tensor",
                              lambda e, bnk=bnk, ocr=ocr: e.matmul(
                                  B.bank(bnk, w=128),
                                  B.ones_bf.ap()[32:33, 0:128],
                                  B.rows.ap()[32:33, (half * 4 + ocr) * 128:
                                              (half * 4 + ocr) * 128 + 128],
                                  start=False, stop=False))
                        em.op("tensor",
                              lambda e, bnk=bnk, ocr=ocr: e.matmul(
                                  B.bank(bnk, w=128),
                                  B.ones_bf.ap()[32:33, 0:128],
                                  B.rows.ap()[32:33, 1024 + (half * 4 + ocr) * 128:
                                              1024 + (half * 4 + ocr) * 128 + 128],
                                  start=False, stop=True),
                              inc=f"{tagH}_vm{i}")
                        em.op("vector",
                              lambda e, t=t, ocr=ocr, bnk=bnk: e.tensor_copy(
                                  B.pl("v", (t * 8 + ocr * 2) * 65,
                                       2 * 65).rearrange(
                                      "p (h n) -> p h n", h=2)[:, :, 0:64],
                                  B.bank(bnk, w=128).rearrange(
                                      "p (h n) -> p h n", h=2)),
                              inc=f"{tagH}_vev{i}", waits=(f"{tagH}_vm{i}",))
            em.op("vector",
                  lambda e: e.memset(B.pl("v").rearrange(
                      "p (g n) -> p g n", n=65)[:, :, 64:65], 1.0),
                  inc=f"{tagH}_vones", waits=(f"{tagH}_vev{4*NT-1}",))
            for lp in range(4):
                em.op("vector",
                      lambda e, lp=lp: e.tensor_tensor(
                          B.sbf.ap()[:, 0:2048], B.pl("kt", lp * 2048, 2048),
                          B.pl("kt", lp * 2048, 2048), ALU.mult),
                      inc=f"{tagH}_ksq{lp}",
                      waits=(f"{tagH}_ktev{lp}_3", f"{tagH}_kred{lp-1}") if lp
                      else (f"{tagH}_ktev{lp}_3",))
                for qblk in range(4):
                    em.op("tensor",
                          lambda e, qblk=qblk: e.matmul(
                              B.bank(4 + qblk, p1=1), B.ones_bf.ap()[:, 0:1],
                              B.sbf.ap()[:, qblk * 512:(qblk + 1) * 512],
                              start=True, stop=True),
                          inc=(f"{tagH}_knm{lp}" if qblk == 3 else None),
                          waits=(f"{tagH}_ksq{lp}",) if qblk == 0 else ())
                em.op("vector",
                      lambda e, lp=lp: e.tensor_reduce(
                          ev[0:1, 1024 + lp:1025 + lp],
                          B.psum_t.ap()[0:1, 4 * 512:8 * 512], AX.X, ALU.max),
                      inc=f"{tagH}_kred{lp}", waits=(f"{tagH}_knm{lp}",))
            em.op("scalar", lambda e: e.activation(
                ev[0:1, 1032:1036], ev[0:1, 1024:1028], AF.Ln,
                bias=0.0, scale=1.0),
                inc=f"{tagH}_lnk", waits=(f"{tagH}_kred3",))
        em.run_block(ph_qkv)

        def qt_emit(em, p, ws, qb, hp, tagH=tagH):
            for c in range(8):
                em.op("tensor",
                      lambda e, c=c, hp=hp, qb=qb: e.matmul(
                          B.bank(hp % 4),
                          B.pl("wqt", hp * 1024 + c * 128, 128),
                          B.pl("ut", c * 2048 + qb * 512, 512),
                          start=(c == 0), stop=(c == 7)),
                      inc=(f"{p}_qtm" if c == 7 else None),
                      waits=ws if c == 0 else ())

        def ph_attn(em, half=half, tagH=tagH):
            emit_attn_core(
                em, B, tagH, causal=False, n_hp=4, ktmax_f=lambda qb: NT,
                kt_slice=lambda hp, kt, part: B.pl(
                    "kt", hp * 2048 + kt * 128, 128,
                    p0=0 if part == "A" else 64,
                    p1=64 if part == "A" else 128),
                v_slice=lambda hp, kt, part: B.pl(
                    "v", (kt * 8 + hp * 2 + (0 if part == "A" else 1)) * 65,
                    65),
                qt_emit=qt_emit,
                qt_bias=lambda hp: fm_col(B.bqkv.ap(), half * 4 + hp),
                wo_slice=lambda ci, oh: B.pl(
                    "wot", (half * 4 + ci) * 1024 + oh * 512, 512),
                n_wo_chunks=4,
                bias_rows=(lambda oh: [B.rows.ap()[0:1, oh * 512:
                                                   oh * 512 + 512],
                                       B.rows.ap()[0:1, 1024 + oh * 512:
                                                   1024 + oh * 512 + 512]])
                if half == 0 else (lambda oh: []),
                wo_marks=(f"{tag}h0_wev{19}", f"{tag}_bol"),
                xadd_acc=True)
        em.run_block(ph_attn)

        if DEBUG_DUMP and tag.startswith("s0"):
            dbg_xh = em.nc.dram_tensor(f"dbg_xh{half}", [128, 16384], F32,
                                       kind="ExternalOutput")
            em.run_block(lambda em, dbg_xh=dbg_xh, half=half: em.op(
                "gpsimd", lambda e: e.dma_start(dbg_xh.ap(), B.xres.ap()),
                inc=f"dbgxh{half}", dma=True, grp="dbg"))


LAY_L1F = _lay([
    ("wq", 8 * 1024), ("wk", 8 * 1024), ("wv", 8 * 1024), ("wo", 8 * 1024),
    ("ws1", 2 * 2048),
    ("gt", 2 * 1024),
    ("vg", 16 * HD),
    ("ctxtg", 8 * 128),
    ("attng", 4 * 128),
    ("o128", 2 * 1024),
    ("ktg", 8 * 128),
    ("qtg", 8 * 128),
])


def emit_l1_ffn(em, B, tag, Wd, xin_mark):
    ev = B.ev.ap()
    st = B.stats.ap()
    s32 = B.s32.ap()

    def ph_ln(em):
        em.gate(("aux_id", "aux_ones", "aux_caus", "aux_tril",
                 "aux_g", "aux_e"))
        emit_ln_stats(em, B, tag, waits=(xin_mark, "k_consts"))
        for nm, slot in (("L1_ffn_Wq", "wq"), ("L1_ffn_Wk", "wk"),
                         ("L1_ffn_Wv", "wv"), ("L1_ffn_Wo", "wo")):
            src = Wd[nm].ap().rearrange("(c p) n -> p c n", p=128)
            em.op("gpsimd", lambda e, src=src, slot=slot: e.dma_start(
                B.pl(slot).rearrange("p (c n) -> p c n", c=8), src),
                inc=f"{tag}_{slot}in", dma=True)
    em.run_block(ph_ln)

    def ph_groups(em):
        for t in range(NT):
            x_t = B.xres.ap()[:, t * DM:(t + 1) * DM]
            wx = [f"{tag}_rstd"]
            if t:
                wx.append(f"{tag}_gm{t-1}")
            em.op("vector",
                  lambda e, x_t=x_t, t=t: e.tensor_scalar(
                      s32[:, 0:1024], x_t, fm_col(st, 32 + t),
                      fm_col(st, 48 + t), ALU.subtract, ALU.mult),
                  inc=f"{tag}_x32{t}", waits=tuple(wx))
            em.op("vector", lambda e: e.tensor_copy(B.sbf.ap()[:, 0:1024],
                                                    s32[:, 0:1024]),
                  inc=f"{tag}_xh{t}", waits=(f"{tag}_x32{t}",))
            em.op("vector",
                  lambda e: e.tensor_tensor(B.sbf.ap()[:, 1024:2048],
                                            s32[:, 0:1024],
                                            B.sbf.ap()[:, 0:1024],
                                            ALU.subtract),
                  inc=f"{tag}_xl{t}", waits=(f"{tag}_xh{t}",))
            gsl = B.auxg.ap()[:, 120 - 8 * t:248 - 8 * t]
            for dh in range(2):
                for so in (0, 1024):
                    em.op("tensor",
                          lambda e, gsl=gsl, dh=dh, so=so, t=t: e.matmul(
                              B.bank(dh), gsl,
                              B.sbf.ap()[:, so + dh * 512:so + dh * 512 + 512],
                              start=(t == 0 and so == 0),
                              stop=(t == NT - 1 and so == 1024)),
                          inc=(f"{tag}_gm{t}" if dh == 1 and so == 1024
                               else None),
                          waits=(f"{tag}_xl{t}",) if dh == 0 and so == 0
                          else ())
        em.op("vector", lambda e: e.tensor_copy(ev[:, 0:512], B.bank(0)),
              inc=f"{tag}_gev0", waits=(f"{tag}_gm{NT-1}",))
        em.op("vector", lambda e: e.tensor_copy(ev[:, 512:1024], B.bank(1)),
              inc=f"{tag}_gev1")
        em.op("scalar", lambda e: e.copy(B.sbf.ap()[:, 0:1024],
                                         ev[:, 0:1024]),
              inc=f"{tag}_gh", waits=(f"{tag}_gev0", f"{tag}_gev1",
                                      f"{tag}_xl{NT-1}"))
        em.op("vector", lambda e: e.tensor_tensor(
            B.sbf.ap()[:, 1024:2048], ev[:, 0:1024], B.sbf.ap()[:, 0:1024],
            ALU.subtract),
            inc=f"{tag}_gl", waits=(f"{tag}_gh",))
        for i in range(16):
            hl, c = i // 8, i % 8
            bnk = 2 + i % 2
            em.op("tensor",
                  lambda e, hl=hl, c=c, bnk=bnk: e.matmul(
                      B.bank(bnk, w=128),
                      B.sbf.ap()[:, hl * 1024 + c * 128:hl * 1024 + (c + 1) * 128],
                      B.id_bf.ap()[:], start=True, stop=True),
                  inc=f"{tag}_gtr{i}",
                  waits=((f"{tag}_gl",) if i == 0 else ()) +
                  ((f"{tag}_gte{i-2}",) if i >= 2 else ()))
            em.op("vector",
                  lambda e, hl=hl, c=c, bnk=bnk: e.tensor_copy(
                      B.pl("gt", hl * 1024 + c * 128, 128),
                      B.bank(bnk, w=128)),
                  inc=f"{tag}_gte{i}", waits=(f"{tag}_gtr{i}",))
    em.run_block(ph_groups)

    def ph_route(em):
        for sg in range(2):
            src = Wd["L1_ffn_Ws1"].ap().rearrange(
                "(c p) n -> p c n", p=128)[:, sg * 4:(sg + 1) * 4]
            em.op("sync", lambda e, src=src: e.dma_start(
                s32.rearrange("p (c n) -> p c n", c=4), src),
                inc=f"{tag}_w1in{sg}", dma=True,
                waits=(f"{tag}_w1l{sg-1}",) if sg else ())
            em.op("scalar",
                  lambda e, sg=sg: e.copy(B.pl("ws1", sg * 1024, 1024),
                                          s32[:, 0:1024]),
                  inc=f"{tag}_w1h{sg}", waits=(f"{tag}_w1in{sg}",))
            em.op("vector",
                  lambda e, sg=sg: e.tensor_tensor(
                      B.pl("ws1", 2048 + sg * 1024, 1024), s32[:, 0:1024],
                      B.pl("ws1", sg * 1024, 1024), ALU.subtract),
                  inc=f"{tag}_w1l{sg}", waits=(f"{tag}_w1h{sg}",))
        em.op("sync", lambda e: e.dma_start(
            ev[0:1, 1100:1356], Wd["L1_ffn_bs1"].ap().unsqueeze(0)),
            inc=f"{tag}_b1in", dma=True)
        em.op("scalar", lambda e: e.copy(B.rows.ap()[0:1, 0:256],
                                         ev[0:1, 1100:1356]),
              inc=f"{tag}_b1h", waits=(f"{tag}_b1in",))
        em.op("vector", lambda e: e.tensor_tensor(
            B.rows.ap()[0:1, 256:512], ev[0:1, 1100:1356],
            B.rows.ap()[0:1, 0:256], ALU.subtract),
            inc=f"{tag}_b1l", waits=(f"{tag}_b1h",))
        em.op("sync", lambda e: e.dma_start(
            ev[:, 1400:1402], Wd["L1_ffn_Ws2"].ap().rearrange(
                "(c p) n -> p c n", p=128),
            allow_slow_non_contiguous=True),
            inc=f"{tag}_w2in", dma=True)
        em.op("sync", lambda e: e.dma_start(
            ev[0:1, 1402:1403], Wd["L1_ffn_bs2"].ap().unsqueeze(0)),
            inc=f"{tag}_b2in", dma=True)
        em.op("scalar", lambda e: e.copy(B.sbf.ap()[:, 1024:1026],
                                         ev[:, 1400:1402]),
              inc=f"{tag}_w2h", waits=(f"{tag}_w2in",))
        em.op("vector", lambda e: e.tensor_tensor(
            B.sbf.ap()[:, 1026:1028], ev[:, 1400:1402],
            B.sbf.ap()[:, 1024:1026], ALU.subtract),
            inc=f"{tag}_w2l", waits=(f"{tag}_w2h",))
        em.op("scalar", lambda e: e.copy(B.sbf.ap()[0:1, 1028:1029],
                                         ev[0:1, 1402:1403]),
              inc=f"{tag}_b2h", waits=(f"{tag}_b2in",))
        em.op("vector", lambda e: e.tensor_tensor(
            B.sbf.ap()[0:1, 1029:1030], ev[0:1, 1402:1403],
            B.sbf.ap()[0:1, 1028:1029], ALU.subtract),
            inc=f"{tag}_b2l", waits=(f"{tag}_b2h",))
        n = 0
        for gof, wof in ((0, 0), (0, 2048), (1024, 0)):
            for c in range(8):
                em.op("tensor",
                      lambda e, gof=gof, wof=wof, c=c, n=n: e.matmul(
                          B.bank(4, w=256),
                          B.pl("gt", gof + c * 128, 128),
                          B.pl("ws1", wof + c * 256, 256),
                          start=(n == 0), stop=False),
                      waits=(f"{tag}_w1l1", f"{tag}_b1l") if n == 0 else ())
                n += 1
        em.op("tensor", lambda e: e.matmul(
            B.bank(4, w=256), B.ones_bf.ap()[0:1, 0:128],
            B.rows.ap()[0:1, 0:256], start=False, stop=False))
        em.op("tensor", lambda e: e.matmul(
            B.bank(4, w=256), B.ones_bf.ap()[0:1, 0:128],
            B.rows.ap()[0:1, 256:512], start=False, stop=True),
            inc=f"{tag}_hm")
        em.op("scalar", lambda e: e.activation(s32[:, 0:256], B.bank(4, w=256),
                                               AF.Relu, bias=0.0, scale=1.0),
              inc=f"{tag}_hrel", waits=(f"{tag}_hm",))
        em.op("vector", lambda e: e.tensor_copy(B.sbf.ap()[:, 0:256],
                                                s32[:, 0:256]),
              inc=f"{tag}_hh", waits=(f"{tag}_hrel",))
        em.op("vector", lambda e: e.tensor_tensor(
            B.sbf.ap()[:, 256:512], s32[:, 0:256], B.sbf.ap()[:, 0:256],
            ALU.subtract),
            inc=f"{tag}_hl", waits=(f"{tag}_hh",))
        for i in range(4):
            hl, c = i // 2, i % 2
            bnk = 5
            em.op("tensor",
                  lambda e, hl=hl, c=c: e.matmul(
                      B.bank(5, c0=i * 0, w=128) if False else
                      B.bank(5, c0=(i % 4) * 128, w=128),
                      B.sbf.ap()[:, hl * 256 + c * 128:hl * 256 + (c + 1) * 128],
                      B.id_bf.ap()[:], start=True, stop=True),
                  inc=(f"{tag}_htr" if i == 3 else None),
                  waits=(f"{tag}_hl",) if i == 0 else ())
        em.op("vector", lambda e: e.tensor_copy(B.sbf.ap()[:, 512:1024],
                                                B.bank(5, w=512)),
              inc=f"{tag}_hte", waits=(f"{tag}_htr",))
        n = 0
        for hof, wof in ((512, 1024), (512, 1026), (768, 1024)):
            for c in range(2):
                em.op("tensor",
                      lambda e, hof=hof, wof=wof, c=c, n=n: e.matmul(
                          B.bank(6, w=1),
                          B.sbf.ap()[:, hof + c * 128:hof + (c + 1) * 128],
                          B.sbf.ap()[c * 128:(c + 1) * 128, wof + c:wof + c + 1]
                          if False else
                          B.sbf.ap()[:, wof + c:wof + c + 1],
                          start=(n == 0), stop=False),
                      waits=(f"{tag}_hte", f"{tag}_w2l", f"{tag}_b2l")
                      if n == 0 else ())
                n += 1
        em.op("tensor", lambda e: e.matmul(
            B.bank(6, w=1), B.ones_bf.ap()[0:1, 0:128],
            B.sbf.ap()[0:1, 1028:1029], start=False, stop=False))
        em.op("tensor", lambda e: e.matmul(
            B.bank(6, w=1), B.ones_bf.ap()[0:1, 0:128],
            B.sbf.ap()[0:1, 1029:1030], start=False, stop=True),
            inc=f"{tag}_sm")
        em.op("vector", lambda e: e.tensor_copy(B.sbf.ap()[:, 1030:1031],
                                                B.bank(6, w=1)),
              inc=f"{tag}_sh", waits=(f"{tag}_sm",))
        em.op("vector", lambda e: e.drain() or 1)
        em.op("vector", lambda e: e.scalar_tensor_tensor(
            B.sbf.ap()[:, 1031:1032], B.bank(6, w=1), 1.0,
            B.sbf.ap()[:, 1030:1031], ALU.mult, ALU.subtract),
            inc=f"{tag}_sl", waits=(f"{tag}_sh",))
        for i, col in enumerate((1030, 1031)):
            em.op("tensor",
                  lambda e, col=col, i=i: e.matmul(
                      B.bank(7, p1=1, w=128), B.sbf.ap()[:, col:col + 1],
                      B.id_bf.ap()[:], start=(i == 0), stop=(i == 1)),
                  inc=(f"{tag}_str" if i == 1 else None),
                  waits=(f"{tag}_sl",) if i == 0 else ())
        em.op("vector", lambda e: e.tensor_reduce(
            ev[0:1, 1403:1404], B.bank(7, p1=1, w=128), AX.X, ALU.max,
            negate=True),
            inc=f"{tag}_smx", waits=(f"{tag}_str",))
        em.op("scalar", lambda e: e.activation(
            ev[0:1, 1404:1532], B.bank(7, p1=1, w=128), AF.Exp,
            bias=ev[0:1, 1403:1404], scale=1.0),
            inc=f"{tag}_sex", waits=(f"{tag}_smx",))
        em.op("vector", lambda e: e.tensor_reduce(
            ev[0:1, 1532:1533], ev[0:1, 1404:1532], AX.X, ALU.add),
            inc=f"{tag}_ssum", waits=(f"{tag}_sex",))
        em.op("vector", lambda e: e.reciprocal(ev[0:1, 1533:1534],
                                               ev[0:1, 1532:1533]),
              inc=f"{tag}_src", waits=(f"{tag}_ssum",))
        em.op("vector", lambda e: e.tensor_scalar(
            ev[0:1, 1404:1532], ev[0:1, 1404:1532], ev[0:1, 1533:1534], None,
            ALU.mult),
            inc=f"{tag}_spr", waits=(f"{tag}_src",))
        em.op("vector", lambda e: e.tensor_copy(B.sbf.ap()[0:1, 1032:1160],
                                                ev[0:1, 1404:1532]),
              inc=f"{tag}_sph", waits=(f"{tag}_spr",))
        em.op("vector", lambda e: e.tensor_tensor(
            B.sbf.ap()[0:1, 1160:1288], ev[0:1, 1404:1532],
            B.sbf.ap()[0:1, 1032:1160], ALU.subtract),
            inc=f"{tag}_spl", waits=(f"{tag}_sph",))
        for i, co in enumerate((1032, 1160)):
            em.op("tensor",
                  lambda e, co=co, i=i: e.matmul(
                      B.bank(4, w=1), B.sbf.ap()[0:1, co:co + 128],
                      B.ones_bf.ap()[0:1, 0:1], start=(i == 0), stop=(i == 1)),
                  inc=(f"{tag}_pcm" if i == 1 else None),
                  waits=(f"{tag}_spl", f"{tag}_hrel") if i == 0 else ())
        for i, co in enumerate((1032, 1160)):
            em.op("tensor",
                  lambda e, co=co, i=i: e.matmul(
                      B.bank(5, w=128), B.ones_bf.ap()[0:1, 0:128],
                      B.sbf.ap()[0:1, co:co + 128], start=(i == 0),
                      stop=(i == 1)),
                  inc=(f"{tag}_prm" if i == 1 else None),
                  waits=(f"{tag}_hte",) if i == 0 else ())
        em.op("vector", lambda e: e.tensor_copy(ev[:, 1536:1537],
                                                B.bank(4, w=1)),
              inc=f"{tag}_pce", waits=(f"{tag}_pcm",))
        em.op("vector", lambda e: e.tensor_copy(ev[:, 1537:1665],
                                                B.bank(5, w=128)),
              inc=f"{tag}_pre", waits=(f"{tag}_prm",))
        em.op("vector", lambda e: e.tensor_scalar(
            s32[:, 0:128], ev[:, 1537:1665], ev[:, 1536:1537], None,
            ALU.is_gt, ALU.add, accum_out=ev[:, 1665:1666]),
            inc=f"{tag}_gcnt", waits=(f"{tag}_pce", f"{tag}_pre"))
        em.op("vector", lambda e: e.tensor_scalar(
            s32[:, 128:256], ev[:, 1537:1665], ev[:, 1536:1537], None,
            ALU.is_equal),
            inc=f"{tag}_aeq", waits=(f"{tag}_gcnt",))
        em.op("vector", lambda e: e.tensor_tensor(
            s32[:, 256:384], s32[:, 128:256], B.tril.ap()[:], ALU.mult),
            inc=f"{tag}_eqt", waits=(f"{tag}_aeq",))
        em.op("vector", lambda e: e.tensor_scalar(
            s32[:, 384:512], s32[:, 256:384], 0.0, None, ALU.add, ALU.add,
            accum_out=ev[:, 1666:1667]),
            inc=f"{tag}_eqlw", waits=(f"{tag}_eqt",))
        em.op("vector", lambda e: e.tensor_tensor(
            ev[:, 1667:1668], ev[:, 1665:1666], ev[:, 1666:1667], ALU.add),
            inc=f"{tag}_rank", waits=(f"{tag}_eqlw",))
        em.op("vector", lambda e: e.tensor_scalar(
            ev[:, 1668:1669], ev[:, 1667:1668], 64.0, None, ALU.is_lt),
            inc=f"{tag}_msel", waits=(f"{tag}_rank",))
        em.op("vector", lambda e: e.tensor_scalar(
            ev[:, 1669:1670], ev[:, 1668:1669], float(-NEG), float(NEG),
            ALU.mult, ALU.add),
            inc=f"{tag}_mbias", waits=(f"{tag}_msel",))
    em.run_block(ph_route)

    def ph_gattn(em):
        for pair in range(8):
            bnk = pair % 2
            for c in range(8):
                em.op("tensor",
                      lambda e, pair=pair, c=c, bnk=bnk: e.matmul(
                          B.bank(bnk, w=128),
                          B.pl("wq", c * 1024 + pair * 128, 128),
                          B.pl("gt", c * 128, 128),
                          start=(c == 0), stop=(c == 7)),
                      inc=(f"{tag}_qm{pair}" if c == 7 else None),
                      waits=(f"{tag}_wqin",) + ((f"{tag}_qe{pair-2}",)
                                                if pair >= 2 else ())
                      if c == 0 else ())
            em.op("scalar",
                  lambda e, pair=pair, bnk=bnk: e.activation(
                      B.pl("qtg", pair * 128, 128), B.bank(bnk, w=128),
                      AF.Copy, bias=0.0, scale=0.125),
                  inc=f"{tag}_qe{pair}", waits=(f"{tag}_qm{pair}",))
            em.op("tensor",
                  lambda e, pair=pair, c=0, bnk=bnk: None, waits=()) \
                if False else None
            bnk2 = 2 + pair % 2
            for c in range(8):
                em.op("tensor",
                      lambda e, pair=pair, c=c, bnk2=bnk2: e.matmul(
                          B.bank(bnk2, w=128),
                          B.pl("wk", c * 1024 + pair * 128, 128),
                          B.pl("gt", c * 128, 128),
                          start=(c == 0), stop=(c == 7)),
                      inc=(f"{tag}_km{pair}" if c == 7 else None),
                      waits=(f"{tag}_wkin",) + ((f"{tag}_ke{pair-2}",)
                                                if pair >= 2 else ())
                      if c == 0 else ())
            em.op("vector",
                  lambda e, pair=pair, bnk2=bnk2: e.tensor_copy(
                      B.pl("ktg", pair * 128, 128), B.bank(bnk2, w=128)),
                  inc=f"{tag}_ke{pair}", waits=(f"{tag}_km{pair}",))
        for vh in range(2):
            bnk = 4 + vh
            for c in range(8):
                em.op("tensor",
                      lambda e, vh=vh, c=c, bnk=bnk: e.matmul(
                          B.bank(bnk),
                          B.pl("gt", c * 128, 128),
                          B.pl("wv", c * 1024 + vh * 512, 512),
                          start=(c == 0), stop=(c == 7)),
                      inc=(f"{tag}_vm{vh}" if c == 7 else None),
                      waits=(f"{tag}_wvin",) if c == 0 else ())
            em.op("vector",
                  lambda e, vh=vh, bnk=bnk: e.tensor_copy(
                      B.pl("vg", vh * 512, 512), B.bank(bnk)),
                  inc=f"{tag}_vge{vh}", waits=(f"{tag}_vm{vh}",))
        for pair in range(8):
            p = f"{tag}g{pair}"
            prevp = f"{tag}g{pair-1}" if pair else None
            em.op("vector",
                  lambda e, pair=pair: e.tensor_tensor(
                      B.sbf.ap()[:, 0:128], B.pl("ktg", pair * 128, 128),
                      B.pl("ktg", pair * 128, 128), ALU.mult),
                  inc=f"{p}_ksq",
                  waits=(f"{tag}_ke{pair}",) + ((f"{prevp}_qred",)
                                                if prevp else ())),
            em.op("tensor",
                  lambda e: e.matmul(B.bank(6, p1=1, w=128),
                                     B.ones_bf.ap()[:, 0:1],
                                     B.sbf.ap()[:, 0:128],
                                     start=True, stop=True),
                  inc=f"{p}_knm",
                  waits=(f"{p}_ksq",) + ((f"{prevp}_cpA",) if prevp else ())),
            em.op("vector",
                  lambda e: e.tensor_reduce(ev[0:1, 1700:1701],
                                            B.bank(6, p1=1, w=128), AX.X,
                                            ALU.max),
                  inc=f"{p}_kred", waits=(f"{p}_knm",))
            em.op("vector",
                  lambda e, pair=pair: e.tensor_tensor(
                      B.sbf.ap()[:, 0:128], B.pl("qtg", pair * 128, 128),
                      B.pl("qtg", pair * 128, 128), ALU.mult),
                  inc=f"{p}_qsq", waits=(f"{p}_kred", f"{tag}_qe{pair}")),
            em.op("tensor",
                  lambda e: e.matmul(B.bank(6, p1=1, c0=128, w=128),
                                     B.ones_bf.ap()[:, 0:1],
                                     B.sbf.ap()[:, 0:128],
                                     start=True, stop=True),
                  inc=f"{p}_qnm", waits=(f"{p}_qsq",)),
            em.op("vector",
                  lambda e: e.tensor_reduce(ev[0:1, 1701:1702],
                                            B.bank(6, p1=1, c0=128, w=128),
                                            AX.X, ALU.max),
                  inc=f"{p}_qred", waits=(f"{p}_qnm",))
            em.op("scalar",
                  lambda e: e.activation(ev[0:1, 1703:1705],
                                         ev[0:1, 1700:1702], AF.Ln,
                                         bias=0.0, scale=1.0),
                  inc=f"{p}_lnp", waits=(f"{p}_qred",))
            em.op("vector",
                  lambda e: e.tensor_reduce(ev[0:1, 1705:1706],
                                            ev[0:1, 1703:1705], AX.X, ALU.add),
                  inc=f"{p}_lns", waits=(f"{p}_lnp",))
            em.op("scalar",
                  lambda e: e.activation(B.sbf.ap()[0:1, 1024:1025],
                                         ev[0:1, 1705:1706], AF.Exp,
                                         bias=0.0, scale=0.5),
                  inc=f"{p}_cexp", waits=(f"{p}_lns",))
            em.op("tensor",
                  lambda e: e.matmul(B.bank(6, c0=256, w=1),
                                     B.ones_bf.ap()[0:1, 0:128],
                                     B.sbf.ap()[0:1, 1024:1025],
                                     start=True, stop=True),
                  inc=f"{p}_cbc", waits=(f"{p}_cexp",)),
            em.op("vector",
                  lambda e: e.scalar_tensor_tensor(
                      ev[:, 1706:1707], B.bank(6, c0=256, w=1), -1.0,
                      ev[:, 1669:1670], ALU.mult, ALU.add),
                  inc=f"{p}_negc", waits=(f"{p}_cbc", f"{tag}_mbias")),
            par = pair % 2
            sA, sB = par, 2 + par
            wsc = [f"{p}_negc"]
            if prevp:
                wsc.append(f"{prevp}_exp")
            if pair < 2:
                wsc += [f"{tag}_qe{6+par}", f"{tag}_ke{6+par}"]
            em.op("tensor",
                  lambda e, pair=pair, sA=sA: e.matmul(
                      B.bank(sA, w=128),
                      B.pl("ktg", pair * 128, 128, p0=0, p1=64),
                      B.pl("qtg", pair * 128, 128, p0=0, p1=64),
                      start=True, stop=True, tile_position=(0, 0)),
                  waits=tuple(wsc)),
            em.op("tensor",
                  lambda e, pair=pair, sB=sB: e.matmul(
                      B.bank(sB, w=128),
                      B.pl("ktg", pair * 128, 128, p0=64, p1=128),
                      B.pl("qtg", pair * 128, 128, p0=64, p1=128),
                      start=True, stop=True, tile_position=(64, 0)),
                  inc=f"{p}_sc"),
            slA = B.pl("attng", par * 128, 128)
            slB = B.pl("attng", (2 + par) * 128, 128)
            em.op("scalar",
                  lambda e, sA=sA, slA=slA: e.activation(
                      slA, B.bank(sA, w=128), AF.Exp,
                      bias=ev[:, 1706:1707], scale=1.0),
                  waits=(f"{p}_sc",) + ((f"{prevp}_ctx",) if prevp else ())),
            em.op("scalar",
                  lambda e, sB=sB, slB=slB: e.activation(
                      slB, B.bank(sB, w=128), AF.Exp,
                      bias=ev[:, 1706:1707], scale=1.0),
                  inc=f"{p}_exp")
            wctx = [f"{p}_exp"]
            if prevp:
                wctx += [f"{prevp}_mlA", f"{prevp}_mlB"]
            else:
                wctx += [f"{tag}_vge0", f"{tag}_vge1"]
            em.op("tensor",
                  lambda e, pair=pair, slA=slA: e.matmul(
                      B.bank(4, p0=0, p1=64, w=128),
                      B.pl("vg", (pair * 2) * HD, HD), slA,
                      start=True, stop=True, tile_position=(0, 0)),
                  waits=tuple(wctx)),
            em.op("tensor",
                  lambda e, slA=slA: e.matmul(
                      B.bank(6, p0=64, p1=65, w=128),
                      B.ones_bf.ap()[:, 0:1], slA,
                      start=True, stop=True, tile_position=(0, 64))),
            em.op("tensor",
                  lambda e, pair=pair, slB=slB: e.matmul(
                      B.bank(5, p0=64, p1=128, w=128),
                      B.pl("vg", (pair * 2 + 1) * HD, HD), slB,
                      start=True, stop=True, tile_position=(0, 64))),
            em.op("tensor",
                  lambda e, slB=slB: e.matmul(
                      B.bank(7, p0=32, p1=33, w=128),
                      B.ones_bf.ap()[:, 0:1], slB,
                      start=True, stop=True, tile_position=(0, 32)),
                  inc=f"{p}_ctx")
            def _recip(e, dst, srcp):
                with em.nc.allow_low_precision(reason="bf16 recipZ"):
                    return e.reciprocal(dst, srcp)
            em.op("vector",
                  lambda e: _recip(e, B.sbf.ap()[64:65, 512:640],
                                   B.bank(6, p0=64, p1=65, w=128)),
                  inc=f"{p}_rA", waits=(f"{p}_ctx",)),
            em.op("vector",
                  lambda e: _recip(e, B.sbf.ap()[32:33, 512:640],
                                   B.bank(7, p0=32, p1=33, w=128)),
                  inc=f"{p}_rB"),
            em.op("tensor",
                  lambda e: e.matmul(B.bank(6, p0=0, p1=64, w=128),
                                     B.ones_bf.ap()[64:65, 0:64],
                                     B.sbf.ap()[64:65, 512:640],
                                     start=True, stop=True),
                  inc=f"{p}_bcA", waits=(f"{p}_rA",)),
            em.op("tensor",
                  lambda e: e.matmul(B.bank(7, p0=64, p1=128, w=128),
                                     B.ones_bf.ap()[32:33, 0:64],
                                     B.sbf.ap()[32:33, 512:640],
                                     start=True, stop=True),
                  inc=f"{p}_bcB", waits=(f"{p}_rB",)),
            em.op("scalar",
                  lambda e: e.copy(ev[0:64, 1792:1920],
                                   B.bank(6, p0=0, p1=64, w=128)),
                  inc=f"{p}_cpA", waits=(f"{p}_bcA",)),
            em.op("scalar",
                  lambda e: e.copy(ev[64:128, 1792:1920],
                                   B.bank(7, p0=64, p1=128, w=128)),
                  inc=f"{p}_cpB", waits=(f"{p}_bcB",)),
            em.op("vector",
                  lambda e, pair=pair: e.tensor_tensor(
                      B.pl("ctxtg", pair * 128, 128, p0=0, p1=64),
                      B.bank(4, p0=0, p1=64, w=128), ev[0:64, 1792:1920],
                      ALU.mult),
                  inc=f"{p}_mlA", waits=(f"{p}_cpA",)),
            em.op("vector",
                  lambda e, pair=pair: e.tensor_tensor(
                      B.pl("ctxtg", pair * 128, 128, p0=64, p1=128),
                      B.bank(5, p0=64, p1=128, w=128), ev[64:128, 1792:1920],
                      ALU.mult),
                  inc=f"{p}_mlB", waits=(f"{p}_cpB",))
        for oh in range(2):
            bnk = oh
            for c in range(8):
                em.op("tensor",
                      lambda e, oh=oh, c=c, bnk=bnk: e.matmul(
                          B.bank(bnk),
                          B.pl("ctxtg", c * 128, 128),
                          B.pl("wo", c * 1024 + oh * 512, 512),
                          start=(c == 0), stop=(c == 7)),
                      inc=(f"{tag}_om{oh}" if c == 7 else None),
                      waits=(f"{tag}g7_mlA", f"{tag}g7_mlB", f"{tag}_woin")
                      if c == 0 else ())
            em.op("vector",
                  lambda e, oh=oh, bnk=bnk: e.tensor_scalar(
                      B.pl("o128", oh * 512, 512), B.bank(bnk),
                      ev[:, 1668:1669], None, ALU.mult),
                  inc=f"{tag}_oh{oh}", waits=(f"{tag}_om{oh}",))
            em.op("vector", lambda e: e.drain() or 1)
            em.op("vector",
                  lambda e, oh=oh, bnk=bnk: e.scalar_tensor_tensor(
                      B.pl("o128", 1024 + oh * 512, 512), B.bank(bnk),
                      ev[:, 1668:1669], B.pl("o128", oh * 512, 512),
                      ALU.mult, ALU.subtract),
                  inc=f"{tag}_ol{oh}", waits=(f"{tag}_oh{oh}",))
        for i in range(32):
            t, dh = i // 2, i % 2
            bnk = 2 + i % 4
            esl = B.auxe.ap()[:, t * 128:(t + 1) * 128]
            ws5 = [f"{tag}_ol0", f"{tag}_ol1"]
            if i >= 4:
                ws5.append(f"{tag}_xa{i-4}")
            em.op("tensor",
                  lambda e, esl=esl, dh=dh, bnk=bnk: e.matmul(
                      B.bank(bnk), esl, B.pl("o128", dh * 512, 512),
                      start=True, stop=False),
                  waits=tuple(ws5)),
            em.op("tensor",
                  lambda e, esl=esl, dh=dh, bnk=bnk: e.matmul(
                      B.bank(bnk), esl, B.pl("o128", 1024 + dh * 512, 512),
                      start=False, stop=True),
                  inc=f"{tag}_xm{i}")
            xsl = B.xres.ap()[:, t * DM + dh * 512: t * DM + dh * 512 + 512]
            em.op("vector",
                  lambda e, xsl=xsl, bnk=bnk: e.tensor_tensor(
                      xsl, xsl, B.bank(bnk), ALU.add),
                  inc=f"{tag}_xa{i}", waits=(f"{tag}_xm{i}",))
    em.run_block(ph_gattn)


# version 15 (final)
# speedup vs baseline: 1.0693x; 1.0063x over previous
import math
from collections import defaultdict

import numpy as np
import concourse.bass as bass
import concourse.mybir as mybir
from concourse.bass_utils import run_bass_kernel_spmd

F32 = mybir.dt.float32
BF16 = mybir.dt.bfloat16
AF = mybir.ActivationFunctionType
ALU = mybir.AluOpType
AX = mybir.AxisListType

S = 2048
DM = 1024
NH = 16
HD = 64
DC = 256
NE = 8
GS = 16
KSEL = 64
NT = S // 128
NB = S // 512
EPS = 1e-5
NEG = -1.0e30

N_CORES = 8
SEQ_PER_CORE = 2


def _aux_arrays():
    idm = np.eye(128, dtype=np.float32)
    ones = np.ones((128, 128), dtype=np.float32)
    caus = np.where(np.arange(128)[:, None] > np.arange(128)[None, :], NEG,
                    0.0).astype(np.float32)
    tril01 = (np.arange(128)[None, :] < np.arange(128)[:, None]).astype(np.float32)
    t = np.arange(128)
    u = np.arange(248)
    auxg = ((u[None, :] == (120 + t[:, None] // GS)).astype(np.float32) / GS)
    g = np.arange(128)
    tok = np.arange(S)
    auxe = (g[:, None] == (tok[None, :] // GS)).astype(np.float32)
    return {"aux_id": idm, "aux_ones": ones, "aux_caus": caus,
            "aux_tril": tril01, "aux_g": auxg, "aux_e": auxe}


DEBUG_DRAIN = False
DEBUG_DUMP = False
DEBUG_NB1 = False
ENGINES = ("tensor", "scalar", "vector", "gpsimd", "sync")


class Em:

    def __init__(self, nc):
        self.nc = nc
        self.sems = {e: nc.alloc_semaphore(f"sem_{e}") for e in ENGINES}
        self.dma_sems = {}
        self.mode = "count"
        self.cur = None
        self.cnt = defaultdict(int)
        self.dma_cnt = defaultdict(int)
        self.mark = {}
        self.water = defaultdict(int)
        self.emit_cnt = defaultdict(int)
        self.blk_no = 0

    def _dma_group(self, grp, eng):
        qc = "sw" if eng == "gpsimd" else "hw"
        if grp not in self.dma_sems:
            self.dma_sems[grp] = (self.nc.alloc_semaphore(f"dg_{grp}"), qc)
        assert self.dma_sems[grp][1] == qc, (grp, qc)
        return self.dma_sems[grp][0]

    def op(self, eng, fn, inc=None, waits=(), dma=False, grp=None):
        if dma and grp is None:
            grp = inc or "misc_" + eng
        if self.mode == "count":
            if dma:
                self._dma_group(grp, eng)
                self.dma_cnt[grp] += 1
                if inc is not None:
                    self.mark[inc] = (grp + "!dma", self.dma_cnt[grp] * 16)
            elif inc is not None:
                self.cnt[eng] += 1
                self.mark[inc] = (eng, self.cnt[eng])
            return None
        name, obj = self.cur
        if name != eng:
            return None
        for w in waits:
            prod, val = self.mark[w]
            if self.water[(eng, prod)] >= val:
                continue
            if prod.endswith("!dma"):
                obj.wait_ge(self.dma_sems[prod[:-4]][0], val)
            else:
                obj.wait_ge(self.sems[prod], val)
            self.water[(eng, prod)] = val
        instr = fn(obj)
        if dma:
            instr.then_inc(self.dma_sems[grp][0], 16)
            self.emit_cnt[grp + "!d"] += 1
        elif inc is not None:
            instr.then_inc(self.sems[eng], 1)
            self.emit_cnt[eng] += 1
        return instr

    def gate(self, waits, engines=("tensor", "scalar", "vector")):
        if self.mode == "count":
            return
        name, obj = self.cur
        if name not in engines:
            return
        for w in waits:
            prod, val = self.mark[w]
            if self.water[(name, prod)] >= val:
                continue
            if prod.endswith("!dma"):
                obj.wait_ge(self.dma_sems[prod[:-4]][0], val)
            else:
                obj.wait_ge(self.sems[prod], val)
            self.water[(name, prod)] = val

    def run_block(self, body):
        self.mode = "count"
        self.cur = None
        body(self)
        with self.nc.Block() as blk:
            for eng_name in ENGINES:
                def mk(_n):
                    def section(obj):
                        self.mode = "emit"
                        self.cur = (_n, obj)
                        body(self)
                    return section
                getattr(blk, eng_name)(mk(eng_name))
        self.blk_no += 1
        for e in ENGINES:
            assert self.emit_cnt[e] == self.cnt[e], (
                "count/emit mismatch", self.blk_no, e,
                self.emit_cnt[e], self.cnt[e])
        for key in list(self.water):
            cons, prod = key
            if not prod.endswith("!dma"):
                self.water[key] = self.cnt[prod]


POOL_COLS = 58624


def _lay(entries, extra=()):
    out, off = {}, 0
    for name, n in entries:
        out[name] = (off, n)
        off += n
    assert off <= POOL_COLS, (off, POOL_COLS)
    for name, o, n in extra:
        out[name] = (o, n)
    return out


LAY_L0A = _lay([
    ("w5", 5 * 2048),
    ("kt", 8 * 2048),
    ("v", NT * NH * 65),
    ("cqt", 2 * 2048),
    ("ckvt", 2 * 2048),
    ("xnt", 8 * 512),
    ("qt", 4 * 512),
])
LAY_L0A["attn"] = (LAY_L0A["xnt"][0], 8 * 512)
LAY_L0A["ctxt"] = (LAY_L0A["ckvt"][0], 8 * 512)
LAY_L0A["wo_a"] = (LAY_L0A["w5"][0], 4 * 1024)
LAY_L0A["wo_b"] = (LAY_L0A["w5"][0] + 3 * 2048, 4 * 1024)
_ = LAY_L0A

LAY_L0F = None
LAY_L1A = None
LAY_L1F = None


class Bufs:
    def __init__(self, nc):
        a = nc.alloc_sbuf_tensor
        self.xres = a("xres", [128, NT * DM], F32)
        self.id_bf = a("id_bf", [128, 128], BF16)
        self.ones_bf = a("ones_bf", [128, 128], BF16)
        self.caus = a("caus", [128, 128], F32)
        self.tril = a("tril", [128, 128], F32)
        self.auxg = a("auxg", [128, 248], BF16)
        self.auxe = a("auxe", [128, S], BF16)
        self.stats = a("stats", [128, 128], F32)
        self.bn12 = a("bn12", [128, 12], F32)
        self.bqkv = a("bqkv", [128, 32], F32)
        self.gbfm = a("gbfm", [128, 16], F32)
        self.rows = a("rows", [128, 3072], BF16)
        self.ev = a("ev", [128, 2048], F32)
        self.s32 = a("s32", [128, 1024], F32)
        self.sbf = a("sbf", [128, 2048], BF16)
        self.consts = a("consts", [128, 8], F32)
        self.pool = a("pool", [128, POOL_COLS], BF16)
        self.psum_t = nc.alloc_psum_tensor("ps", [128, 4096], F32)
        self.lay = None

    def bank(self, i, p0=0, p1=128, c0=0, w=512):
        return self.psum_t.ap()[p0:p1, i * 512 + c0: i * 512 + c0 + w]

    def bank_bf(self, i, p0=0, p1=128, c0=0, w=1024):
        return self.psum_t.ap().bitcast(BF16)[p0:p1, i * 1024 + c0: i * 1024 + c0 + w]

    def pl(self, name, c0=0, w=None, p0=0, p1=128):
        off, n = self.lay[name]
        if w is None:
            w = n - c0
        assert c0 + w <= n, (name, c0, w, n)
        return self.pool.ap()[p0:p1, off + c0: off + c0 + w]


def fm_col(ap_2d, c):
    return ap_2d[:, c:c + 1]


def emit_load_x(em, B, x_dram_seq, tag, waits=()):
    src = x_dram_seq.rearrange("(t p) d -> p t d", p=128)
    dst = B.xres.ap().rearrange("p (t d) -> p t d", d=DM)
    em.op("sync", lambda e: e.dma_start(dst, src), inc=f"{tag}_xin",
          waits=waits, dma=True)


def emit_store_x(em, B, out_dram_seq, tag, waits):
    src = B.xres.ap().rearrange("p (t d) -> p t d", d=DM)
    dst = out_dram_seq.rearrange("(t p) d -> p t d", p=128)
    em.op("sync", lambda e: e.dma_start(dst, src), inc=f"{tag}_xout",
          waits=waits, dma=True)


def emit_ln_stats(em, B, tag, rstd2=False, waits=()):
    st = B.stats.ap()
    for t in range(NT):
        x_t = B.xres.ap()[:, t * DM:(t + 1) * DM]
        w0 = list(waits) if t == 0 else [f"{tag}_ag{t-1}"]
        em.op("vector", lambda e, x_t=x_t: e.bn_stats(B.bn12.ap()[:, 0:6],
                                                      x_t[:, 0:512]),
              inc=f"{tag}_bna{t}", waits=tuple(w0))
        em.op("vector", lambda e, x_t=x_t: e.bn_stats(B.bn12.ap()[:, 6:12],
                                                      x_t[:, 512:1024]),
              inc=f"{tag}_bnb{t}")
        em.op("vector", lambda e, t=t: e.bn_aggr(st[:, 2 * t:2 * t + 2],
                                                 B.bn12.ap()),
              inc=f"{tag}_ag{t}", waits=(f"{tag}_bna{t}", f"{tag}_bnb{t}"))
    mv = B.stats.ap().rearrange("p (t two) -> p t two", two=2)
    em.op("vector", lambda e: e.tensor_copy(
        st[:, 32:48].unsqueeze(2), mv[:, 0:16, 0:1]),
        inc=f"{tag}_m", waits=(f"{tag}_ag{NT-1}",))
    em.op("scalar", lambda e: e.activation(
        st[:, 80:96].unsqueeze(2), mv[:, 0:16, 1:2], AF.Ln,
        bias=fm_col(B.consts.ap(), 0), scale=1.0),
        inc=f"{tag}_lnv", waits=(f"{tag}_ag{NT-1}", "k_consts"))
    em.op("scalar", lambda e: e.activation(
        st[:, 48:64], st[:, 80:96], AF.Exp, bias=0.0, scale=-0.5),
        inc=f"{tag}_rstd", waits=(f"{tag}_lnv",))
    if rstd2:
        em.op("scalar", lambda e: e.activation(
            st[:, 80:96].unsqueeze(2), mv[:, 0:16, 1:2], AF.Ln,
            bias=fm_col(B.consts.ap(), 1), scale=float(1.0 + EPS)),
            inc=f"{tag}_lnv2", waits=(f"{tag}_rstd",))
        em.op("scalar", lambda e: e.activation(
            st[:, 64:80], st[:, 80:96], AF.Exp, bias=0.0, scale=-0.5),
            inc=f"{tag}_rstd2", waits=(f"{tag}_lnv2",))


def emit_xnt_tiles(em, B, tag, tiles, dst_name, blockw, rstd_col=48, gb=False,
                   waits=(), lo=False):
    st = B.stats.ap()
    tpb = blockw // 128
    prev = None
    for t in tiles:
        rt = t % tpb
        x_t = B.xres.ap()[:, t * DM:(t + 1) * DM]
        wx = list(waits) + ([prev] if prev else [])
        if not lo:
            em.op("vector",
                  lambda e, x_t=x_t, t=t: e.tensor_scalar(
                      B.sbf.ap()[:, 0:1024], x_t, fm_col(st, 32 + t),
                      fm_col(st, rstd_col + t), ALU.subtract, ALU.mult),
                  inc=f"{tag}_xn{t}", waits=tuple(wx))
        else:
            em.op("vector",
                  lambda e, x_t=x_t, t=t: e.tensor_scalar(
                      B.s32.ap()[:, 0:1024], x_t, fm_col(st, 32 + t),
                      fm_col(st, rstd_col + t), ALU.subtract, ALU.mult),
                  inc=f"{tag}_x32{t}", waits=tuple(wx))
            em.op("vector", lambda e: e.tensor_copy(B.sbf.ap()[:, 0:1024],
                                                    B.s32.ap()[:, 0:1024]),
                  inc=f"{tag}_xn{t}", waits=(f"{tag}_x32{t}",))
            em.op("vector",
                  lambda e: e.tensor_tensor(B.sbf.ap()[:, 1024:2048],
                                            B.s32.ap()[:, 0:1024],
                                            B.sbf.ap()[:, 0:1024],
                                            ALU.subtract),
                  inc=f"{tag}_xlo{t}", waits=(f"{tag}_xn{t}",))
        parts = [("", 0, [f"{tag}_xn{t}"])]
        if lo:
            parts.append(("_lo", 1024, [f"{tag}_xlo{t}", f"{tag}_ev{t}"]))
        for sfx, srcoff, wnames in parts:
            evw = f"{tag}_ev{sfx}{t}"
            trw = f"{tag}_tr{sfx}{t}"
            for c in range(8):
                bnk = 6 + c // 4
                ws = []
                if c == 0:
                    ws = list(wnames)
                    pe = f"{tag}_evprev{t}"
                    if sfx == "" and pe in em.mark:
                        ws.append(pe)
                em.op("tensor",
                      lambda e, c=c, bnk=bnk, srcoff=srcoff: e.matmul(
                          B.bank(bnk, c0=(c % 4) * 128, w=128),
                          B.sbf.ap()[:, srcoff + c * 128:srcoff + (c + 1) * 128],
                          B.id_bf.ap()[:], start=True, stop=True),
                      inc=(trw if c == 7 else None), waits=tuple(ws))
            dst = B.pl(dst_name + sfx).rearrange(
                "p (c n) -> p c n", c=8)[:, :, rt * 128:rt * 128 + 128]
            if not gb:
                em.op("vector", lambda e, dst=dst: e.tensor_copy(
                    dst[:, 0:4],
                    B.bank(6).rearrange("p (c n) -> p c n", c=4)),
                    waits=(trw,), inc=f"{tag}_evh{sfx}{t}")
                em.op("vector", lambda e, dst=dst: e.tensor_copy(
                    dst[:, 4:8],
                    B.bank(7).rearrange("p (c n) -> p c n", c=4)),
                    inc=evw)
            else:
                for c in range(8):
                    bnk = 6 + c // 4
                    em.op("vector",
                          lambda e, c=c, bnk=bnk, dst=dst: e.tensor_scalar(
                              dst[:, c], B.bank(bnk, c0=(c % 4) * 128, w=128),
                              fm_col(B.gbfm.ap(), c),
                              fm_col(B.gbfm.ap(), 8 + c), ALU.mult, ALU.add),
                          waits=((trw,) if c == 0 else ()),
                          inc=(evw if c == 7 else None))
        idx = tiles.index(t)
        nxt = tiles[idx + 1] if idx + 1 < len(tiles) else None
        if nxt is not None and em.mode == "count":
            em.mark[f"{tag}_evprev{nxt}"] = em.mark[
                f"{tag}_ev_lo{t}" if lo else f"{tag}_ev{t}"]
        prev = f"{tag}_tr{'_lo' if lo else ''}{t}"


def emit_l0_cproj(em, B, tag):
    for b in range(NB):
        emit_xnt_tiles(em, B, f"{tag}b{b}", [b * 4 + i for i in range(4)],
                       "xnt", 512, waits=(f"{tag}_rstd", f"{tag}_cev{b-1}")
                       if b else (f"{tag}_rstd",))
        for wi, (wslot, dstn) in enumerate((("wdq", "cqt"), ("wdkv", "ckvt"))):
            for cc in range(2):
                bnk = wi * 2 + cc
                for c in range(8):
                    em.op("tensor",
                          lambda e, wi=wi, cc=cc, c=c, b=b, bnk=bnk: e.matmul(
                              B.bank(bnk),
                              B.pl("w5", wi * 2048 + c * 256 + cc * 128, 128),
                              B.pl("xnt", c * 512, 512),
                              start=(c == 0), stop=(c == 7)),
                          inc=(f"{tag}_cmm{b}_{wi}{cc}" if c == 7 else None),
                          waits=(tuple([f"{tag}b{b}_ev{b*4+3}", "l0a_w5"] +
                                       ([f"{tag}_cev{b-1}"] if b else []))
                                 if c == 0 and cc == 0 and wi == 0 else ()))
                dst = "cqt" if wi == 0 else "ckvt"
                em.op("vector",
                      lambda e, dst=dst, cc=cc, b=b, bnk=bnk: e.tensor_copy(
                          B.pl(dst, cc * 2048 + b * 512, 512), B.bank(bnk)),
                      inc=(f"{tag}_cev{b}" if wi == 1 and cc == 1 else None),
                      waits=(f"{tag}_cmm{b}_{wi}{cc}",))


def emit_l0_kv(em, B, tag):
    for j in range(32):
        hp, qb = j // 4, j % 4
        bnk = j % 2
        ws = [f"{tag}_cev3"]
        if j >= 2:
            ws.append(f"{tag}_ktev{j-2}")
        for cc in range(2):
            em.op("tensor",
                  lambda e, hp=hp, qb=qb, cc=cc, bnk=bnk: e.matmul(
                      B.bank(bnk),
                      B.pl("w5", 3 * 2048 + cc * 1024 + hp * 128, 128),
                      B.pl("ckvt", cc * 2048 + qb * 512, 512),
                      start=(cc == 0), stop=(cc == 1)),
                  inc=(f"{tag}_ktm{j}" if cc == 1 else None),
                  waits=tuple(ws) if cc == 0 else ())
        em.op("vector",
              lambda e, hp=hp, qb=qb, bnk=bnk: e.tensor_copy(
                  B.pl("kt", hp * 2048 + qb * 512, 512), B.bank(bnk)),
              inc=f"{tag}_ktev{j}", waits=(f"{tag}_ktm{j}",))
    for i in range(NT * 2):
        t, h8 = i // 2, i % 2
        bnk = 2 + i % 2
        ws = [f"{tag}_cev3"]
        if i >= 2:
            ws.append(f"{tag}_vev{i-2}")
        for cc in range(2):
            em.op("tensor",
                  lambda e, t=t, h8=h8, cc=cc, bnk=bnk: e.matmul(
                      B.bank(bnk),
                      B.pl("ckvt", cc * 2048 + t * 128, 128),
                      B.pl("w5", 4 * 2048 + cc * 1024 + h8 * 512, 512),
                      start=(cc == 0), stop=(cc == 1)),
                  inc=(f"{tag}_vm{i}" if cc == 1 else None),
                  waits=tuple(ws) if cc == 0 else ())
        em.op("vector",
              lambda e, t=t, h8=h8, bnk=bnk: e.tensor_copy(
                  B.pl("v", (t * 16 + h8 * 8) * 65, 8 * 65).rearrange(
                      "p (h n) -> p h n", h=8)[:, :, 0:64],
                  B.bank(bnk).rearrange("p (h n) -> p h n", h=8)),
              inc=f"{tag}_vev{i}", waits=(f"{tag}_vm{i}",))
    em.op("vector",
          lambda e: e.memset(B.pl("v").rearrange(
              "p (g n) -> p g n", n=65)[:, :, 64:65], 1.0),
          inc=f"{tag}_vones", waits=(f"{tag}_vev{NT*2-1}",))
    for hp in range(8):
        em.op("vector",
              lambda e, hp=hp: e.tensor_tensor(
                  B.sbf.ap()[:, 0:2048], B.pl("kt", hp * 2048, 2048),
                  B.pl("kt", hp * 2048, 2048), ALU.mult),
              inc=f"{tag}_ksq{hp}",
              waits=(f"{tag}_ktev{hp*4+3}", f"{tag}_kred{hp-1}") if hp
              else (f"{tag}_ktev{hp*4+3}",))
        for qb in range(4):
            em.op("tensor",
                  lambda e, qb=qb: e.matmul(
                      B.bank(4 + qb, p1=1),
                      B.ones_bf.ap()[:, 0:1], B.sbf.ap()[:, qb * 512:(qb + 1) * 512],
                      start=True, stop=True),
                  inc=(f"{tag}_knm{hp}" if qb == 3 else None),
                  waits=(f"{tag}_ksq{hp}",) if qb == 0 else ())
        em.op("vector",
              lambda e, hp=hp: e.tensor_reduce(
                  B.ev.ap()[0:1, 1024 + hp:1025 + hp],
                  B.psum_t.ap()[0:1, 4 * 512:8 * 512], AX.X, ALU.max),
              inc=f"{tag}_kred{hp}", waits=(f"{tag}_knm{hp}",))
    em.op("scalar", lambda e: e.activation(
        B.ev.ap()[0:1, 1032:1040], B.ev.ap()[0:1, 1024:1032], AF.Ln,
        bias=0.0, scale=1.0),
        inc=f"{tag}_lnk", waits=(f"{tag}_kred7",))


def emit_attn_core(em, B, tag, *, causal, n_hp, ktmax_f, kt_slice, v_slice,
                   qt_emit, qt_bias, wo_slice, n_wo_chunks, bias_rows,
                   wo_marks, xadd_acc):
    ev = B.ev.ap()
    st_attn = lambda slot: B.pl("attn", slot * 512, 512)

    def pname(qb, hp):
        return f"{tag}a{qb}_{hp}"

    hps_prev_tail = [None]

    nb_eff = 1 if (DEBUG_DUMP and DEBUG_NB1 and tag == "s0l1ah0") else NB
    for qb in range(nb_eff):
        ktmax = ktmax_f(qb)
        assert ktmax % 2 == 0
        npair = ktmax // 2
        waves = [range(0, min(4, n_hp))] + \
            ([range(4, 8)] if n_hp == 8 else [])

        def emit_wave(wv, qb=qb, ktmax=ktmax):
            wvl = list(wv)
            negbase = 1052 if wvl[0] == 0 else 1056
            for i, hp in enumerate(wvl):
                p = pname(qb, hp)
                qslot = hp % 4
                ws = []
                if hp == wvl[0]:
                    if hp >= 4:
                        ws.append(f"{pname(qb, hp-1)}_expg{ktmax-1}")
                    elif qb > 0:
                        ws.append(f"{pname(qb-1, n_hp-1)}_expg"
                                  f"{ktmax_f(qb-1)-1}")
                qt_emit(em, p, tuple(ws), qb, hp)
                prevu = None
                if n_hp == 8:
                    prevu = pname(qb, hp - 4) if hp >= 4 else (
                        pname(qb - 1, hp + 4) if qb > 0 else None)
                else:
                    prevu = pname(qb - 1, hp) if qb > 0 else None
                qbias = qt_bias(hp)
                qtw = [f"{p}_qtm"]
                if prevu:
                    qtw.append(f"{prevu}_scl")
                if isinstance(qbias, float):
                    em.op("scalar",
                          lambda e, qslot=qslot, hp=hp, qbias=qbias:
                          e.activation(
                              B.pl("qt", qslot * 512, 512), B.bank(hp % 4),
                              AF.Copy, bias=qbias, scale=0.125),
                          inc=f"{p}_qtev", waits=tuple(qtw))
                else:
                    em.op("vector",
                          lambda e, qslot=qslot, hp=hp, qbias=qbias:
                          e.tensor_scalar(
                              B.pl("qt", qslot * 512, 512), B.bank(hp % 4),
                              qbias, 0.125, ALU.add, ALU.mult),
                          inc=f"{p}_qtev", waits=tuple(qtw))
                em.op("vector",
                      lambda e, qslot=qslot: e.tensor_tensor(
                          B.sbf.ap()[:, qslot * 512:(qslot + 1) * 512],
                          B.pl("qt", qslot * 512, 512),
                          B.pl("qt", qslot * 512, 512), ALU.mult),
                      inc=f"{p}_qsq", waits=(f"{p}_qtev",))
                bnk = 6 + i // 2
                row = 32 * (i % 2)
                wqn = [f"{p}_qsq"]
                if hp == wvl[0]:
                    lastp = hps_prev_tail[0]
                    if lastp:
                        wqn.append(f"{lastp}_cp")
                    if wvl[0] == 4:
                        wqn.append(f"{pname(qb, 3)}_cp")
                    if qb > 0 and wvl[0] == 0:
                        wqn += [f"{tag}a{qb-1}_xadd6", f"{tag}a{qb-1}_xadd7"]
                em.op("tensor",
                      lambda e, qslot=qslot, bnk=bnk, row=row: e.matmul(
                          B.bank(bnk, p0=row, p1=row + 1),
                          B.ones_bf.ap()[:, 0:1],
                          B.sbf.ap()[:, qslot * 512:(qslot + 1) * 512],
                          start=True, stop=True,
                          tile_position=(0, row)),
                      inc=f"{p}_qnm", waits=tuple(wqn))
                em.op("vector",
                      lambda e, bnk=bnk, row=row, i=i: e.tensor_reduce(
                          ev[row:row + 1, 1040 + i:1041 + i],
                          B.bank(bnk, p0=row, p1=row + 1), AX.X, ALU.max),
                      inc=f"{p}_qred", waits=(f"{p}_qnm",))
            for i, hp in enumerate(wvl):
                row = 32 * (i % 2)
                if row:
                    em.op("vector",
                          lambda e, i=i: e.tensor_copy(
                              ev[0:1, 1040 + i:1041 + i],
                              ev[32:33, 1040 + i:1041 + i]),
                          inc=f"{pname(qb, wvl[i])}_qmv",
                          waits=(f"{pname(qb, wvl[i])}_qred",))
            last_red = [f"{pname(qb, h)}_qred" for h in wvl] + \
                [f"{pname(qb, wvl[i])}_qmv" for i in range(len(wvl))
                 if 32 * (i % 2)]
            nw = len(wvl)
            em.op("scalar",
                  lambda e, nw=nw: e.activation(
                      ev[0:1, 1044:1044 + nw], ev[0:1, 1040:1040 + nw],
                      AF.Ln, bias=0.0, scale=1.0),
                  inc=f"{tag}a{qb}w{wvl[0]}_lnq", waits=tuple(last_red))
            em.op("vector",
                  lambda e, nw=nw, w0=wvl[0]: e.tensor_tensor(
                      ev[0:1, 1048:1048 + nw], ev[0:1, 1044:1044 + nw],
                      ev[0:1, 1032 + w0:1032 + w0 + nw], ALU.add),
                  inc=f"{tag}a{qb}w{wvl[0]}_lns",
                  waits=(f"{tag}a{qb}w{wvl[0]}_lnq", f"{tag}_lnk"))
            em.op("scalar",
                  lambda e, nw=nw, w0=wvl[0]: e.activation(
                      B.rows.ap()[64:65, 1024 + w0:1024 + w0 + nw],
                      ev[0:1, 1048:1048 + nw], AF.Exp, bias=0.0, scale=0.5),
                  inc=f"{tag}a{qb}w{wvl[0]}_cexp",
                  waits=(f"{tag}a{qb}w{wvl[0]}_lns",))
            for i, hp in enumerate(wvl):
                em.op("tensor",
                      lambda e, i=i, hp=hp: e.matmul(
                          B.bank(6, c0=64 + i, w=1),
                          B.ones_bf.ap()[64:65, 0:128],
                          B.rows.ap()[64:65, 1024 + hp:1025 + hp],
                          start=True, stop=True),
                      inc=(f"{tag}a{qb}w{wvl[0]}_cbc" if i == nw - 1
                           else None),
                      waits=(f"{tag}a{qb}w{wvl[0]}_cexp",) if i == 0 else ())
            em.op("vector",
                  lambda e, nw=nw, negbase=negbase: e.tensor_scalar(
                      ev[:, negbase:negbase + nw], B.bank(6, c0=64, w=nw),
                      -1.0, None, ALU.mult),
                  inc=f"{tag}a{qb}w{wvl[0]}_negc",
                  waits=(f"{tag}a{qb}w{wvl[0]}_cbc",))
            if em.mode == "count":
                em.mark[f"{tag}a{qb}w{wvl[0]}_negcrd"] = \
                    em.mark[f"{tag}a{qb}w{wvl[0]}_negc"]

        emit_wave(waves[0])

        pending = [None]

        def emit_epi1(hp, qb=qb, ktmax=ktmax):
            p = pname(qb, hp)
            ctxe = f"{p}_ctxq{ktmax-1}"

            em.op("scalar",
                  lambda e: e.activation(ev[64:65, 0:1024],
                                         B.psum_t.ap()[64:65, 2048:3072],
                                         AF.Ln, bias=0.0,
                                         scale=7.1700422e15),
                  inc=f"{p}_lnz", waits=(ctxe,))
            em.op("scalar",
                  lambda e: e.activation(B.rows.ap()[64:65, 0:1024],
                                         ev[64:65, 0:1024], AF.Exp,
                                         bias=B.consts.ap()[64:65, 2:3],
                                         scale=-1.0),
                  inc=f"{p}_recip", waits=(f"{p}_lnz",))
            wbc = [f"{p}_recip",
                   f"{tag}a{qb}w{4 if (n_hp == 8 and hp >= 4) else 0}_negcrd"]
            em.op("tensor",
                  lambda e: e.matmul(B.bank(6, p0=0, p1=64),
                                     B.ones_bf.ap()[64:65, 0:64],
                                     B.rows.ap()[64:65, 0:512],
                                     start=True, stop=True),
                  inc=f"{p}_bcA", waits=tuple(wbc))
            em.op("tensor",
                  lambda e: e.matmul(B.bank(7, p0=0, p1=64),
                                     B.ones_bf.ap()[64:65, 0:64],
                                     B.rows.ap()[64:65, 512:1024],
                                     start=True, stop=True),
                  inc=f"{p}_bcB")
            em.op("scalar",
                  lambda e: e.copy(ev[0:64, 0:1024],
                                   B.psum_t.ap()[0:64, 3072:4096]),
                  inc=f"{p}_cp", waits=(f"{p}_bcB",))
            em.op("vector",
                  lambda e, hp=hp: e.tensor_tensor(
                      B.pl("ctxt", hp * 512, 512, p0=0, p1=64),
                      B.bank(4, p0=0, p1=64), ev[0:64, 0:512], ALU.mult),
                  inc=f"{p}_evA", waits=(f"{p}_cp", ctxe))
            em.op("vector",
                  lambda e: e.tensor_tensor(
                      B.rows.ap()[0:64, 2048:2560],
                      B.bank(5, p0=0, p1=64), ev[0:64, 512:1024], ALU.mult),
                  inc=f"{p}_evB")

        def emit_epi2(hp, qb=qb):
            p = pname(qb, hp)
            em.op("tensor",
                  lambda e: e.matmul(B.bank(5, p0=64, p1=128),
                                     B.id_bf.ap()[0:64, 0:64],
                                     B.rows.ap()[0:64, 2048:2560],
                                     start=True, stop=True,
                                     tile_position=(0, 64)),
                  inc=f"{p}_mv", waits=(f"{p}_evB",))
            em.op("vector",
                  lambda e, hp=hp: e.tensor_copy(
                      B.pl("ctxt", hp * 512, 512, p0=64, p1=128),
                      B.bank(5, p0=64, p1=128)),
                  inc=f"{p}_cpB2", waits=(f"{p}_mv",))

        def flush_epi():
            if pending[0] is not None:
                emit_epi1(pending[0])
                emit_epi2(pending[0])
                pending[0] = None

        for hp in range(n_hp):
            if n_hp == 8 and hp == 4:
                flush_epi()
                emit_wave(waves[1])
            p = pname(qb, hp)
            wavebase = 0 if hp < 4 else 4
            negbase = 1052 if wavebase == 0 else 1056
            negcol = negbase + (hp - wavebase)
            prevp = pname(qb, hp - 1) if hp > 0 else \
                (pname(qb - 1, n_hp - 1) if qb > 0 else None)
            prev_ktmax = (ktmax if hp > 0 else ktmax_f(qb - 1)) \
                if prevp else 0
            def emit_ctx(kt, hp=hp, p=p, prevp=prevp, ktmax=ktmax):
                first, last = (kt == 0), (kt == ktmax - 1)
                for head, cbnk in (("A", 4), ("B", 5)):
                    slot = 2 * (kt % 2) + (0 if head == "A" else 1)
                    wctx = [f"{p}_expg{kt}"]
                    wctx_b = []
                    if kt == 0 and prevp:
                        wctx += [f"{prevp}_evA"]
                        wctx_b = [f"{prevp}_cpB2"]
                    em.op("tensor",
                          lambda e, kt=kt, hp=hp, head=head, cbnk=cbnk,
                          slot=slot, first=first, last=last: e.matmul(
                              B.bank(cbnk, p0=0, p1=65),
                              v_slice(hp, kt, head),
                              B.pl("attn", slot * 512, 512),
                              start=first, stop=last),
                          inc=(f"{p}_ctxq{kt}" if head == "B" else None),
                          waits=tuple(wctx) if head == "A"
                          else tuple(wctx_b))

            for kt in range(ktmax):
                bA = 2 * (kt % 2)
                bB = bA + 1
                if kt == 0 and pending[0] is not None:
                    emit_epi1(pending[0])
                for hi, (head, pp0, pp1, bnk) in enumerate(
                        (("A", 0, 64, bA), ("B", 64, 128, bB))):
                    wsc = []
                    if hi == 0:
                        if kt >= 2:
                            wsc = [f"{p}_expg{kt-2}"]
                        else:
                            wsc = [f"{p}_qtev"]
                            if hp in (0, 4):
                                wsc.append(f"{pname(qb, min(hp + 3, n_hp - 1))}_qtev")
                            if prevp:
                                wsc.append(
                                    f"{prevp}_expg{prev_ktmax - 2 + kt}")
                    em.op("tensor",
                          lambda e, kt=kt, hp=hp, head=head, pp0=pp0,
                          pp1=pp1, bnk=bnk: e.matmul(
                              B.bank(bnk), kt_slice(hp, kt, head),
                              B.pl("qt", (hp % 4) * 512, 512,
                                   p0=pp0, p1=pp1),
                              start=True, stop=True,
                              tile_position=(pp0, 0)),
                          inc=(f"{p}_scq{kt}" if hi == 1 else None),
                          waits=tuple(wsc))
                if kt == ktmax - 1 and em.mode == "count":
                    em.mark[f"{p}_scl"] = em.mark[f"{p}_scq{kt}"]
                rel = (kt - 4 * qb) if causal else -1
                straddle = causal and rel >= 0
                expw = [f"{p}_scq{kt}", f"{tag}a{qb}w{wavebase}_negc"]
                if kt >= 2:
                    expw.append(f"{p}_ctxq{kt-2}")
                elif prevp:
                    expw.append(f"{prevp}_ctxq{prev_ktmax - 2 + kt}")
                if not straddle:
                    em.op("scalar",
                          lambda e, bA=bA: e.activation(
                              B.pl("attn", bA * 512, 1024),
                              B.psum_t.ap()[:, bA * 512:(bA + 2) * 512],
                              AF.Exp, bias=fm_col(ev, negcol), scale=1.0),
                          inc=f"{p}_expg{kt}", waits=tuple(expw))
                else:
                    for hi2, bnk in enumerate((bA, bB)):
                        em.op("vector",
                              lambda e, bnk=bnk, rel=rel: e.tensor_tensor(
                                  B.bank(bnk, c0=rel * 128, w=128),
                                  B.bank(bnk, c0=rel * 128, w=128),
                                  B.caus.ap()[:], ALU.add),
                              inc=(f"{p}_msk{kt}" if hi2 == 1 else None),
                              waits=(f"{p}_scq{kt}",) if hi2 == 0 else ())
                    msw = [f"{p}_msk{kt}"] + expw[1:]
                    if rel > 0:
                        for hi2, bnk in enumerate((bA, bB)):
                            em.op("vector",
                                  lambda e, bnk=bnk, rel=rel: e.memset(
                                      B.pl("attn", bnk * 512, rel * 128),
                                      0.0),
                                  inc=f"{p}_ms{kt}",
                                  waits=tuple(msw) if hi2 == 0 else ())
                    expw2 = [f"{p}_msk{kt}"] + expw[1:] + \
                        ([f"{p}_ms{kt}"] if rel > 0 else [])
                    c0 = max(rel, 0) * 128
                    for hi2, bnk in enumerate((bA, bB)):
                        em.op("scalar",
                              lambda e, bnk=bnk, c0=c0: e.activation(
                                  B.pl("attn", bnk * 512 + c0, 512 - c0),
                                  B.bank(bnk, c0=c0, w=512 - c0),
                                  AF.Exp, bias=fm_col(ev, negcol),
                                  scale=1.0),
                              inc=(f"{p}_expg{kt}" if hi2 == 1 else None),
                              waits=tuple(expw2) if hi2 == 0 else ())
                if kt == 1 and pending[0] is not None:
                    emit_epi2(pending[0])
                    pending[0] = None
                if kt >= 1:
                    emit_ctx(kt - 1)
            emit_ctx(ktmax - 1)
            pending[0] = hp
        flush_epi()
        hps_prev_tail[0] = pname(qb, n_hp - 1)
        if DEBUG_DUMP and tag == "s0l1ah0" and qb == 0:
            _lp = pname(qb, n_hp - 1)
            if not hasattr(em, "_dbg"):
                em._dbg = {}
                em._dbg["c"] = em.nc.dram_tensor(
                    "dbg_ctxt", [128, 4096], F32, kind="ExternalOutput")
                em._dbg["q"] = em.nc.dram_tensor(
                    "dbg_qt", [128, 2048], F32, kind="ExternalOutput")
                em._dbg["r"] = em.nc.dram_tensor(
                    "dbg_rz", [128, 3072], F32, kind="ExternalOutput")
            dbg_c, dbg_q, dbg_r = em._dbg["c"], em._dbg["q"], em._dbg["r"]
            em.op("gpsimd", lambda e: e.dma_start(
                dbg_c.ap(), B.pl("ctxt", 0, 4096)),
                inc="dbgct", dma=True, grp="dbg",
                waits=(f"{_lp}_evA", f"{_lp}_cpB2"))
            em.op("gpsimd", lambda e: e.dma_start(
                dbg_q.ap(), B.pl("qt", 0, 2048)),
                inc="dbgqt", dma=True, grp="dbg",
                waits=(f"{_lp}_qtev",))
            em.op("gpsimd", lambda e: e.dma_start(
                dbg_r.ap(), B.rows.ap()),
                inc="dbgrz", dma=True, grp="dbg",
                waits=(f"{_lp}_recip",))
        lastp = pname(qb, n_hp - 1)
        for i in range(8):
            rt, oh = i // 2, i % 2
            bnk = 6 + i % 2
            wsj = [f"{lastp}_evA", f"{lastp}_cpB2", f"{lastp}_cp"] + \
                list(wo_marks)
            if i >= 2:
                wsj.append(f"{tag}a{qb}_xadd{i-2}")
            brows = bias_rows(oh)
            nmm = n_wo_chunks + len(brows)
            k = 0
            for ci in range(n_wo_chunks):
                em.op("tensor",
                      lambda e, ci=ci, rt=rt, oh=oh, bnk=bnk, k=k, nmm=nmm:
                      e.matmul(
                          B.bank(bnk), B.pl("ctxt", ci * 512 + rt * 128, 128),
                          wo_slice(ci, oh), start=(k == 0),
                          stop=(k == nmm - 1)),
                      inc=(f"{tag}a{qb}_opm{i}" if k == nmm - 1 else None),
                      waits=tuple(wsj) if k == 0 else ())
                k += 1
            for br in brows:
                em.op("tensor",
                      lambda e, br=br, bnk=bnk, k=k, nmm=nmm: e.matmul(
                          B.bank(bnk), B.ones_bf.ap()[0:1, 0:128], br,
                          start=(k == 0), stop=(k == nmm - 1)),
                      inc=(f"{tag}a{qb}_opm{i}" if k == nmm - 1 else None))
                k += 1
            xsl = B.xres.ap()[:, (qb * 4 + rt) * DM + oh * 512:
                              (qb * 4 + rt) * DM + oh * 512 + 512]
            em.op("vector",
                  lambda e, xsl=xsl, bnk=bnk: e.tensor_tensor(
                      xsl, xsl, B.bank(bnk), ALU.add),
                  inc=f"{tag}a{qb}_xadd{i}", waits=(f"{tag}a{qb}_opm{i}",))


def emit_l0a_attn(em, B, tag, wo_marks):
    def qt_emit(em, p, ws, qb, hp):
        ws = ws + ("l0a_w5", f"{tag}_cev3")
        for cc in range(2):
            em.op("tensor",
                  lambda e, cc=cc, hp=hp, qb=qb: e.matmul(
                      B.bank(hp % 4),
                      B.pl("w5", 2 * 2048 + cc * 1024 + hp * 128, 128),
                      B.pl("cqt", cc * 2048 + qb * 512, 512),
                      start=(cc == 0), stop=(cc == 1)),
                  inc=(f"{p}_qtm" if cc == 1 else None),
                  waits=ws if cc == 0 else ())

    emit_attn_core(
        em, B, tag, causal=True, n_hp=8, ktmax_f=lambda qb: 4 * (qb + 1),
        kt_slice=lambda hp, kt, part: B.pl(
            "kt", hp * 2048 + kt * 128, 128,
            p0=0 if part == "A" else 64, p1=64 if part == "A" else 128),
        v_slice=lambda hp, kt, part: B.pl(
            "v", (kt * 16 + hp * 2 + (0 if part == "A" else 1)) * 65, 65),
        qt_emit=qt_emit, qt_bias=lambda hp: 0.0,
        wo_slice=lambda ci, oh: B.pl("wo_a", ci * 1024 + oh * 512, 512)
        if ci < 4 else B.pl("wo_b", (ci - 4) * 1024 + oh * 512, 512),
        n_wo_chunks=8, bias_rows=lambda oh: [], wo_marks=wo_marks,
        xadd_acc=True)


WSPECS_L0A = [
    ("L0_attn_W_dq", "w5", 0, 8),
    ("L0_attn_W_dkv", "w5", 2048, 8),
    ("L0_attn_W_uq", "w5", 2 * 2048, 2),
    ("L0_attn_W_uk", "w5", 3 * 2048, 2),
    ("L0_attn_W_uv", "w5", 4 * 2048, 2),
]


def build_core_program(n_seq=SEQ_PER_CORE,
                       phases=("l0a", "l0f", "l1a", "l1f")):
    nc = bass.Bass("TRN2", target_bir_lowering=False)
    x_d = nc.dram_tensor("x", [n_seq, S, DM], F32, kind="ExternalInput")
    out_d = nc.dram_tensor("out", [n_seq, S, DM], F32, kind="ExternalOutput")
    aux = {k: nc.dram_tensor(k, list(v.shape), F32, kind="ExternalInput")
           for k, v in _aux_arrays().items()}
    W = {}
    wshapes = {"L0_attn_W_dq": [DM, DC], "L0_attn_W_dkv": [DM, DC],
               "L0_attn_W_uq": [DC, DM], "L0_attn_W_uk": [DC, DM],
               "L0_attn_W_uv": [DC, DM], "L0_attn_W_o": [DM, DM],
               "L0_ffn_gate_w": [DM, NE], "L0_ffn_gate_b": [NE],
               "L0_ffn_expert_w": [NE, DM, DM], "L0_ffn_expert_b": [NE, DM],
               "L1_attn_ln_gamma": [DM], "L1_attn_ln_beta": [DM],
               "L1_attn_W_qkv": [3 * DM, DM], "L1_attn_b_qkv": [3 * DM],
               "L1_attn_W_out": [DM, DM], "L1_attn_b_out": [DM],
               "L1_ffn_Wq": [DM, DM], "L1_ffn_Wk": [DM, DM],
               "L1_ffn_Wv": [DM, DM], "L1_ffn_Wo": [DM, DM],
               "L1_ffn_Ws1": [DM, DC], "L1_ffn_bs1": [DC],
               "L1_ffn_Ws2": [DC, 1], "L1_ffn_bs2": [1]}
    need = set()
    if "l0a" in phases:
        need |= {"L0_attn_W_dq", "L0_attn_W_uq", "L0_attn_W_dkv",
                 "L0_attn_W_uk", "L0_attn_W_uv", "L0_attn_W_o"}
    if "l0f" in phases:
        need |= {"L0_ffn_gate_w", "L0_ffn_gate_b", "L0_ffn_expert_w",
                 "L0_ffn_expert_b"}
    if "l1a" in phases:
        need |= {"L1_attn_ln_gamma", "L1_attn_ln_beta", "L1_attn_W_qkv",
                 "L1_attn_b_qkv", "L1_attn_W_out", "L1_attn_b_out"}
    if "l1f" in phases:
        need |= {"L1_ffn_Wq", "L1_ffn_Wk", "L1_ffn_Wv", "L1_ffn_Wo",
                 "L1_ffn_Ws1", "L1_ffn_bs1", "L1_ffn_Ws2", "L1_ffn_bs2"}
    for name in sorted(need):
        W[name] = nc.dram_tensor(name, wshapes[name], F32,
                                 kind="ExternalInput")

    B = Bufs(nc)
    em = Em(nc)

    def load_aux(em):
        em.op("vector", lambda e: e.memset(B.consts.ap()[:, 0:1], float(EPS)))
        em.op("vector", lambda e: e.memset(B.consts.ap()[:, 1:2],
                                           float(EPS * EPS)))
        em.op("vector", lambda e: e.memset(B.consts.ap()[:, 2:3], 36.5),
              inc="k_consts")
        em.op("gpsimd", lambda e: e.dma_start(B.id_bf.ap(), aux["aux_id"].ap()),
              inc="aux_id", dma=True)
        em.op("gpsimd", lambda e: e.dma_start(B.ones_bf.ap(),
                                              aux["aux_ones"].ap()),
              inc="aux_ones", dma=True)
        em.op("sync", lambda e: e.dma_start(B.caus.ap(), aux["aux_caus"].ap()),
              inc="aux_caus", dma=True)
        em.op("sync", lambda e: e.dma_start(B.tril.ap(), aux["aux_tril"].ap()),
              inc="aux_tril", dma=True)
        em.op("gpsimd", lambda e: e.dma_start(B.auxg.ap(), aux["aux_g"].ap()),
              inc="aux_g", dma=True)
        em.op("gpsimd", lambda e: e.dma_start(B.auxe.ap(), aux["aux_e"].ap()),
              inc="aux_e", dma=True)

    def load_w5(em):
        for name, slot, coff, nchunk in WSPECS_L0A:
            src = W[name].ap().rearrange("(c p) n -> p c n", p=128)
            dst = B.pl(slot, coff, nchunk * (2048 // nchunk)).rearrange(
                "p (c n) -> p c n", c=nchunk)
            em.op("gpsimd", lambda e, src=src, dst=dst: e.dma_start(dst, src),
                  inc=("l0a_w5" if name == "L0_attn_W_uv" else None), dma=True,
                  grp="w5")

    def run_seq(seq):
        tag = f"s{seq}"
        B.lay = LAY_L0A

        def ph_load(em):
            emit_load_x(em, B, x_d.ap()[seq], tag,
                        waits=(f"s{seq-1}_xout",) if seq > 0 else ())
            if seq == 0:
                load_aux(em)
            if "l0a" in phases:
                load_w5(em)
        em.run_block(ph_load)

        if "l0a" in phases:
            t0 = tag + "l0a"

            def ph_ln(em):
                em.gate(("aux_id", "aux_ones", "aux_caus", "aux_tril",
                         "aux_g", "aux_e"))
                emit_ln_stats(em, B, t0, waits=(f"{tag}_xin", "k_consts"))
            em.run_block(ph_ln)
            em.run_block(lambda em: emit_l0_cproj(em, B, t0))

            def ph_kv(em):
                emit_l0_kv(em, B, t0)
                src = W["L0_attn_W_o"].ap().rearrange("(c p) n -> p c n", p=128)
                em.op("gpsimd",
                      lambda e: e.dma_start(
                          B.pl("wo_a").rearrange("p (c n) -> p c n", c=4),
                          src[:, 0:4]),
                      inc="l0a_woa", waits=(f"{t0}_cmm3_11",), dma=True)
                em.op("gpsimd",
                      lambda e: e.dma_start(
                          B.pl("wo_b").rearrange("p (c n) -> p c n", c=4),
                          src[:, 4:8]),
                      inc="l0a_wob", waits=(f"{t0}_ktm31", f"{t0}_vm31"),
                      dma=True)
            em.run_block(ph_kv)
            em.run_block(lambda em: emit_l0a_attn(em, B, t0,
                                                  ("l0a_woa", "l0a_wob")))

        if "l0f" in phases:
            B.lay = LAY_L0F
            emit_l0_ffn(em, B, tag + "l0f", W, f"{tag}_xin")

        if "l1a" in phases:
            B.lay = LAY_L1A
            if DEBUG_DRAIN:
                def ph_drain(em):
                    if em.mode == "count":
                        return
                    name, obj = em.cur
                    for g, (sem, qc) in em.dma_sems.items():
                        obj.wait_ge(sem, em.dma_cnt[g] * 16)
                em.run_block(ph_drain)
            emit_l1_attn(em, B, tag + "l1a", W, f"{tag}_xin")

        if "l1f" in phases:
            B.lay = LAY_L1F
            emit_l1_ffn(em, B, tag + "l1f", W, f"{tag}_xin")

        if DEBUG_DUMP and seq == 0:
            dbg_pool = nc.dram_tensor("dbg_pool", [128, POOL_COLS], F32,
                                      kind="ExternalOutput")
            dbg_rows = nc.dram_tensor("dbg_rows", [128, 3072], F32,
                                      kind="ExternalOutput")
            dbg_misc = nc.dram_tensor("dbg_misc", [128, 2048 + 1024 + 128 + 32
                                                   + 16 + 2048], F32,
                                      kind="ExternalOutput")

            def ph_dump(em):
                em.op("gpsimd", lambda e: e.dma_start(
                    dbg_pool.ap(), B.pool.ap()), inc="dbg0", dma=True,
                    grp="dbg")
                em.op("gpsimd", lambda e: e.dma_start(
                    dbg_rows.ap(), B.rows.ap()), inc="dbg1", dma=True,
                    grp="dbg")
                m = dbg_misc.ap()
                em.op("gpsimd", lambda e: e.dma_start(
                    m[:, 0:2048], B.ev.ap()), dma=True, grp="dbg")
                em.op("gpsimd", lambda e: e.dma_start(
                    m[:, 2048:3072], B.s32.ap()), dma=True, grp="dbg")
                em.op("gpsimd", lambda e: e.dma_start(
                    m[:, 3072:3200], B.stats.ap()), dma=True, grp="dbg")
                em.op("gpsimd", lambda e: e.dma_start(
                    m[:, 3200:3232], B.bqkv.ap()), dma=True, grp="dbg")
                em.op("gpsimd", lambda e: e.dma_start(
                    m[:, 3232:3248], B.gbfm.ap()), dma=True, grp="dbg")
                em.op("gpsimd", lambda e: e.dma_start(
                    m[:, 3248:5296], B.sbf.ap()), inc="dbg2", dma=True,
                    grp="dbg")
            em.run_block(ph_dump)

        em.run_block(lambda em: emit_store_x(em, B, out_d.ap()[seq], tag, ()))

    for seq in range(n_seq):
        run_seq(seq)

    with nc.Block() as blk:
        @blk.sync
        def _(e):
            for grp, (sem, qc) in em.dma_sems.items():
                if qc == "hw":
                    e.wait_ge(sem, em.dma_cnt[grp] * 16)
        @blk.gpsimd
        def _(e):
            for grp, (sem, qc) in em.dma_sems.items():
                if qc == "sw":
                    e.wait_ge(sem, em.dma_cnt[grp] * 16)
    return nc


def _shard_inputs(inputs, n_seq=SEQ_PER_CORE):
    aux = _aux_arrays()
    maps = []
    x = np.ascontiguousarray(inputs["L0_attn_x"])
    for c in range(N_CORES):
        m = {"x": x[c * n_seq:(c + 1) * n_seq]}
        for k, v in inputs.items():
            if k != "L0_attn_x":
                m[k] = np.ascontiguousarray(v)
        m.update(aux)
        maps.append(m)
    return maps


def kernel(**inputs):
    nc = build_core_program()
    maps = _shard_inputs(inputs)
    declared = set()
    for a in nc.m.functions[0].allocations:
        try:
            declared.add(a.memorylocations[0].name)
        except AttributeError:
            pass
    maps = [{k: v for k, v in m.items() if k in declared} for m in maps]
    res = run_bass_kernel_spmd(nc, maps, core_ids=list(range(N_CORES)))
    return np.concatenate([r["out"] for r in res.results], axis=0)


LAY_L0F = _lay([
    ("xnt", 8 * 2048),
    ("xnt_lo", 8 * 2048),
    ("gw", 2 * 64),
    ("wt8", 2048),
    ("be8", 2 * 1024),
    ("wfb", 128),
])
LAY_L0F["ew"] = (LAY_L0F["xnt_lo"][0], 2 * 8192)


def emit_l0_ffn(em, B, tag, Wd, xin_mark):
    ev = B.ev.ap()
    s32 = B.s32.ap()
    def ph_ln(em):
        em.gate(("aux_id", "aux_ones", "aux_caus", "aux_tril",
                 "aux_g", "aux_e"))
        emit_ln_stats(em, B, tag, waits=(xin_mark, "k_consts"))
    em.run_block(ph_ln)

    def ph_xnt(em):
        emit_xnt_tiles(em, B, f"{tag}x", list(range(NT)), "xnt", 2048,
                       waits=(f"{tag}_rstd",), lo=True)
    em.run_block(ph_xnt)

    def ph_gate(em):
        src = Wd["L0_ffn_gate_w"].ap().rearrange("(c p) n -> p c n", p=128)
        em.op("sync", lambda e: e.dma_start(
            s32[:, 0:64].rearrange("p (c n) -> p c n", c=8), src),
            inc=f"{tag}_gwin", dma=True)
        em.op("scalar", lambda e: e.copy(B.pl("gw", 0, 64), s32[:, 0:64]),
              inc=f"{tag}_gwh", waits=(f"{tag}_gwin",))
        em.op("vector", lambda e: e.tensor_tensor(
            B.pl("gw", 64, 64), s32[:, 0:64], B.pl("gw", 0, 64), ALU.subtract),
            inc=f"{tag}_gwl", waits=(f"{tag}_gwh",))
        srcb = Wd["L0_ffn_gate_b"].ap().unsqueeze(1)
        em.op("sync", lambda e: e.dma_start(ev[0:8, 1984:1985], srcb),
              inc=f"{tag}_gbin", dma=True)
        for qb in range(4):
            n = 0
            for slot, wi in (("xnt", 0), ("xnt", 64), ("xnt_lo", 0)):
                for c in range(8):
                    last = (n == 23)
                    em.op("tensor",
                          lambda e, qb=qb, slot=slot, wi=wi, c=c, n=n,
                          last=last: e.matmul(
                              B.bank(qb, p1=8),
                              B.pl("gw", wi + c * 8, 8),
                              B.pl(slot, c * 2048 + qb * 512, 512),
                              start=(n == 0), stop=last),
                          inc=(f"{tag}_glm{qb}" if last else None),
                          waits=((f"{tag}_gwl", f"{tag}x_ev_lo{NT-1}")
                                 if n == 0 else ()))
                    n += 1
        for qb in range(4):
            em.op("vector",
                  lambda e, qb=qb: e.tensor_scalar(
                      ev[0:8, qb * 512:(qb + 1) * 512], B.bank(qb, p1=8),
                      ev[0:8, 1984:1985], None, ALU.add),
                  inc=f"{tag}_glev{qb}",
                  waits=(f"{tag}_glm{qb}", f"{tag}_gbin"))
        em.op("scalar", lambda e: e.copy(B.sbf.ap()[0:8, 0:2048],
                                         ev[0:8, 0:2048]),
              inc=f"{tag}_glh", waits=tuple(f"{tag}_glev{q}" for q in range(4)))
        em.op("vector", lambda e: e.tensor_tensor(
            B.rows.ap()[0:8, 0:2048], ev[0:8, 0:2048],
            B.sbf.ap()[0:8, 0:2048], ALU.subtract),
            inc=f"{tag}_gll", waits=(f"{tag}_glh",))
        for t in range(NT):
            bnk = 4 + t % 2
            em.op("tensor",
                  lambda e, t=t, bnk=bnk: e.matmul(
                      B.bank(bnk, c0=0, w=8), B.sbf.ap()[0:8, t * 128:(t + 1) * 128],
                      B.id_bf.ap()[0:8, 0:8], start=True, stop=False),
                  waits=(f"{tag}_gll",) + ((f"{tag}_gw{t-2}",) if t >= 2 else ())),
            em.op("tensor",
                  lambda e, t=t, bnk=bnk: e.matmul(
                      B.bank(bnk, c0=0, w=8), B.rows.ap()[0:8, t * 128:(t + 1) * 128],
                      B.id_bf.ap()[0:8, 0:8], start=False, stop=True),
                  inc=f"{tag}_gtr{t}")
            co = 1024 + t * 8
            em.op("vector",
                  lambda e, t=t, bnk=bnk: e.tensor_reduce(
                      ev[:, 1990:1991], B.bank(bnk, c0=0, w=8), AX.X, ALU.max,
                      negate=True),
                  inc=f"{tag}_gmx{t}",
                  waits=(f"{tag}_gtr{t}",) + ((f"{tag}_gex{t-1}",) if t else ()))
            em.op("scalar",
                  lambda e, t=t, bnk=bnk: e.activation(
                      s32[:, 960:968], B.bank(bnk, c0=0, w=8), AF.Exp,
                      bias=ev[:, 1990:1991], scale=1.0),
                  inc=f"{tag}_gex{t}",
                  waits=(f"{tag}_gmx{t}",) + ((f"{tag}_gpr{t-1}",) if t else ()))
            em.op("vector",
                  lambda e: e.tensor_reduce(ev[:, 1991:1992], s32[:, 960:968],
                                            AX.X, ALU.add),
                  inc=f"{tag}_gsm{t}", waits=(f"{tag}_gex{t}",))
            em.op("vector",
                  lambda e: e.reciprocal(ev[:, 1992:1993], ev[:, 1991:1992]),
                  inc=f"{tag}_grc{t}", waits=(f"{tag}_gsm{t}",))
            em.op("vector",
                  lambda e: e.tensor_scalar(s32[:, 968:976], s32[:, 960:968],
                                            ev[:, 1992:1993], None, ALU.mult),
                  inc=f"{tag}_gpr{t}", waits=(f"{tag}_grc{t}",))
            em.op("vector",
                  lambda e: e.tensor_scalar(s32[:, 976:984], s32[:, 968:976],
                                            0.1, None, ALU.is_ge),
                  inc=f"{tag}_gsel{t}", waits=(f"{tag}_gpr{t}",))
            em.op("vector",
                  lambda e: e.tensor_tensor(s32[:, 984:992], s32[:, 968:976],
                                            s32[:, 976:984], ALU.mult),
                  inc=f"{tag}_gw0{t}", waits=(f"{tag}_gsel{t}",))
            em.op("vector",
                  lambda e: e.tensor_reduce(ev[:, 1993:1994], s32[:, 984:992],
                                            AX.X, ALU.add),
                  inc=f"{tag}_gws{t}", waits=(f"{tag}_gw0{t}",))
            em.op("vector",
                  lambda e: e.tensor_scalar(ev[:, 1994:1995], ev[:, 1993:1994],
                                            0.0, ev[:, 1993:1994], ALU.is_equal,
                                            ALU.add),
                  inc=f"{tag}_grw{t}", waits=(f"{tag}_gws{t}",))
            em.op("vector",
                  lambda e: e.reciprocal(ev[:, 1995:1996], ev[:, 1994:1995]),
                  inc=f"{tag}_grr{t}", waits=(f"{tag}_grw{t}",))
            em.op("vector",
                  lambda e, t=t: e.tensor_scalar(s32[:, t * 8:(t + 1) * 8],
                                                 s32[:, 984:992],
                                                 ev[:, 1995:1996], None,
                                                 ALU.mult),
                  inc=f"{tag}_gw{t}", waits=(f"{tag}_grr{t}",))
            em.op("vector",
                  lambda e, t=t: e.tensor_copy(B.pl("wfb", t * 8, 8),
                                               s32[:, t * 8:(t + 1) * 8]),
                  inc=f"{tag}_gwb{t}", waits=(f"{tag}_gw{t}",))
        for t in range(NT):
            bnk = 6 + t % 2
            em.op("tensor",
                  lambda e, t=t, bnk=bnk: e.matmul(
                      B.bank(bnk, p1=8, c0=0, w=128),
                      B.pl("wfb", t * 8, 8),
                      B.id_bf.ap()[:], start=True, stop=True),
                  inc=f"{tag}_wtm{t}",
                  waits=(f"{tag}_gwb{t}",) + ((f"{tag}_wte{t-2}",) if t >= 2
                                              else ())),
            em.op("vector",
                  lambda e, t=t, bnk=bnk: e.tensor_copy(
                      B.pl("wt8", t * 128, 128, p0=0, p1=8),
                      B.bank(bnk, p0=0, p1=8, c0=0, w=128)),
                  inc=f"{tag}_wte{t}", waits=(f"{tag}_wtm{t}",))
        em.op("sync", lambda e: e.dma_start(
            ev[0:8, 0:1024], Wd["L0_ffn_expert_b"].ap()),
            inc=f"{tag}_bein", dma=True, grp=f"{tag}_bein",
            waits=(f"{tag}_gll",))
        em.op("scalar", lambda e: e.copy(B.pl("be8", 0, 1024, p0=0, p1=8),
                                         ev[0:8, 0:1024]),
              inc=f"{tag}_beh", waits=(f"{tag}_bein", f"{tag}_gtr{NT-1}")),
        em.op("vector", lambda e: e.tensor_tensor(
            B.pl("be8", 1024, 1024, p0=0, p1=8), ev[0:8, 0:1024],
            B.pl("be8", 0, 1024, p0=0, p1=8), ALU.subtract),
            inc=f"{tag}_bel", waits=(f"{tag}_beh",))
    em.run_block(ph_gate)

    def ph_exp(em):
        for e_ in range(NE):
            src = Wd["L0_ffn_expert_w"].ap()[e_].rearrange(
                "(c p) n -> p c n", p=128)
            ws = []
            if e_ >= 2:
                ws.append(f"{tag}_em{e_-2}")
            em.op("gpsimd",
                  lambda e, e_=e_, src=src: e.dma_start(
                      B.pl("ew", (e_ % 2) * 8192, 8192).rearrange(
                          "p (c n) -> p c n", c=8), src),
                  inc=f"{tag}_ewin{e_}", waits=tuple(ws), dma=True,
                  grp=f"ew{e_ % 2}")
        em.op("vector", lambda e: e.drain() or 1)
        for e_ in range(NE):
            for i in range(32):
                tt, nh = i // 2, i % 2
                bnk = i % 4
                last_mm = (i == 31)
                ws = [f"{tag}_ewin{e_}"]
                if e_ > 0 or i >= 4:
                    pj = (e_ * 32 + i) - 4
                    ws.append(f"{tag}_stt{pj}")
                for c in range(8):
                    em.op("tensor",
                          lambda e, e_=e_, tt=tt, nh=nh, c=c, bnk=bnk: e.matmul(
                              B.bank(bnk),
                              B.pl("xnt", c * 2048 + tt * 128, 128),
                              B.pl("ew", (e_ % 2) * 8192 + c * 1024 + nh * 512,
                                   512),
                              start=(c == 0), stop=(c == 7)),
                          inc=(f"{tag}_em{e_}" if last_mm and c == 7 else
                               (f"{tag}_pm{e_*32+i}" if c == 7 else None)),
                          waits=tuple(ws) if c == 0 else ())
                mmark = f"{tag}_em{e_}" if last_mm else f"{tag}_pm{e_*32+i}"
                xsl = B.xres.ap()[:, tt * DM + nh * 512: tt * DM + nh * 512 + 512]
                em.op("vector",
                      lambda e, xsl=xsl, bnk=bnk, tt=tt, e_=e_: e.scalar_tensor_tensor(
                          xsl, B.bank(bnk), s32[:, tt * 8 + e_:tt * 8 + e_ + 1],
                          xsl, ALU.mult, ALU.add),
                      inc=f"{tag}_stt{e_*32+i}", waits=(mmark,))
        for i in range(32):
            tt, nh = i // 2, i % 2
            bnk = 4 + i % 2
            ws = [f"{tag}_stt{NE*32-1}"]
            if i >= 2:
                ws.append(f"{tag}_bad{i-2}")
            em.op("tensor",
                  lambda e, tt=tt, nh=nh, bnk=bnk: e.matmul(
                      B.bank(bnk), B.pl("wt8", tt * 128, 128, p0=0, p1=8),
                      B.pl("be8", nh * 512, 512, p0=0, p1=8),
                      start=True, stop=False),
                  waits=tuple(ws)),
            em.op("tensor",
                  lambda e, tt=tt, nh=nh, bnk=bnk: e.matmul(
                      B.bank(bnk), B.pl("wt8", tt * 128, 128, p0=0, p1=8),
                      B.pl("be8", 1024 + nh * 512, 512, p0=0, p1=8),
                      start=False, stop=True),
                  inc=f"{tag}_bm{i}"),
            xsl = B.xres.ap()[:, tt * DM + nh * 512: tt * DM + nh * 512 + 512]
            em.op("vector",
                  lambda e, xsl=xsl, bnk=bnk: e.tensor_tensor(
                      xsl, xsl, B.bank(bnk), ALU.add),
                  inc=f"{tag}_bad{i}", waits=(f"{tag}_bm{i}",))
    em.run_block(ph_exp)


LAY_L1A = _lay([
    ("ut", 8 * 2048),
    ("kt", 4 * 2048),
    ("v", NT * 8 * 65),
    ("ctxt", 8 * 512),
    ("attn", 8 * 512),
    ("wstr", 2 * 1024),
    ("wnat", 2 * 1024),
    ("wqt", 4 * 1024),
    ("wot", 8 * 1024),
])
LAY_L1A["qt"] = (LAY_L1A["wstr"][0], 4 * 512)


def emit_l1_attn(em, B, tag, Wd, xin_mark):
    ev = B.ev.ap()
    st = B.stats.ap()

    def ph_ln(em):
        em.gate(("aux_id", "aux_ones", "aux_caus", "aux_tril",
                 "aux_g", "aux_e"))
        emit_ln_stats(em, B, tag, rstd2=True, waits=(xin_mark, "k_consts"))
        em.op("sync", lambda e: e.dma_start(
            B.gbfm.ap()[:, 0:8],
            Wd["L1_attn_ln_gamma"].ap().rearrange("(c p) -> p c", p=128),
            allow_slow_non_contiguous=True),
            inc=f"{tag}_gam", dma=True)
        em.op("sync", lambda e: e.dma_start(
            B.gbfm.ap()[:, 8:16],
            Wd["L1_attn_ln_beta"].ap().rearrange("(c p) -> p c", p=128),
            allow_slow_non_contiguous=True),
            inc=f"{tag}_bet", dma=True)
        em.op("sync", lambda e: e.dma_start(
            B.bqkv.ap()[:, 0:24],
            Wd["L1_attn_b_qkv"].ap().rearrange("(c p) -> p c", p=128),
            allow_slow_non_contiguous=True),
            inc=f"{tag}_bq", dma=True)
        em.op("sync", lambda e: e.dma_start(
            ev[0:1, 0:1024], Wd["L1_attn_b_out"].ap().unsqueeze(0)),
            inc=f"{tag}_bo", dma=True)
        em.op("vector", lambda e: e.tensor_scalar(
            B.bqkv.ap()[:, 24:32], B.bqkv.ap()[:, 0:8], 0.125, None, ALU.mult),
            inc=f"{tag}_bqs", waits=(f"{tag}_bq",))
        em.op("scalar", lambda e: e.copy(B.rows.ap()[0:1, 0:1024],
                                         ev[0:1, 0:1024]),
              inc=f"{tag}_boh", waits=(f"{tag}_bo",))
        em.op("vector", lambda e: e.tensor_tensor(
            B.rows.ap()[0:1, 1024:2048], ev[0:1, 0:1024],
            B.rows.ap()[0:1, 0:1024], ALU.subtract),
            inc=f"{tag}_bol", waits=(f"{tag}_boh",))
        em.op("sync", lambda e: e.dma_start(
            ev[32:33, 0:1024], Wd["L1_attn_b_qkv"].ap()[2048:3072].unsqueeze(0)),
            inc=f"{tag}_bv", dma=True)
        em.op("scalar", lambda e: e.copy(B.sbf.ap()[32:33, 0:1024],
                                         ev[32:33, 0:1024]),
              inc=f"{tag}_bvh", waits=(f"{tag}_bv",))
        em.op("vector", lambda e: e.tensor_tensor(
            B.sbf.ap()[32:33, 1024:2048], ev[32:33, 0:1024],
            B.sbf.ap()[32:33, 0:1024], ALU.subtract),
            inc=f"{tag}_bvl", waits=(f"{tag}_bvh",))
        em.op("vector", lambda e: e.tensor_copy(B.rows.ap()[32:33, 0:2048],
                                                B.sbf.ap()[32:33, 0:2048]),
              inc=f"{tag}_bvr", waits=(f"{tag}_bvl",))
    em.run_block(ph_ln)

    def ph_ut(em):
        emit_xnt_tiles(em, B, f"{tag}u", list(range(NT)), "ut", 2048,
                       rstd_col=64, gb=True,
                       waits=(f"{tag}_rstd2", f"{tag}_gam", f"{tag}_bet"))
        em.op("vector", lambda e: e.tensor_scalar(
            st[:, 80:96], st[:, 48:64], 1.0, None, ALU.add),
            inc=f"{tag}_hs1", waits=(f"{tag}_rstd2",))
        em.op("vector", lambda e: e.tensor_tensor(
            st[:, 96:112], st[:, 32:48], st[:, 48:64], ALU.mult),
            inc=f"{tag}_hs2", waits=(f"{tag}_hs1",))
        for t in range(NT):
            x_t = B.xres.ap()[:, t * DM:(t + 1) * DM]
            em.op("vector",
                  lambda e, x_t=x_t, t=t: e.tensor_scalar(
                      x_t, x_t, fm_col(st, 80 + t), fm_col(st, 96 + t),
                      ALU.mult, ALU.subtract),
                  inc=f"{tag}_hf{t}",
                  waits=(f"{tag}_hs2", f"{tag}u_xn{t}"))
    em.run_block(ph_ut)

    if DEBUG_DUMP and tag.startswith("s0"):
        dbg_xu = em.nc.dram_tensor("dbg_xu", [128, 16384], F32,
                                   kind="ExternalOutput")
        em.run_block(lambda em: em.op(
            "gpsimd", lambda e: e.dma_start(dbg_xu.ap(), B.xres.ap()),
            inc="dbgxu", dma=True, grp="dbg"))

    for half in range(2):
        tagH = f"{tag}h{half}"

        def ph_qkv(em, half=half, tagH=tagH):
            seqs = [("q", half * 4 + j) for j in range(4)] + \
                   [("k", 8 + half * 4 + j) for j in range(4)] + \
                   [("v", 16 + half * 4 + j) for j in range(4)] + \
                   ([("o", j) for j in range(8)] if half == 0 else [])
            for si, (kind, oc) in enumerate(seqs):
                par = si % 2
                wsrc = Wd["L1_attn_W_qkv"] if kind != "o" else Wd["L1_attn_W_out"]
                src = wsrc.ap()[oc * 128:(oc + 1) * 128, :]
                wsl = []
                if si >= 2:
                    wsl.append(f"{tagH}_wtr{si-2}")
                em.op("gpsimd",
                      lambda e, src=src, par=par: e.dma_start(
                          B.pl("wnat", par * 1024, 1024), src),
                      inc=f"{tagH}_wn{si}", waits=tuple(wsl), dma=True,
                      grp=f"{tag}_wn{par}")
                for c in range(8):
                    bnk = 6 + c // 4
                    ws2 = []
                    if c == 0:
                        ws2 = [f"{tagH}_wn{si}"]
                        if si >= 1:
                            ws2.append(f"{tagH}_wev{si-1}")
                    em.op("tensor",
                          lambda e, c=c, bnk=bnk, par=par: e.matmul(
                              B.bank(bnk, c0=(c % 4) * 128, w=128),
                              B.pl("wnat", par * 1024 + c * 128, 128),
                              B.id_bf.ap()[:], start=True, stop=True),
                          inc=(f"{tagH}_wtr{si}" if c == 7 else None),
                          waits=tuple(ws2))
                if kind == "q":
                    dst = B.pl("wqt", (oc - half * 4) * 1024, 1024)
                elif kind == "o":
                    dst = None
                else:
                    dst = B.pl("wstr", par * 1024, 1024)
                if kind != "o":
                    em.op("vector",
                          lambda e, dst=dst: e.tensor_copy(
                              dst.rearrange("p (c n) -> p c n", c=8),
                              B.bank(6, w=1024).rearrange("p (c n) -> p c n",
                                                          c=8)),
                          inc=f"{tagH}_wev{si}", waits=(f"{tagH}_wtr{si}",))
                else:
                    d3 = B.pl("wot").rearrange("p (c n) -> p c n", c=8)[
                        :, :, oc * 128:(oc + 1) * 128]
                    em.op("vector",
                          lambda e, d3=d3: e.tensor_copy(
                              d3, B.bank(6, w=1024).rearrange(
                                  "p (c n) -> p c n", c=8)),
                          inc=f"{tagH}_wev{si}", waits=(f"{tagH}_wtr{si}",))
                if kind == "k":
                    gp = oc - 8
                    lp = gp - half * 4
                    for qblk in range(4):
                        bnk = qblk % 2
                        for c in range(8):
                            ws3 = []
                            if c == 0:
                                ws3 = [f"{tagH}_wev{si}"]
                                if qblk >= 2:
                                    ws3.append(f"{tagH}_ktev{lp}_{qblk-2}")
                                elif lp > 0 or half > 0:
                                    ws3.append(f"{tagH}_ktev{lp-1}_{2+qblk}"
                                               if lp > 0 else
                                               f"{tag}h{half-1}_ktev3_{2+qblk}")
                            em.op("tensor",
                                  lambda e, c=c, qblk=qblk, bnk=bnk, par=par:
                                  e.matmul(
                                      B.bank(bnk),
                                      B.pl("wstr", par * 1024 + c * 128, 128),
                                      B.pl("ut", c * 2048 + qblk * 512, 512),
                                      start=(c == 0), stop=(c == 7)),
                                  inc=(f"{tagH}_ktm{lp}_{qblk}" if c == 7
                                       else None),
                                  waits=tuple(ws3))
                        em.op("vector",
                              lambda e, lp=lp, qblk=qblk, bnk=bnk, gp=gp:
                              e.tensor_scalar(
                                  B.pl("kt", lp * 2048 + qblk * 512, 512),
                                  B.bank(bnk), fm_col(B.bqkv.ap(), 8 + gp),
                                  None, ALU.add),
                              inc=f"{tagH}_ktev{lp}_{qblk}",
                              waits=(f"{tagH}_ktm{lp}_{qblk}",))
                elif kind == "v":
                    ocr = oc - 16 - half * 4
                    for t in range(NT):
                        bnk = 2 + t % 2
                        i = ocr * NT + t
                        for c in range(8):
                            ws4 = []
                            if c == 0:
                                ws4 = [f"{tagH}_wev{si}", f"{tag}_bvr"]
                                if i >= 2:
                                    ws4.append(f"{tagH}_vev{i-2}")
                                elif half > 0:
                                    ws4.append(f"{tag}h{half-1}_vev{4*NT-2+i}")
                            em.op("tensor",
                                  lambda e, c=c, t=t, bnk=bnk, par=par:
                                  e.matmul(
                                      B.bank(bnk, w=128),
                                      B.pl("ut", c * 2048 + t * 128, 128),
                                      B.pl("wstr", par * 1024 + c * 128, 128),
                                      start=(c == 0), stop=False),
                                  waits=tuple(ws4))
                        em.op("tensor",
                              lambda e, bnk=bnk, ocr=ocr: e.matmul(
                                  B.bank(bnk, w=128),
                                  B.ones_bf.ap()[32:33, 0:128],
                                  B.rows.ap()[32:33, (half * 4 + ocr) * 128:
                                              (half * 4 + ocr) * 128 + 128],
                                  start=False, stop=False))
                        em.op("tensor",
                              lambda e, bnk=bnk, ocr=ocr: e.matmul(
                                  B.bank(bnk, w=128),
                                  B.ones_bf.ap()[32:33, 0:128],
                                  B.rows.ap()[32:33, 1024 + (half * 4 + ocr) * 128:
                                              1024 + (half * 4 + ocr) * 128 + 128],
                                  start=False, stop=True),
                              inc=f"{tagH}_vm{i}")
                        em.op("vector",
                              lambda e, t=t, ocr=ocr, bnk=bnk: e.tensor_copy(
                                  B.pl("v", (t * 8 + ocr * 2) * 65,
                                       2 * 65).rearrange(
                                      "p (h n) -> p h n", h=2)[:, :, 0:64],
                                  B.bank(bnk, w=128).rearrange(
                                      "p (h n) -> p h n", h=2)),
                              inc=f"{tagH}_vev{i}", waits=(f"{tagH}_vm{i}",))
            em.op("vector",
                  lambda e: e.memset(B.pl("v").rearrange(
                      "p (g n) -> p g n", n=65)[:, :, 64:65], 1.0),
                  inc=f"{tagH}_vones", waits=(f"{tagH}_vev{4*NT-1}",))
            for lp in range(4):
                em.op("vector",
                      lambda e, lp=lp: e.tensor_tensor(
                          B.sbf.ap()[:, 0:2048], B.pl("kt", lp * 2048, 2048),
                          B.pl("kt", lp * 2048, 2048), ALU.mult),
                      inc=f"{tagH}_ksq{lp}",
                      waits=(f"{tagH}_ktev{lp}_3", f"{tagH}_kred{lp-1}") if lp
                      else (f"{tagH}_ktev{lp}_3",))
                for qblk in range(4):
                    em.op("tensor",
                          lambda e, qblk=qblk: e.matmul(
                              B.bank(4 + qblk, p1=1), B.ones_bf.ap()[:, 0:1],
                              B.sbf.ap()[:, qblk * 512:(qblk + 1) * 512],
                              start=True, stop=True),
                          inc=(f"{tagH}_knm{lp}" if qblk == 3 else None),
                          waits=(f"{tagH}_ksq{lp}",) if qblk == 0 else ())
                em.op("vector",
                      lambda e, lp=lp: e.tensor_reduce(
                          ev[0:1, 1024 + lp:1025 + lp],
                          B.psum_t.ap()[0:1, 4 * 512:8 * 512], AX.X, ALU.max),
                      inc=f"{tagH}_kred{lp}", waits=(f"{tagH}_knm{lp}",))
            em.op("scalar", lambda e: e.activation(
                ev[0:1, 1032:1036], ev[0:1, 1024:1028], AF.Ln,
                bias=0.0, scale=1.0),
                inc=f"{tagH}_lnk", waits=(f"{tagH}_kred3",))
        em.run_block(ph_qkv)

        def qt_emit(em, p, ws, qb, hp, tagH=tagH):
            for c in range(8):
                em.op("tensor",
                      lambda e, c=c, hp=hp, qb=qb: e.matmul(
                          B.bank(hp % 4),
                          B.pl("wqt", hp * 1024 + c * 128, 128),
                          B.pl("ut", c * 2048 + qb * 512, 512),
                          start=(c == 0), stop=(c == 7)),
                      inc=(f"{p}_qtm" if c == 7 else None),
                      waits=ws if c == 0 else ())

        def ph_attn(em, half=half, tagH=tagH):
            emit_attn_core(
                em, B, tagH, causal=False, n_hp=4, ktmax_f=lambda qb: NT,
                kt_slice=lambda hp, kt, part: B.pl(
                    "kt", hp * 2048 + kt * 128, 128,
                    p0=0 if part == "A" else 64,
                    p1=64 if part == "A" else 128),
                v_slice=lambda hp, kt, part: B.pl(
                    "v", (kt * 8 + hp * 2 + (0 if part == "A" else 1)) * 65,
                    65),
                qt_emit=qt_emit,
                qt_bias=lambda hp: fm_col(B.bqkv.ap(), half * 4 + hp),
                wo_slice=lambda ci, oh: B.pl(
                    "wot", (half * 4 + ci) * 1024 + oh * 512, 512),
                n_wo_chunks=4,
                bias_rows=(lambda oh: [B.rows.ap()[0:1, oh * 512:
                                                   oh * 512 + 512],
                                       B.rows.ap()[0:1, 1024 + oh * 512:
                                                   1024 + oh * 512 + 512]])
                if half == 0 else (lambda oh: []),
                wo_marks=(f"{tag}h0_wev{19}", f"{tag}_bol"),
                xadd_acc=True)
        em.run_block(ph_attn)

        if DEBUG_DUMP and tag.startswith("s0"):
            dbg_xh = em.nc.dram_tensor(f"dbg_xh{half}", [128, 16384], F32,
                                       kind="ExternalOutput")
            em.run_block(lambda em, dbg_xh=dbg_xh, half=half: em.op(
                "gpsimd", lambda e: e.dma_start(dbg_xh.ap(), B.xres.ap()),
                inc=f"dbgxh{half}", dma=True, grp="dbg"))


LAY_L1F = _lay([
    ("wq", 8 * 1024), ("wk", 8 * 1024), ("wv", 8 * 1024), ("wo", 8 * 1024),
    ("ws1", 2 * 2048),
    ("gt", 2 * 1024),
    ("vg", 16 * HD),
    ("ctxtg", 8 * 128),
    ("attng", 4 * 128),
    ("o128", 2 * 1024),
    ("ktg", 8 * 128),
    ("qtg", 8 * 128),
])


def emit_l1_ffn(em, B, tag, Wd, xin_mark):
    ev = B.ev.ap()
    st = B.stats.ap()
    s32 = B.s32.ap()

    def ph_ln(em):
        em.gate(("aux_id", "aux_ones", "aux_caus", "aux_tril",
                 "aux_g", "aux_e"))
        emit_ln_stats(em, B, tag, waits=(xin_mark, "k_consts"))
        for nm, slot in (("L1_ffn_Wq", "wq"), ("L1_ffn_Wk", "wk"),
                         ("L1_ffn_Wv", "wv"), ("L1_ffn_Wo", "wo")):
            src = Wd[nm].ap().rearrange("(c p) n -> p c n", p=128)
            em.op("gpsimd", lambda e, src=src, slot=slot: e.dma_start(
                B.pl(slot).rearrange("p (c n) -> p c n", c=8), src),
                inc=f"{tag}_{slot}in", dma=True)
    em.run_block(ph_ln)

    def ph_groups(em):
        for t in range(NT):
            x_t = B.xres.ap()[:, t * DM:(t + 1) * DM]
            wx = [f"{tag}_rstd"]
            if t:
                wx.append(f"{tag}_gm{t-1}")
            em.op("vector",
                  lambda e, x_t=x_t, t=t: e.tensor_scalar(
                      s32[:, 0:1024], x_t, fm_col(st, 32 + t),
                      fm_col(st, 48 + t), ALU.subtract, ALU.mult),
                  inc=f"{tag}_x32{t}", waits=tuple(wx))
            em.op("vector", lambda e: e.tensor_copy(B.sbf.ap()[:, 0:1024],
                                                    s32[:, 0:1024]),
                  inc=f"{tag}_xh{t}", waits=(f"{tag}_x32{t}",))
            em.op("vector",
                  lambda e: e.tensor_tensor(B.sbf.ap()[:, 1024:2048],
                                            s32[:, 0:1024],
                                            B.sbf.ap()[:, 0:1024],
                                            ALU.subtract),
                  inc=f"{tag}_xl{t}", waits=(f"{tag}_xh{t}",))
            gsl = B.auxg.ap()[:, 120 - 8 * t:248 - 8 * t]
            for dh in range(2):
                for so in (0, 1024):
                    em.op("tensor",
                          lambda e, gsl=gsl, dh=dh, so=so, t=t: e.matmul(
                              B.bank(dh), gsl,
                              B.sbf.ap()[:, so + dh * 512:so + dh * 512 + 512],
                              start=(t == 0 and so == 0),
                              stop=(t == NT - 1 and so == 1024)),
                          inc=(f"{tag}_gm{t}" if dh == 1 and so == 1024
                               else None),
                          waits=(f"{tag}_xl{t}",) if dh == 0 and so == 0
                          else ())
        em.op("vector", lambda e: e.tensor_copy(ev[:, 0:512], B.bank(0)),
              inc=f"{tag}_gev0", waits=(f"{tag}_gm{NT-1}",))
        em.op("vector", lambda e: e.tensor_copy(ev[:, 512:1024], B.bank(1)),
              inc=f"{tag}_gev1")
        em.op("scalar", lambda e: e.copy(B.sbf.ap()[:, 0:1024],
                                         ev[:, 0:1024]),
              inc=f"{tag}_gh", waits=(f"{tag}_gev0", f"{tag}_gev1",
                                      f"{tag}_xl{NT-1}"))
        em.op("vector", lambda e: e.tensor_tensor(
            B.sbf.ap()[:, 1024:2048], ev[:, 0:1024], B.sbf.ap()[:, 0:1024],
            ALU.subtract),
            inc=f"{tag}_gl", waits=(f"{tag}_gh",))
        for i in range(16):
            hl, c = i // 8, i % 8
            bnk = 2 + i % 2
            em.op("tensor",
                  lambda e, hl=hl, c=c, bnk=bnk: e.matmul(
                      B.bank(bnk, w=128),
                      B.sbf.ap()[:, hl * 1024 + c * 128:hl * 1024 + (c + 1) * 128],
                      B.id_bf.ap()[:], start=True, stop=True),
                  inc=f"{tag}_gtr{i}",
                  waits=((f"{tag}_gl",) if i == 0 else ()) +
                  ((f"{tag}_gte{i-2}",) if i >= 2 else ()))
            em.op("vector",
                  lambda e, hl=hl, c=c, bnk=bnk: e.tensor_copy(
                      B.pl("gt", hl * 1024 + c * 128, 128),
                      B.bank(bnk, w=128)),
                  inc=f"{tag}_gte{i}", waits=(f"{tag}_gtr{i}",))
    em.run_block(ph_groups)

    def ph_route(em):
        for sg in range(2):
            src = Wd["L1_ffn_Ws1"].ap().rearrange(
                "(c p) n -> p c n", p=128)[:, sg * 4:(sg + 1) * 4]
            em.op("sync", lambda e, src=src: e.dma_start(
                s32.rearrange("p (c n) -> p c n", c=4), src),
                inc=f"{tag}_w1in{sg}", dma=True,
                waits=(f"{tag}_w1l{sg-1}",) if sg else ())
            em.op("scalar",
                  lambda e, sg=sg: e.copy(B.pl("ws1", sg * 1024, 1024),
                                          s32[:, 0:1024]),
                  inc=f"{tag}_w1h{sg}", waits=(f"{tag}_w1in{sg}",))
            em.op("vector",
                  lambda e, sg=sg: e.tensor_tensor(
                      B.pl("ws1", 2048 + sg * 1024, 1024), s32[:, 0:1024],
                      B.pl("ws1", sg * 1024, 1024), ALU.subtract),
                  inc=f"{tag}_w1l{sg}", waits=(f"{tag}_w1h{sg}",))
        em.op("sync", lambda e: e.dma_start(
            ev[0:1, 1100:1356], Wd["L1_ffn_bs1"].ap().unsqueeze(0)),
            inc=f"{tag}_b1in", dma=True)
        em.op("scalar", lambda e: e.copy(B.rows.ap()[0:1, 0:256],
                                         ev[0:1, 1100:1356]),
              inc=f"{tag}_b1h", waits=(f"{tag}_b1in",))
        em.op("vector", lambda e: e.tensor_tensor(
            B.rows.ap()[0:1, 256:512], ev[0:1, 1100:1356],
            B.rows.ap()[0:1, 0:256], ALU.subtract),
            inc=f"{tag}_b1l", waits=(f"{tag}_b1h",))
        em.op("sync", lambda e: e.dma_start(
            ev[:, 1400:1402], Wd["L1_ffn_Ws2"].ap().rearrange(
                "(c p) n -> p c n", p=128),
            allow_slow_non_contiguous=True),
            inc=f"{tag}_w2in", dma=True)
        em.op("sync", lambda e: e.dma_start(
            ev[0:1, 1402:1403], Wd["L1_ffn_bs2"].ap().unsqueeze(0)),
            inc=f"{tag}_b2in", dma=True)
        em.op("scalar", lambda e: e.copy(B.sbf.ap()[:, 1024:1026],
                                         ev[:, 1400:1402]),
              inc=f"{tag}_w2h", waits=(f"{tag}_w2in",))
        em.op("vector", lambda e: e.tensor_tensor(
            B.sbf.ap()[:, 1026:1028], ev[:, 1400:1402],
            B.sbf.ap()[:, 1024:1026], ALU.subtract),
            inc=f"{tag}_w2l", waits=(f"{tag}_w2h",))
        em.op("scalar", lambda e: e.copy(B.sbf.ap()[0:1, 1028:1029],
                                         ev[0:1, 1402:1403]),
              inc=f"{tag}_b2h", waits=(f"{tag}_b2in",))
        em.op("vector", lambda e: e.tensor_tensor(
            B.sbf.ap()[0:1, 1029:1030], ev[0:1, 1402:1403],
            B.sbf.ap()[0:1, 1028:1029], ALU.subtract),
            inc=f"{tag}_b2l", waits=(f"{tag}_b2h",))
        n = 0
        for gof, wof in ((0, 0), (0, 2048), (1024, 0)):
            for c in range(8):
                em.op("tensor",
                      lambda e, gof=gof, wof=wof, c=c, n=n: e.matmul(
                          B.bank(4, w=256),
                          B.pl("gt", gof + c * 128, 128),
                          B.pl("ws1", wof + c * 256, 256),
                          start=(n == 0), stop=False),
                      waits=(f"{tag}_w1l1", f"{tag}_b1l") if n == 0 else ())
                n += 1
        em.op("tensor", lambda e: e.matmul(
            B.bank(4, w=256), B.ones_bf.ap()[0:1, 0:128],
            B.rows.ap()[0:1, 0:256], start=False, stop=False))
        em.op("tensor", lambda e: e.matmul(
            B.bank(4, w=256), B.ones_bf.ap()[0:1, 0:128],
            B.rows.ap()[0:1, 256:512], start=False, stop=True),
            inc=f"{tag}_hm")
        em.op("scalar", lambda e: e.activation(s32[:, 0:256], B.bank(4, w=256),
                                               AF.Relu, bias=0.0, scale=1.0),
              inc=f"{tag}_hrel", waits=(f"{tag}_hm",))
        em.op("vector", lambda e: e.tensor_copy(B.sbf.ap()[:, 0:256],
                                                s32[:, 0:256]),
              inc=f"{tag}_hh", waits=(f"{tag}_hrel",))
        em.op("vector", lambda e: e.tensor_tensor(
            B.sbf.ap()[:, 256:512], s32[:, 0:256], B.sbf.ap()[:, 0:256],
            ALU.subtract),
            inc=f"{tag}_hl", waits=(f"{tag}_hh",))
        for i in range(4):
            hl, c = i // 2, i % 2
            bnk = 5
            em.op("tensor",
                  lambda e, hl=hl, c=c: e.matmul(
                      B.bank(5, c0=i * 0, w=128) if False else
                      B.bank(5, c0=(i % 4) * 128, w=128),
                      B.sbf.ap()[:, hl * 256 + c * 128:hl * 256 + (c + 1) * 128],
                      B.id_bf.ap()[:], start=True, stop=True),
                  inc=(f"{tag}_htr" if i == 3 else None),
                  waits=(f"{tag}_hl",) if i == 0 else ())
        em.op("vector", lambda e: e.tensor_copy(B.sbf.ap()[:, 512:1024],
                                                B.bank(5, w=512)),
              inc=f"{tag}_hte", waits=(f"{tag}_htr",))
        n = 0
        for hof, wof in ((512, 1024), (512, 1026), (768, 1024)):
            for c in range(2):
                em.op("tensor",
                      lambda e, hof=hof, wof=wof, c=c, n=n: e.matmul(
                          B.bank(6, w=1),
                          B.sbf.ap()[:, hof + c * 128:hof + (c + 1) * 128],
                          B.sbf.ap()[c * 128:(c + 1) * 128, wof + c:wof + c + 1]
                          if False else
                          B.sbf.ap()[:, wof + c:wof + c + 1],
                          start=(n == 0), stop=False),
                      waits=(f"{tag}_hte", f"{tag}_w2l", f"{tag}_b2l")
                      if n == 0 else ())
                n += 1
        em.op("tensor", lambda e: e.matmul(
            B.bank(6, w=1), B.ones_bf.ap()[0:1, 0:128],
            B.sbf.ap()[0:1, 1028:1029], start=False, stop=False))
        em.op("tensor", lambda e: e.matmul(
            B.bank(6, w=1), B.ones_bf.ap()[0:1, 0:128],
            B.sbf.ap()[0:1, 1029:1030], start=False, stop=True),
            inc=f"{tag}_sm")
        em.op("vector", lambda e: e.tensor_copy(B.sbf.ap()[:, 1030:1031],
                                                B.bank(6, w=1)),
              inc=f"{tag}_sh", waits=(f"{tag}_sm",))
        em.op("vector", lambda e: e.drain() or 1)
        em.op("vector", lambda e: e.scalar_tensor_tensor(
            B.sbf.ap()[:, 1031:1032], B.bank(6, w=1), 1.0,
            B.sbf.ap()[:, 1030:1031], ALU.mult, ALU.subtract),
            inc=f"{tag}_sl", waits=(f"{tag}_sh",))
        for i, col in enumerate((1030, 1031)):
            em.op("tensor",
                  lambda e, col=col, i=i: e.matmul(
                      B.bank(7, p1=1, w=128), B.sbf.ap()[:, col:col + 1],
                      B.id_bf.ap()[:], start=(i == 0), stop=(i == 1)),
                  inc=(f"{tag}_str" if i == 1 else None),
                  waits=(f"{tag}_sl",) if i == 0 else ())
        em.op("vector", lambda e: e.tensor_reduce(
            ev[0:1, 1403:1404], B.bank(7, p1=1, w=128), AX.X, ALU.max,
            negate=True),
            inc=f"{tag}_smx", waits=(f"{tag}_str",))
        em.op("scalar", lambda e: e.activation(
            ev[0:1, 1404:1532], B.bank(7, p1=1, w=128), AF.Exp,
            bias=ev[0:1, 1403:1404], scale=1.0),
            inc=f"{tag}_sex", waits=(f"{tag}_smx",))
        em.op("vector", lambda e: e.tensor_reduce(
            ev[0:1, 1532:1533], ev[0:1, 1404:1532], AX.X, ALU.add),
            inc=f"{tag}_ssum", waits=(f"{tag}_sex",))
        em.op("vector", lambda e: e.reciprocal(ev[0:1, 1533:1534],
                                               ev[0:1, 1532:1533]),
              inc=f"{tag}_src", waits=(f"{tag}_ssum",))
        em.op("vector", lambda e: e.tensor_scalar(
            ev[0:1, 1404:1532], ev[0:1, 1404:1532], ev[0:1, 1533:1534], None,
            ALU.mult),
            inc=f"{tag}_spr", waits=(f"{tag}_src",))
        em.op("vector", lambda e: e.tensor_copy(B.sbf.ap()[0:1, 1032:1160],
                                                ev[0:1, 1404:1532]),
              inc=f"{tag}_sph", waits=(f"{tag}_spr",))
        em.op("vector", lambda e: e.tensor_tensor(
            B.sbf.ap()[0:1, 1160:1288], ev[0:1, 1404:1532],
            B.sbf.ap()[0:1, 1032:1160], ALU.subtract),
            inc=f"{tag}_spl", waits=(f"{tag}_sph",))
        for i, co in enumerate((1032, 1160)):
            em.op("tensor",
                  lambda e, co=co, i=i: e.matmul(
                      B.bank(4, w=1), B.sbf.ap()[0:1, co:co + 128],
                      B.ones_bf.ap()[0:1, 0:1], start=(i == 0), stop=(i == 1)),
                  inc=(f"{tag}_pcm" if i == 1 else None),
                  waits=(f"{tag}_spl", f"{tag}_hrel") if i == 0 else ())
        for i, co in enumerate((1032, 1160)):
            em.op("tensor",
                  lambda e, co=co, i=i: e.matmul(
                      B.bank(5, w=128), B.ones_bf.ap()[0:1, 0:128],
                      B.sbf.ap()[0:1, co:co + 128], start=(i == 0),
                      stop=(i == 1)),
                  inc=(f"{tag}_prm" if i == 1 else None),
                  waits=(f"{tag}_hte",) if i == 0 else ())
        em.op("vector", lambda e: e.tensor_copy(ev[:, 1536:1537],
                                                B.bank(4, w=1)),
              inc=f"{tag}_pce", waits=(f"{tag}_pcm",))
        em.op("vector", lambda e: e.tensor_copy(ev[:, 1537:1665],
                                                B.bank(5, w=128)),
              inc=f"{tag}_pre", waits=(f"{tag}_prm",))
        em.op("vector", lambda e: e.tensor_scalar(
            s32[:, 0:128], ev[:, 1537:1665], ev[:, 1536:1537], None,
            ALU.is_gt, ALU.add, accum_out=ev[:, 1665:1666]),
            inc=f"{tag}_gcnt", waits=(f"{tag}_pce", f"{tag}_pre"))
        em.op("vector", lambda e: e.tensor_scalar(
            s32[:, 128:256], ev[:, 1537:1665], ev[:, 1536:1537], None,
            ALU.is_equal),
            inc=f"{tag}_aeq", waits=(f"{tag}_gcnt",))
        em.op("vector", lambda e: e.tensor_tensor(
            s32[:, 256:384], s32[:, 128:256], B.tril.ap()[:], ALU.mult),
            inc=f"{tag}_eqt", waits=(f"{tag}_aeq",))
        em.op("vector", lambda e: e.tensor_scalar(
            s32[:, 384:512], s32[:, 256:384], 0.0, None, ALU.add, ALU.add,
            accum_out=ev[:, 1666:1667]),
            inc=f"{tag}_eqlw", waits=(f"{tag}_eqt",))
        em.op("vector", lambda e: e.tensor_tensor(
            ev[:, 1667:1668], ev[:, 1665:1666], ev[:, 1666:1667], ALU.add),
            inc=f"{tag}_rank", waits=(f"{tag}_eqlw",))
        em.op("vector", lambda e: e.tensor_scalar(
            ev[:, 1668:1669], ev[:, 1667:1668], 64.0, None, ALU.is_lt),
            inc=f"{tag}_msel", waits=(f"{tag}_rank",))
        em.op("vector", lambda e: e.tensor_scalar(
            ev[:, 1669:1670], ev[:, 1668:1669], float(-NEG), float(NEG),
            ALU.mult, ALU.add),
            inc=f"{tag}_mbias", waits=(f"{tag}_msel",))
    em.run_block(ph_route)

    def ph_gattn(em):
        for pair in range(8):
            bnk = pair % 2
            for c in range(8):
                em.op("tensor",
                      lambda e, pair=pair, c=c, bnk=bnk: e.matmul(
                          B.bank(bnk, w=128),
                          B.pl("wq", c * 1024 + pair * 128, 128),
                          B.pl("gt", c * 128, 128),
                          start=(c == 0), stop=(c == 7)),
                      inc=(f"{tag}_qm{pair}" if c == 7 else None),
                      waits=(f"{tag}_wqin",) + ((f"{tag}_qe{pair-2}",)
                                                if pair >= 2 else ())
                      if c == 0 else ())
            em.op("scalar",
                  lambda e, pair=pair, bnk=bnk: e.activation(
                      B.pl("qtg", pair * 128, 128), B.bank(bnk, w=128),
                      AF.Copy, bias=0.0, scale=0.125),
                  inc=f"{tag}_qe{pair}", waits=(f"{tag}_qm{pair}",))
            em.op("tensor",
                  lambda e, pair=pair, c=0, bnk=bnk: None, waits=()) \
                if False else None
            bnk2 = 2 + pair % 2
            for c in range(8):
                em.op("tensor",
                      lambda e, pair=pair, c=c, bnk2=bnk2: e.matmul(
                          B.bank(bnk2, w=128),
                          B.pl("wk", c * 1024 + pair * 128, 128),
                          B.pl("gt", c * 128, 128),
                          start=(c == 0), stop=(c == 7)),
                      inc=(f"{tag}_km{pair}" if c == 7 else None),
                      waits=(f"{tag}_wkin",) + ((f"{tag}_ke{pair-2}",)
                                                if pair >= 2 else ())
                      if c == 0 else ())
            em.op("vector",
                  lambda e, pair=pair, bnk2=bnk2: e.tensor_copy(
                      B.pl("ktg", pair * 128, 128), B.bank(bnk2, w=128)),
                  inc=f"{tag}_ke{pair}", waits=(f"{tag}_km{pair}",))
        for vh in range(2):
            bnk = 4 + vh
            for c in range(8):
                em.op("tensor",
                      lambda e, vh=vh, c=c, bnk=bnk: e.matmul(
                          B.bank(bnk),
                          B.pl("gt", c * 128, 128),
                          B.pl("wv", c * 1024 + vh * 512, 512),
                          start=(c == 0), stop=(c == 7)),
                      inc=(f"{tag}_vm{vh}" if c == 7 else None),
                      waits=(f"{tag}_wvin",) if c == 0 else ())
            em.op("vector",
                  lambda e, vh=vh, bnk=bnk: e.tensor_copy(
                      B.pl("vg", vh * 512, 512), B.bank(bnk)),
                  inc=f"{tag}_vge{vh}", waits=(f"{tag}_vm{vh}",))
        for pair in range(8):
            p = f"{tag}g{pair}"
            prevp = f"{tag}g{pair-1}" if pair else None
            em.op("vector",
                  lambda e, pair=pair: e.tensor_tensor(
                      B.sbf.ap()[:, 0:128], B.pl("ktg", pair * 128, 128),
                      B.pl("ktg", pair * 128, 128), ALU.mult),
                  inc=f"{p}_ksq",
                  waits=(f"{tag}_ke{pair}",) + ((f"{prevp}_qred",)
                                                if prevp else ())),
            em.op("tensor",
                  lambda e: e.matmul(B.bank(6, p1=1, w=128),
                                     B.ones_bf.ap()[:, 0:1],
                                     B.sbf.ap()[:, 0:128],
                                     start=True, stop=True),
                  inc=f"{p}_knm",
                  waits=(f"{p}_ksq",) + ((f"{prevp}_cpA",) if prevp else ())),
            em.op("vector",
                  lambda e: e.tensor_reduce(ev[0:1, 1700:1701],
                                            B.bank(6, p1=1, w=128), AX.X,
                                            ALU.max),
                  inc=f"{p}_kred", waits=(f"{p}_knm",))
            em.op("vector",
                  lambda e, pair=pair: e.tensor_tensor(
                      B.sbf.ap()[:, 0:128], B.pl("qtg", pair * 128, 128),
                      B.pl("qtg", pair * 128, 128), ALU.mult),
                  inc=f"{p}_qsq", waits=(f"{p}_kred", f"{tag}_qe{pair}")),
            em.op("tensor",
                  lambda e: e.matmul(B.bank(6, p1=1, c0=128, w=128),
                                     B.ones_bf.ap()[:, 0:1],
                                     B.sbf.ap()[:, 0:128],
                                     start=True, stop=True),
                  inc=f"{p}_qnm", waits=(f"{p}_qsq",)),
            em.op("vector",
                  lambda e: e.tensor_reduce(ev[0:1, 1701:1702],
                                            B.bank(6, p1=1, c0=128, w=128),
                                            AX.X, ALU.max),
                  inc=f"{p}_qred", waits=(f"{p}_qnm",))
            em.op("scalar",
                  lambda e: e.activation(ev[0:1, 1703:1705],
                                         ev[0:1, 1700:1702], AF.Ln,
                                         bias=0.0, scale=1.0),
                  inc=f"{p}_lnp", waits=(f"{p}_qred",))
            em.op("vector",
                  lambda e: e.tensor_reduce(ev[0:1, 1705:1706],
                                            ev[0:1, 1703:1705], AX.X, ALU.add),
                  inc=f"{p}_lns", waits=(f"{p}_lnp",))
            em.op("scalar",
                  lambda e: e.activation(B.sbf.ap()[0:1, 1024:1025],
                                         ev[0:1, 1705:1706], AF.Exp,
                                         bias=0.0, scale=0.5),
                  inc=f"{p}_cexp", waits=(f"{p}_lns",))
            em.op("tensor",
                  lambda e: e.matmul(B.bank(6, c0=256, w=1),
                                     B.ones_bf.ap()[0:1, 0:128],
                                     B.sbf.ap()[0:1, 1024:1025],
                                     start=True, stop=True),
                  inc=f"{p}_cbc", waits=(f"{p}_cexp",)),
            em.op("vector",
                  lambda e: e.scalar_tensor_tensor(
                      ev[:, 1706:1707], B.bank(6, c0=256, w=1), -1.0,
                      ev[:, 1669:1670], ALU.mult, ALU.add),
                  inc=f"{p}_negc", waits=(f"{p}_cbc", f"{tag}_mbias")),
            par = pair % 2
            sA, sB = par, 2 + par
            wsc = [f"{p}_negc"]
            if prevp:
                wsc.append(f"{prevp}_exp")
            if pair < 2:
                wsc += [f"{tag}_qe{6+par}", f"{tag}_ke{6+par}"]
            em.op("tensor",
                  lambda e, pair=pair, sA=sA: e.matmul(
                      B.bank(sA, w=128),
                      B.pl("ktg", pair * 128, 128, p0=0, p1=64),
                      B.pl("qtg", pair * 128, 128, p0=0, p1=64),
                      start=True, stop=True, tile_position=(0, 0)),
                  waits=tuple(wsc)),
            em.op("tensor",
                  lambda e, pair=pair, sB=sB: e.matmul(
                      B.bank(sB, w=128),
                      B.pl("ktg", pair * 128, 128, p0=64, p1=128),
                      B.pl("qtg", pair * 128, 128, p0=64, p1=128),
                      start=True, stop=True, tile_position=(64, 0)),
                  inc=f"{p}_sc"),
            slA = B.pl("attng", par * 128, 128)
            slB = B.pl("attng", (2 + par) * 128, 128)
            em.op("scalar",
                  lambda e, sA=sA, slA=slA: e.activation(
                      slA, B.bank(sA, w=128), AF.Exp,
                      bias=ev[:, 1706:1707], scale=1.0),
                  waits=(f"{p}_sc",) + ((f"{prevp}_ctx",) if prevp else ())),
            em.op("scalar",
                  lambda e, sB=sB, slB=slB: e.activation(
                      slB, B.bank(sB, w=128), AF.Exp,
                      bias=ev[:, 1706:1707], scale=1.0),
                  inc=f"{p}_exp")
            wctx = [f"{p}_exp"]
            if prevp:
                wctx += [f"{prevp}_mlA", f"{prevp}_mlB"]
            else:
                wctx += [f"{tag}_vge0", f"{tag}_vge1"]
            em.op("tensor",
                  lambda e, pair=pair, slA=slA: e.matmul(
                      B.bank(4, p0=0, p1=64, w=128),
                      B.pl("vg", (pair * 2) * HD, HD), slA,
                      start=True, stop=True, tile_position=(0, 0)),
                  waits=tuple(wctx)),
            em.op("tensor",
                  lambda e, slA=slA: e.matmul(
                      B.bank(6, p0=64, p1=65, w=128),
                      B.ones_bf.ap()[:, 0:1], slA,
                      start=True, stop=True, tile_position=(0, 64))),
            em.op("tensor",
                  lambda e, pair=pair, slB=slB: e.matmul(
                      B.bank(5, p0=64, p1=128, w=128),
                      B.pl("vg", (pair * 2 + 1) * HD, HD), slB,
                      start=True, stop=True, tile_position=(0, 64))),
            em.op("tensor",
                  lambda e, slB=slB: e.matmul(
                      B.bank(7, p0=32, p1=33, w=128),
                      B.ones_bf.ap()[:, 0:1], slB,
                      start=True, stop=True, tile_position=(0, 32)),
                  inc=f"{p}_ctx")
            def _recip(e, dst, srcp):
                with em.nc.allow_low_precision(reason="bf16 recipZ"):
                    return e.reciprocal(dst, srcp)
            em.op("vector",
                  lambda e: _recip(e, B.sbf.ap()[64:65, 512:640],
                                   B.bank(6, p0=64, p1=65, w=128)),
                  inc=f"{p}_rA", waits=(f"{p}_ctx",)),
            em.op("vector",
                  lambda e: _recip(e, B.sbf.ap()[32:33, 512:640],
                                   B.bank(7, p0=32, p1=33, w=128)),
                  inc=f"{p}_rB"),
            em.op("tensor",
                  lambda e: e.matmul(B.bank(6, p0=0, p1=64, w=128),
                                     B.ones_bf.ap()[64:65, 0:64],
                                     B.sbf.ap()[64:65, 512:640],
                                     start=True, stop=True),
                  inc=f"{p}_bcA", waits=(f"{p}_rA",)),
            em.op("tensor",
                  lambda e: e.matmul(B.bank(7, p0=64, p1=128, w=128),
                                     B.ones_bf.ap()[32:33, 0:64],
                                     B.sbf.ap()[32:33, 512:640],
                                     start=True, stop=True),
                  inc=f"{p}_bcB", waits=(f"{p}_rB",)),
            em.op("scalar",
                  lambda e: e.copy(ev[0:64, 1792:1920],
                                   B.bank(6, p0=0, p1=64, w=128)),
                  inc=f"{p}_cpA", waits=(f"{p}_bcA",)),
            em.op("scalar",
                  lambda e: e.copy(ev[64:128, 1792:1920],
                                   B.bank(7, p0=64, p1=128, w=128)),
                  inc=f"{p}_cpB", waits=(f"{p}_bcB",)),
            em.op("vector",
                  lambda e, pair=pair: e.tensor_tensor(
                      B.pl("ctxtg", pair * 128, 128, p0=0, p1=64),
                      B.bank(4, p0=0, p1=64, w=128), ev[0:64, 1792:1920],
                      ALU.mult),
                  inc=f"{p}_mlA", waits=(f"{p}_cpA",)),
            em.op("vector",
                  lambda e, pair=pair: e.tensor_tensor(
                      B.pl("ctxtg", pair * 128, 128, p0=64, p1=128),
                      B.bank(5, p0=64, p1=128, w=128), ev[64:128, 1792:1920],
                      ALU.mult),
                  inc=f"{p}_mlB", waits=(f"{p}_cpB",))
        for oh in range(2):
            bnk = oh
            for c in range(8):
                em.op("tensor",
                      lambda e, oh=oh, c=c, bnk=bnk: e.matmul(
                          B.bank(bnk),
                          B.pl("ctxtg", c * 128, 128),
                          B.pl("wo", c * 1024 + oh * 512, 512),
                          start=(c == 0), stop=(c == 7)),
                      inc=(f"{tag}_om{oh}" if c == 7 else None),
                      waits=(f"{tag}g7_mlA", f"{tag}g7_mlB", f"{tag}_woin")
                      if c == 0 else ())
            em.op("vector",
                  lambda e, oh=oh, bnk=bnk: e.tensor_scalar(
                      B.pl("o128", oh * 512, 512), B.bank(bnk),
                      ev[:, 1668:1669], None, ALU.mult),
                  inc=f"{tag}_oh{oh}", waits=(f"{tag}_om{oh}",))
            em.op("vector", lambda e: e.drain() or 1)
            em.op("vector",
                  lambda e, oh=oh, bnk=bnk: e.scalar_tensor_tensor(
                      B.pl("o128", 1024 + oh * 512, 512), B.bank(bnk),
                      ev[:, 1668:1669], B.pl("o128", oh * 512, 512),
                      ALU.mult, ALU.subtract),
                  inc=f"{tag}_ol{oh}", waits=(f"{tag}_oh{oh}",))
        for i in range(32):
            t, dh = i // 2, i % 2
            bnk = 2 + i % 4
            esl = B.auxe.ap()[:, t * 128:(t + 1) * 128]
            ws5 = [f"{tag}_ol0", f"{tag}_ol1"]
            if i >= 4:
                ws5.append(f"{tag}_xa{i-4}")
            em.op("tensor",
                  lambda e, esl=esl, dh=dh, bnk=bnk: e.matmul(
                      B.bank(bnk), esl, B.pl("o128", dh * 512, 512),
                      start=True, stop=False),
                  waits=tuple(ws5)),
            em.op("tensor",
                  lambda e, esl=esl, dh=dh, bnk=bnk: e.matmul(
                      B.bank(bnk), esl, B.pl("o128", 1024 + dh * 512, 512),
                      start=False, stop=True),
                  inc=f"{tag}_xm{i}")
            xsl = B.xres.ap()[:, t * DM + dh * 512: t * DM + dh * 512 + 512]
            em.op("vector",
                  lambda e, xsl=xsl, bnk=bnk: e.tensor_tensor(
                      xsl, xsl, B.bank(bnk), ALU.add),
                  inc=f"{tag}_xa{i}", waits=(f"{tag}_xm{i}",))
    em.run_block(ph_gattn)
